# version 01
# baseline (speedup 1.0000x reference)
import contextlib
import numpy as np
import concourse.bass as bass
import concourse.mybir as mybir
from concourse.bass_utils import run_bass_kernel_spmd

F32 = mybir.dt.float32
AF = mybir.ActivationFunctionType
ALU = mybir.AluOpType

HD = 4096
DSSM = 8192
NH = 64
KC = 4
EPS = 1e-5
CONVD = DSSM + 256
T = 2048
NCH = 16
PH1 = 512
NPH1 = 4
PH2 = 256
NPH2 = 8
W1 = PH1 + 3
RANKS = (8, 16, 32, 64)
NF1 = 67
HG = 8
LRG = 8


class Sem:
    def __init__(self, stack, nc, name):
        self.h = stack.enter_context(nc.semaphore(name))
        self.c = 0


def build_program():
    nc = bass.Bass("TRN2", target_bir_lowering=False)

    d_x1 = nc.dram_tensor("x1", [NPH1, 128, 32 * W1], F32, kind="ExternalInput").ap()
    d_x2 = nc.dram_tensor("x2", [NPH2, 128, 32 * PH2], F32, kind="ExternalInput").ap()
    d_w1a = nc.dram_tensor("w1a", [NF1, 128, 32 * 128], F32, kind="ExternalInput").ap()
    d_w1z = nc.dram_tensor("w1z", [64, 128, 32 * 128], F32, kind="ExternalInput").ap()
    d_wout = nc.dram_tensor("wout", [32, 2, 128, 32 * 128], F32, kind="ExternalInput").ap()
    d_wffn = nc.dram_tensor("wffn", [32, 128, 32 * 128], F32, kind="ExternalInput").ap()
    d_apack = nc.dram_tensor("apack", [128, 32 * 64], F32, kind="ExternalInput").ap()
    d_blt = nc.dram_tensor("blt", [64, HD], F32, kind="ExternalInput").ap()
    d_gate = nc.dram_tensor("gatew", [128, 32 * 4], F32, kind="ExternalInput").ap()
    d_convw = nc.dram_tensor("convw", [128, 66 * KC], F32, kind="ExternalInput").ap()
    d_dtb = nc.dram_tensor("dtb", [64, 1], F32, kind="ExternalInput").ap()
    d_alog = nc.dram_tensor("alog", [64, 1], F32, kind="ExternalInput").ap()
    d_dcol = nc.dram_tensor("dcol", [128, 64], F32, kind="ExternalInput").ap()
    d_nw = nc.dram_tensor("nw", [128, 64], F32, kind="ExternalInput").ap()
    d_bffn = nc.dram_tensor("bffn", [128, 32], F32, kind="ExternalInput").ap()
    d_gateb = nc.dram_tensor("gateb", [4, 1], F32, kind="ExternalInput").ap()
    d_hmask = nc.dram_tensor("hmask", [128, 1], F32, kind="ExternalInput").ap()
    d_hinv = nc.dram_tensor("hinv", [128, 1], F32, kind="ExternalInput").ap()
    d_out = nc.dram_tensor("out", [HD, T], F32, kind="ExternalOutput").ap()

    s_xt = nc.dram_tensor("s_xt", [NCH, 128, NH, 128], F32).ap()
    s_b = nc.dram_tensor("s_b", [NCH, 128, 128], F32).ap()
    s_bt = nc.dram_tensor("s_bt", [NCH, 128, 128], F32).ap()
    s_c = nc.dram_tensor("s_c", [NCH, 128, 128], F32).ap()
    s_cum = nc.dram_tensor("s_cum", [NCH, 64, 128], F32).ap()
    s_ncum = nc.dram_tensor("s_ncum", [NCH, 64, 128], F32).ap()
    s_dtp = nc.dram_tensor("s_dtp", [NCH, 64, 128], F32).ap()
    s_ffn = nc.dram_tensor("s_ffn", [32, 128, T], F32).ap()
    s_ab = nc.dram_tensor("s_ab", [2, T], F32).ap()
    cc_in = nc.dram_tensor("cc_in", [128, NH * 128], F32)
    cc_out = nc.dram_tensor("cc_out", [128, NH * 128], F32)

    with contextlib.ExitStack() as st:
        def sbt(name, shape):
            return st.enter_context(nc.sbuf_tensor("t_" + name, shape, F32))

        ident = sbt("ident", [128, 128])
        maskbig = sbt("maskbig", [128, 128])
        jt64 = sbt("jt64", [128, 64])
        convw = sbt("convw", [128, 66 * KC])
        dtb = sbt("dtb", [64, 1])
        alog = sbt("alog", [64, 1])
        negea = sbt("negea", [64, 1])
        gateb = sbt("gateb", [4, 1])
        dcol = sbt("dcol", [128, 64])
        nw = sbt("nw", [128, 64])
        bffn = sbt("bffn", [128, 32])
        hmask = sbt("hmask", [128, 1])
        hinv = sbt("hinv", [128, 1])
        ones1 = sbt("ones1", [1, 128])
        onescol = sbt("onescol", [128, 1])
        zz64 = sbt("zz64", [64, 128])
        slab = [sbt(f"slab{i}", [128, 32 * 128]) for i in range(2)]
        HT = sbt("HT", [128, NH * 128])
        xtg = [sbt(f"xtg{i}", [128, HG * 128]) for i in range(2)]
        aec = sbt("aec", [64, 1])
        aerow = sbt("aerow", [1, 64])
        epsc = sbt("epsc", [128, 1])
        r8k = sbt("r8k", [128, 1])

        UC = 33200
        uni = sbt("uni", [128, UC])
        cur = [0]

        def ualloc(pdim, cols):
            off = cur[0]
            cur[0] += cols
            assert cur[0] <= UC, f"union overflow {cur[0]}"
            return uni[0:pdim, off:off + cols]

        xn1 = ualloc(128, 32 * W1)
        tmpx = ualloc(128, W1)
        tmpv2 = ualloc(128, W1)
        tmpg = ualloc(128, W1)
        arep = ualloc(128, W1)
        brep = ualloc(128, W1)
        arow = ualloc(1, W1)
        brow = ualloc(1, W1)
        srow = ualloc(1, W1)
        s2row = ualloc(1, W1)
        xbcf = ualloc(128, W1)
        cvs = ualloc(128, PH1)
        cvs2 = ualloc(128, PH1)
        bph = ualloc(128, PH1)
        cph = ualloc(128, PH1)
        xh = ualloc(128, PH1)
        btph = ualloc(128, 4 * 128)
        dtp1 = ualloc(64, PH1)
        loga1 = ualloc(64, PH1)
        cumc1 = [ualloc(64, 128) for _ in range(4)]
        ngc1 = ualloc(64, 128)
        wr1 = ualloc(64, 128)
        wdt1 = [ualloc(64, 128) for _ in range(4)]
        wdtT1 = [ualloc(128, 64) for _ in range(4)]
        ae1 = [ualloc(128, 64) for _ in range(4)]
        xtst = [ualloc(128, 128) for _ in range(4)]
        btw1 = [ualloc(128, 128) for _ in range(2)]
        scph = ualloc(4, PH1)
        scT = [ualloc(128, 4) for _ in range(2)]
        rcol = [ualloc(128, 1) for _ in range(2)]
        bestc = ualloc(128, 1)
        dstep = ualloc(128, 1)
        maskj = [ualloc(128, 64) for _ in range(2)]
        p64m = [ualloc(128, 64) for _ in range(2)]
        p64T = ualloc(64, PH1)
        ffst = [ualloc(128, PH1) for _ in range(2)]
        apk = ualloc(128, 32 * 64)
        bltt = [ualloc(64, 128) for _ in range(2)]
        gatew = ualloc(128, 32 * 4)
        assert cur[0] <= UC, cur[0]

        cur[0] = 0
        ztile = ualloc(128, 64 * PH2)
        xn2 = ualloc(128, 32 * PH2)
        xst2 = ualloc(128, PH2)
        tmp2 = ualloc(128, PH2)
        LT = ualloc(2, LRG * 128)
        RT = ualloc(2, LRG * 128)
        onesrow = ualloc(1, 128)
        btc2 = ualloc(128, 128)
        bc2 = ualloc(128, 128)
        cc2 = ualloc(128, 128)
        packin = ualloc(128, 128)
        packT = ualloc(128, 128)
        dtpc2 = ualloc(64, 128)
        wr2 = ualloc(64, 128)
        wdt2 = ualloc(64, 128)
        wdtT2 = ualloc(128, 64)
        ae2 = ualloc(128, 64)
        ecT = ualloc(128, 64)
        g2sb = ualloc(128, 128)
        esb = [ualloc(128, 128) for _ in range(2)]
        esbE = [ualloc(128, 128) for _ in range(2)]
        wsb = [ualloc(128, 128) for _ in range(2)]
        y2sb = [ualloc(128, 128) for _ in range(2)]
        ytf = [ualloc(128, 128) for _ in range(2)]
        sszt = [ualloc(128, 128) for _ in range(2)]
        sqt2 = sszt
        btw2 = [ualloc(128, 128) for _ in range(2)]
        arowt = ualloc(1, PH2)
        browt = ualloc(1, PH2)
        rssum = ualloc(1, 128)
        rrow = ualloc(1, 128)
        rrep = ualloc(128, 128)
        outst = [ualloc(128, PH2) for _ in range(2)]
        ffin = [ualloc(128, PH2) for _ in range(2)]
        arep2 = ualloc(128, PH2)
        brep2 = ualloc(128, PH2)
        assert cur[0] <= UC, cur[0]

        pb = [st.enter_context(nc.psum_tensor(f"pb{i}", [128, 512], F32)) for i in range(8)]

        SD = Sem(st, nc, "SD")
        GD = Sem(st, nc, "GD")
        VD = Sem(st, nc, "VD")
        TM = Sem(st, nc, "TM")
        VE = Sem(st, nc, "VE")
        AC = Sem(st, nc, "AC")
        GC = Sem(st, nc, "GC")
        CCS = Sem(st, nc, "CCS")

        block = st.enter_context(nc.Block())
        TE, VC, SC, GP, SY = block.tensor, block.vector, block.scalar, block.gpsimd, block.sync

        pbank_free = [None] * 8
        slab_free = [None, None]

        def bank_wait(e, i):
            if pbank_free[i] is not None:
                sem, val = pbank_free[i]
                e.wait_ge(sem.h, val)

        def load_slab(si, src_ap):
            gate = slab_free[si]
            def _(sy, si=si, src_ap=src_ap, gate=gate):
                if gate is not None:
                    sy.wait_ge(gate[0].h, gate[1])
                sy.dma_start(out=slab[si][:, :], in_=src_ap).then_inc(SD.h, 16)
            SY(_)
            SD.c += 16
            return SD.c

        def _(sy):
            for dst, src in [(convw, d_convw), (dtb, d_dtb), (alog, d_alog), (gateb, d_gateb),
                             (dcol, d_dcol), (nw, d_nw), (bffn, d_bffn), (hmask, d_hmask),
                             (hinv, d_hinv)]:
                sy.dma_start(out=dst[:, :], in_=src[:, :]).then_inc(SD.h, 16)
            sy.dma_start(out=apk, in_=d_apack[:, :]).then_inc(SD.h, 16)
            sy.dma_start(out=gatew, in_=d_gate[:, :]).then_inc(SD.h, 16)
        SY(_)
        SD.c += 11 * 16
        sd_consts = SD.c

        def _(g):
            g.iota(maskbig[:, :], [[1, 128]], base=0, channel_multiplier=-1,
                   allow_small_or_imprecise_dtypes=True)
            g.tensor_scalar(ident[:, :], maskbig[:, :], 0.0, None, ALU.is_equal)
            g.tensor_scalar(maskbig[:, :], maskbig[:, :], 0.0, None, ALU.is_ge)
            g.tensor_scalar(maskbig[:, :], maskbig[:, :], -1.0, 1e30, ALU.add, ALU.mult)
            g.iota(jt64[:, :], [[1, 64]], base=0, channel_multiplier=0,
                   allow_small_or_imprecise_dtypes=True)
            g.memset(ones1[:, :], 1.0)
            g.memset(onescol[:, :], 1.0)
            g.memset(zz64[:, :], 0.0)
            g.memset(epsc[:, :], 1e-5)
            g.memset(r8k[:, :], 1.0 / 8192.0)
            g.memset(HT[:, :], 0.0).then_inc(GC.h, 1)
        GP(_)
        GC.c += 1
        gc_pre = GC.c

        def _(sc):
            sc.wait_ge(SD.h, sd_consts)
            sc.activation(out=negea[:, :], in_=alog[:, :], func=AF.Exp).then_inc(AC.h, 1)
        SC(_)
        AC.c += 1
        def _(v, w=AC.c):
            v.wait_ge(AC.h, w)
            v.tensor_scalar(negea[:, :], negea[:, :], -1.0, None, ALU.mult).then_inc(VE.h, 1)
        VC(_)
        VE.c += 1
        ve_pre = VE.c

        xt_tok = [[0] * NH for _ in range(NCH)]
        scr_tok = [0] * NCH
        cum_war = [None] * 4
        wdt_war = [0] * 4
        dtp_war = dict(ve=0, gd=0)
        bph_war = dict(gd=0, tm=0)
        cph_war = dict(gd=0)
        xh_war = dict(tm=0)
        xtst_war = [0] * 4
        btph_war = dict(gd=0, ve=0)
        ffst_war = [0, 0]
        aec_war = dict(tm=0)
        ngc_war = dict(gd=0)
        stats_war = dict(ac=0, ve=0, gc=0)
        bltt_war = [0, 0]
        conv_ve_tok = 0

        for p1 in range(NPH1):
            def _(sy, p1=p1, t=TM.c, v=VE.c, g=GC.c):
                if p1 > 0:
                    sy.wait_ge(TM.h, t)
                    sy.wait_ge(VE.h, v)
                    sy.wait_ge(GC.h, g)
                sy.dma_start(out=xn1, in_=d_x1[p1]).then_inc(SD.h, 16)
            SY(_)
            SD.c += 16
            x_tok = SD.c

            sq_tm_tok = [0] * 32
            for c in range(32):
                def _(sc, c=c, x_tok=x_tok, w=(sq_tm_tok[c - 2] if c >= 2 else 0),
                      w2=stats_war["ve"], w3=stats_war["gc"]):
                    if c == 0:
                        sc.wait_ge(SD.h, x_tok)
                        if w2:
                            sc.wait_ge(VE.h, w2)
                        if w3:
                            sc.wait_ge(GC.h, w3)
                    if w:
                        sc.wait_ge(TM.h, w)
                    sc.activation(out=(tmpx if c % 2 == 0 else tmpg), in_=xn1[:, c * W1:(c + 1) * W1],
                                  func=AF.Square).then_inc(AC.h, 1)
                SC(_)
                AC.c += 1
                def _(te, c=c, ac_sq=AC.c, x_tok=x_tok, ve_gate=VE.c):
                    te.wait_ge(AC.h, ac_sq)
                    if c == 0:
                        te.wait_ge(SD.h, x_tok)
                        te.wait_ge(VE.h, ve_gate)
                        for i in range(4):
                            bank_wait(te, i)
                    sq = tmpx if c % 2 == 0 else tmpg
                    te.matmul(pb[0][0:1, 0:512], onescol[:, :], xn1[:, c * W1 + 3:(c + 1) * W1],
                              start=(c == 0), stop=(c == 31))
                    te.matmul(pb[1][0:1, 0:512], onescol[:, :], sq[:, 3:W1],
                              start=(c == 0), stop=(c == 31))
                    te.matmul(pb[2][0:1, 0:3], onescol[:, :], xn1[:, c * W1:c * W1 + 3],
                              start=(c == 0), stop=(c == 31))
                    te.matmul(pb[3][0:1, 0:3], onescol[:, :], sq[:, 0:3],
                              start=(c == 0), stop=(c == 31)).then_inc(TM.h, 1)
                TE(_)
                TM.c += 1
                sq_tm_tok[c] = TM.c
            tm_stats = TM.c

            def _(v, tm_stats=tm_stats, w=stats_war["ac"]):
                v.wait_ge(TM.h, tm_stats)
                if w:
                    v.wait_ge(AC.h, w)
                v.tensor_copy(srow[0:1, 3:W1], pb[0][0:1, 0:512])
                v.tensor_copy(srow[0:1, 0:3], pb[2][0:1, 0:3])
                v.tensor_copy(s2row[0:1, 3:W1], pb[1][0:1, 0:512])
                v.tensor_copy(s2row[0:1, 0:3], pb[3][0:1, 0:3])
                v.drain()
                v.tensor_scalar(srow, srow, 1.0 / HD, None, ALU.mult)
                v.tensor_scalar(s2row, s2row, 1.0 / HD, None, ALU.mult)
                v.drain()
                v.tensor_tensor(brow, srow, srow, ALU.mult)
                v.drain()
                v.tensor_tensor(s2row, s2row, brow, ALU.subtract).then_inc(VE.h, 1)
            VC(_)
            VE.c += 1
            for i in range(4):
                pbank_free[i] = (VE, VE.c)
            def _(sc, w=VE.c, wg=gc_pre):
                sc.wait_ge(VE.h, w)
                sc.wait_ge(GC.h, wg)
                sc.activation(out=arow, in_=s2row, func=AF.Sqrt, bias=epsc[0:1, :], scale=1.0).then_inc(AC.h, 1)
            SC(_)
            AC.c += 1
            stats_war["ac"] = AC.c
            def _(v, w=AC.c, p1=p1):
                v.wait_ge(AC.h, w)
                v.reciprocal(arow, arow)
                v.drain()
                v.scalar_tensor_tensor(brow, srow, -1.0, arow, ALU.mult, ALU.mult).then_inc(VE.h, 1)
            VC(_)
            VE.c += 1
            def _(g, p1=p1, w=VE.c):
                g.wait_ge(VE.h, w)
                g.dma_start(out=s_ab[0:1, p1 * PH1:(p1 + 1) * PH1], in_=arow[0:1, 3:W1]).then_inc(GD.h, 16)
                g.dma_start(out=s_ab[1:2, p1 * PH1:(p1 + 1) * PH1], in_=brow[0:1, 3:W1]).then_inc(GD.h, 16)
            GP(_)
            GD.c += 32

            def _(te, w=VE.c):
                te.wait_ge(VE.h, w)
                for i in range(4, 8):
                    bank_wait(te, i)
                te.matmul(pb[4][:, 0:512], ones1[0:1, :], arow[0:1, 0:512], start=True, stop=True)
                te.matmul(pb[5][:, 0:3], ones1[0:1, :], arow[0:1, 512:515], start=True, stop=True)
                te.matmul(pb[6][:, 0:512], ones1[0:1, :], brow[0:1, 0:512], start=True, stop=True)
                te.matmul(pb[7][:, 0:3], ones1[0:1, :], brow[0:1, 512:515], start=True, stop=True).then_inc(TM.h, 1)
            TE(_)
            TM.c += 1
            def _(sc, w=TM.c):
                sc.wait_ge(TM.h, w)
                sc.copy(arep[:, 0:512], pb[4][:, 0:512])
                sc.copy(arep[:, 512:515], pb[5][:, 0:3])
                sc.copy(brep[:, 0:512], pb[6][:, 0:512])
                sc.copy(brep[:, 512:515], pb[7][:, 0:3]).then_inc(AC.h, 1)
            SC(_)
            AC.c += 1
            for i in range(4, 8):
                pbank_free[i] = (AC, AC.c)
            ac_rep = AC.c

            def _(v, w=ac_rep):
                v.wait_ge(AC.h, w)
                for c in range(0, 32, 4):
                    v.tensor_tensor(tmpx, xn1[:, c * W1:(c + 1) * W1], arep, ALU.mult)
                    v.tensor_tensor(tmpv2, xn1[:, (c + 2) * W1:(c + 3) * W1], arep, ALU.mult)
                    v.tensor_tensor(xn1[:, c * W1:(c + 1) * W1], tmpx, brep, ALU.add)
                    v.tensor_tensor(xn1[:, (c + 2) * W1:(c + 3) * W1], tmpv2, brep, ALU.add)
                v.drain()
                v.tensor_copy(srow[0:1, 0:1], srow[0:1, 0:1]).then_inc(VE.h, 1)
            VC(_)
            VE.c += 1
            ve_xn = VE.c
            stats_war["ve"] = VE.c
            def _(g, w=ac_rep):
                g.wait_ge(AC.h, w)
                for c in range(1, 32, 2):
                    g.tensor_tensor(tmpg, xn1[:, c * W1:(c + 1) * W1], arep, ALU.mult)
                    g.tensor_tensor(xn1[:, c * W1:(c + 1) * W1], tmpg, brep, ALU.add)
                g.memset(aerow[0:1, 0:1], 1.0).then_inc(GC.h, 1)
            GP(_)
            GC.c += 1
            gc_xn = GC.c
            stats_war["gc"] = GC.c

            for f in range(NF1):
                si = f % 2
                ld = load_slab(si, d_w1a[f])
                bA, bB = (0, 1) if f % 2 == 0 else (2, 3)
                def _(te, f=f, si=si, ld=ld, bA=bA, bB=bB, ve_xn=ve_xn, gc_xn=gc_xn):
                    te.wait_ge(SD.h, ld)
                    if f == 0:
                        te.wait_ge(VE.h, ve_xn)
                        te.wait_ge(GC.h, gc_xn)
                    bank_wait(te, bA)
                    bank_wait(te, bB)
                    for k in range(32):
                        te.matmul(pb[bA][:, 0:512], slab[si][:, k * 128:(k + 1) * 128],
                                  xn1[:, k * W1 + 3:(k + 1) * W1], start=(k == 0), stop=(k == 31))
                    ins = None
                    for k in range(32):
                        ins = te.matmul(pb[bB][:, 0:3], slab[si][:, k * 128:(k + 1) * 128],
                                        xn1[:, k * W1:k * W1 + 3], start=(k == 0), stop=(k == 31))
                    ins.then_inc(TM.h, 1)
                TE(_)
                TM.c += 1
                slab_free[si] = (TM, TM.c)
                tm_f = TM.c

                if f == 0:
                    def _(sc, tm_f=tm_f, bA=bA, w1=dtp_war["ve"], w2=dtp_war["gd"]):
                        sc.wait_ge(TM.h, tm_f)
                        if w1:
                            sc.wait_ge(VE.h, w1)
                        if w2:
                            sc.wait_ge(GD.h, w2)
                        sc.activation(out=dtp1, in_=pb[bA][0:64, 0:512], func=AF.Abs,
                                      bias=dtb[:, :])
                        sc.activation(out=loga1, in_=dtp1, func=AF.Exp, scale=-1.0)
                        sc.activation(out=tmpx[0:64, 0:512], in_=loga1, func=AF.Ln, bias=1.0)
                        sc.activation(out=tmpg[0:64, 0:512], in_=pb[bA][0:64, 0:512],
                                      func=AF.Relu, bias=dtb[:, :]).then_inc(AC.h, 1)
                    SC(_)
                    AC.c += 1
                    pbank_free[bA] = (AC, AC.c)
                    pbank_free[bB] = (AC, AC.c)
                    def _(v, w=AC.c):
                        v.wait_ge(AC.h, w)
                        v.wait_ge(VE.h, ve_pre)
                        v.tensor_tensor(dtp1, tmpx[0:64, 0:512], tmpg[0:64, 0:512], ALU.add)
                        v.drain()
                        v.tensor_scalar(loga1, dtp1, negea[:, :], None, ALU.mult).then_inc(VE.h, 1)
                    VC(_)
                    VE.c += 1
                    stats_war["ve"] = VE.c
                    for c4 in range(4):
                        cg = p1 * 4 + c4
                        cw = cum_war[c4]
                        def _(v, c4=c4, cw=cw, w_ngc=ngc_war["gd"]):
                            if cw is not None:
                                v.wait_ge(AC.h, cw["ac"])
                                v.wait_ge(GD.h, cw["gd"])
                            if w_ngc:
                                v.wait_ge(GD.h, w_ngc)
                            if c4 == 0:
                                v.drain()
                            v.tensor_tensor_scan(cumc1[c4], loga1[:, c4 * 128:(c4 + 1) * 128],
                                                 zz64[:, :], 0.0, ALU.add, ALU.add)
                            v.drain()
                            v.tensor_scalar(ngc1, cumc1[c4], -1.0, None, ALU.mult).then_inc(VE.h, 1)
                        VC(_)
                        VE.c += 1
                        ve_cum = VE.c
                        def _(g, c4=c4, cg=cg, ve_cum=ve_cum):
                            g.wait_ge(VE.h, ve_cum)
                            g.dma_start(out=s_cum[cg], in_=cumc1[c4]).then_inc(GD.h, 16)
                            g.dma_start(out=s_ncum[cg], in_=ngc1).then_inc(GD.h, 16)
                            g.dma_start(out=s_dtp[cg], in_=dtp1[:, c4 * 128:(c4 + 1) * 128]).then_inc(GD.h, 16)
                        GP(_)
                        GD.c += 48
                        gd_cum = GD.c
                        ngc_war["gd"] = GD.c
                        def _(sc, c4=c4, ve_cum=ve_cum, w=aec_war["tm"]):
                            sc.wait_ge(VE.h, ve_cum)
                            if w:
                                sc.wait_ge(TM.h, w)
                            sc.activation(out=wr1, in_=cumc1[c4], func=AF.Exp,
                                          bias=cumc1[c4][:, 127:128], scale=-1.0)
                            sc.activation(out=aec[:, :], in_=cumc1[c4][:, 127:128],
                                          func=AF.Exp).then_inc(AC.h, 1)
                        SC(_)
                        AC.c += 1
                        ac_wr = AC.c
                        def _(v, c4=c4, ac_wr=ac_wr, w=wdt_war[c4]):
                            v.wait_ge(AC.h, ac_wr)
                            if w:
                                v.wait_ge(TM.h, w)
                            v.tensor_tensor(wdt1[c4], wr1, dtp1[:, c4 * 128:(c4 + 1) * 128],
                                            ALU.mult).then_inc(VE.h, 1)
                        VC(_)
                        VE.c += 1
                        ve_wdt = VE.c
                        def _(te, c4=c4, ve_wdt=ve_wdt):
                            te.wait_ge(VE.h, ve_wdt)
                            bank_wait(te, 6)
                            te.transpose(pb[6][:, 0:64], wdt1[c4], ident[0:64, 0:64])
                            te.transpose(pb[6][0:1, 64:128], aec[:, :], ident[0:64, 0:64]).then_inc(TM.h, 1)
                        TE(_)
                        TM.c += 1
                        wdt_war[c4] = TM.c
                        aec_war["tm"] = TM.c
                        def _(sc, c4=c4, w=TM.c):
                            sc.wait_ge(TM.h, w)
                            sc.copy(wdtT1[c4], pb[6][:, 0:64])
                            sc.copy(aerow[0:1, :], pb[6][0:1, 64:128]).then_inc(AC.h, 1)
                        SC(_)
                        AC.c += 1
                        pbank_free[6] = (AC, AC.c)
                        def _(te, w=AC.c):
                            te.wait_ge(AC.h, w)
                            bank_wait(te, 7)
                            te.matmul(pb[7][:, 0:64], ones1[0:1, :], aerow[0:1, :],
                                      start=True, stop=True).then_inc(TM.h, 1)
                        TE(_)
                        TM.c += 1
                        def _(sc, c4=c4, w=TM.c):
                            sc.wait_ge(TM.h, w)
                            sc.copy(ae1[c4], pb[7][:, 0:64]).then_inc(AC.h, 1)
                        SC(_)
                        AC.c += 1
                        pbank_free[7] = (AC, AC.c)
                        cum_war[c4] = dict(ac=AC.c, gd=gd_cum)
                    dtp_war["ve"] = VE.c
                    dtp_war["gd"] = GD.c
                    continue

                def _(sc, tm_f=tm_f, bA=bA, bB=bB, w=conv_ve_tok):
                    sc.wait_ge(TM.h, tm_f)
                    if w:
                        sc.wait_ge(VE.h, w)
                    sc.copy(xbcf[:, 3:W1], pb[bA][:, 0:512])
                    sc.copy(xbcf[:, 0:3], pb[bB][:, 0:3]).then_inc(AC.h, 1)
                SC(_)
                AC.c += 1
                pbank_free[bA] = (AC, AC.c)
                pbank_free[bB] = (AC, AC.c)
                ac_evac = AC.c

                fc = f - 1
                def _(v, fc=fc, ac_evac=ac_evac):
                    v.wait_ge(AC.h, ac_evac)
                    v.tensor_scalar(cvs, xbcf[:, 0:512], convw[:, fc * 4:fc * 4 + 1], None, ALU.mult)
                    v.tensor_scalar(cvs2, xbcf[:, 1:513], convw[:, fc * 4 + 1:fc * 4 + 2], None, ALU.mult)
                    v.scalar_tensor_tensor(cvs, xbcf[:, 2:514], convw[:, fc * 4 + 2:fc * 4 + 3],
                                           cvs, ALU.mult, ALU.add)
                    v.scalar_tensor_tensor(cvs2, xbcf[:, 3:515], convw[:, fc * 4 + 3:fc * 4 + 4],
                                           cvs2, ALU.mult, ALU.add)
                    v.drain()
                    v.tensor_tensor(cvs, cvs, cvs2, ALU.add).then_inc(VE.h, 1)
                VC(_)
                VE.c += 1
                conv_ve_tok = VE.c
                ve_conv = VE.c

                dst = bph if f == 1 else (cph if f == 2 else xh)
                war = (bph_war if f == 1 else (cph_war if f == 2 else xh_war))
                def _(sc, dst=dst, ve_conv=ve_conv, w1=war.get("gd", 0), w2=war.get("tm", 0)):
                    sc.wait_ge(VE.h, ve_conv)
                    if w1:
                        sc.wait_ge(GD.h, w1)
                    if w2:
                        sc.wait_ge(TM.h, w2)
                    sc.activation(out=dst, in_=cvs, func=AF.Silu).then_inc(AC.h, 1)
                SC(_)
                AC.c += 1
                ac_silu = AC.c

                if f == 1:
                    for c4 in range(4):
                        cg = p1 * 4 + c4
                        def _(g, c4=c4, cg=cg, ac_silu=ac_silu):
                            g.wait_ge(AC.h, ac_silu)
                            g.dma_start(out=s_b[cg], in_=bph[:, c4 * 128:(c4 + 1) * 128]).then_inc(GD.h, 16)
                        GP(_)
                        GD.c += 16
                        def _(te, c4=c4, ac_silu=ac_silu):
                            te.wait_ge(AC.h, ac_silu)
                            bank_wait(te, 6)
                            te.transpose(pb[6][:, 0:128], bph[:, c4 * 128:(c4 + 1) * 128],
                                         ident[:, :]).then_inc(TM.h, 1)
                        TE(_)
                        TM.c += 1
                        bph_war["tm"] = TM.c
                        def _(sc, c4=c4, w=TM.c, w2=btph_war["gd"], w3=btph_war["ve"]):
                            sc.wait_ge(TM.h, w)
                            if c4 == 0:
                                if w2:
                                    sc.wait_ge(GD.h, w2)
                                if w3:
                                    sc.wait_ge(VE.h, w3)
                            sc.copy(btph[:, c4 * 128:(c4 + 1) * 128], pb[6][:, 0:128]).then_inc(AC.h, 1)
                        SC(_)
                        AC.c += 1
                        pbank_free[6] = (AC, AC.c)
                        def _(g, c4=c4, cg=cg, w=AC.c):
                            g.wait_ge(AC.h, w)
                            g.dma_start(out=s_bt[cg], in_=btph[:, c4 * 128:(c4 + 1) * 128]).then_inc(GD.h, 16)
                        GP(_)
                        GD.c += 16
                        scr_tok[cg] = GD.c
                    bph_war["gd"] = GD.c
                    btph_war["gd"] = GD.c
                elif f == 2:
                    for c4 in range(4):
                        cg = p1 * 4 + c4
                        def _(g, c4=c4, cg=cg, ac_silu=ac_silu):
                            g.wait_ge(AC.h, ac_silu)
                            g.dma_start(out=s_c[cg], in_=cph[:, c4 * 128:(c4 + 1) * 128]).then_inc(GD.h, 16)
                        GP(_)
                        GD.c += 16
                        scr_tok[cg] = max(scr_tok[cg], GD.c)
                    cph_war["gd"] = GD.c
                else:
                    h = f - 3
                    for c4 in range(4):
                        cg = p1 * 4 + c4
                        bsel = 6 if c4 % 2 == 0 else 7
                        def _(te, c4=c4, ac_silu=ac_silu, bsel=bsel):
                            te.wait_ge(AC.h, ac_silu)
                            bank_wait(te, bsel)
                            te.transpose(pb[bsel][:, 0:128], xh[:, c4 * 128:(c4 + 1) * 128],
                                         ident[:, :]).then_inc(TM.h, 1)
                        TE(_)
                        TM.c += 1
                        xh_war["tm"] = TM.c
                        def _(sc, c4=c4, w=TM.c, bsel=bsel, w2=xtst_war[c4]):
                            sc.wait_ge(TM.h, w)
                            if w2:
                                sc.wait_ge(GD.h, w2)
                            sc.copy(xtst[c4], pb[bsel][:, 0:128]).then_inc(AC.h, 1)
                        SC(_)
                        AC.c += 1
                        pbank_free[bsel] = (AC, AC.c)
                        def _(g, c4=c4, cg=cg, h=h, w=AC.c):
                            g.wait_ge(AC.h, w)
                            g.dma_start(out=s_xt[cg, :, h, :], in_=xtst[c4]).then_inc(GD.h, 16)
                        GP(_)
                        GD.c += 16
                        xt_tok[cg][h] = GD.c
                        xtst_war[c4] = GD.c

            for c4 in range(4):
                cg = p1 * 4 + c4
                for hgi in range(NH // HG):
                    gsel = hgi % 2
                    def _(g, cg=cg, hgi=hgi, gsel=gsel, w=xt_tok[cg][NH - 1], t=TM.c):
                        g.wait_ge(GD.h, w)
                        if t:
                            g.wait_ge(TM.h, t)
                        g.dma_start(out=xtg[gsel][:, :],
                                    in_=s_xt[cg, :, hgi * HG:(hgi + 1) * HG, :]
                                    .rearrange("p h q -> p (h q)")).then_inc(GD.h, 16)
                    GP(_)
                    GD.c += 16
                    gd_xtg = GD.c
                    for hh in range(HG):
                        h = hgi * HG + hh
                        pi = h % 2
                        def _(v, c4=c4, h=h, pi=pi, w=TM.c):
                            if w:
                                v.wait_ge(TM.h, w)
                            v.tensor_scalar(btw1[pi], btph[:, c4 * 128:(c4 + 1) * 128],
                                            wdtT1[c4][:, h:h + 1], None, ALU.mult).then_inc(VE.h, 1)
                        VC(_)
                        VE.c += 1
                        btph_war["ve"] = VE.c
                        ve_btw = VE.c
                        bsel = 6 if h % 2 == 0 else 7
                        def _(te, pi=pi, hh=hh, gsel=gsel, bsel=bsel, ve_btw=ve_btw, gd_xtg=gd_xtg):
                            te.wait_ge(VE.h, ve_btw)
                            te.wait_ge(GD.h, gd_xtg)
                            bank_wait(te, bsel)
                            te.matmul(pb[bsel][:, 0:128], btw1[pi],
                                      xtg[gsel][:, hh * 128:(hh + 1) * 128],
                                      start=True, stop=True).then_inc(TM.h, 1)
                        TE(_)
                        TM.c += 1
                        def _(v, c4=c4, h=h, bsel=bsel, w=TM.c):
                            v.wait_ge(TM.h, w)
                            v.scalar_tensor_tensor(HT[:, h * 128:(h + 1) * 128],
                                                   HT[:, h * 128:(h + 1) * 128],
                                                   ae1[c4][:, h:h + 1], pb[bsel][:, 0:128],
                                                   ALU.mult, ALU.add).then_inc(VE.h, 1)
                        VC(_)
                        VE.c += 1
                        pbank_free[bsel] = (VE, VE.c)

            def _(te):
                bank_wait(te, 4)
                ins = None
                for k in range(32):
                    ins = te.matmul(pb[4][0:4, 0:512], gatew[:, k * 4:(k + 1) * 4],
                                    xn1[:, k * W1 + 3:(k + 1) * W1], start=(k == 0), stop=(k == 31))
                ins.then_inc(TM.h, 1)
            TE(_)
            TM.c += 1
            def _(v, w=TM.c):
                v.wait_ge(TM.h, w)
                v.tensor_scalar(scph, pb[4][0:4, 0:512], gateb[:, :], None, ALU.add).then_inc(VE.h, 1)
            VC(_)
            VE.c += 1
            pbank_free[4] = (VE, VE.c)
            ve_scg = VE.c

            for c4 in range(4):
                pi = c4 % 2
                def _(te, c4=c4, ve_scg=ve_scg):
                    te.wait_ge(VE.h, ve_scg)
                    bank_wait(te, 5)
                    te.transpose(pb[5][:, 0:4], scph[0:4, c4 * 128:(c4 + 1) * 128],
                                 ident[0:4, 0:4]).then_inc(TM.h, 1)
                TE(_)
                TM.c += 1
                def _(v, pi=pi, w=TM.c):
                    v.wait_ge(TM.h, w)
                    v.tensor_copy(scT[pi], pb[5][:, 0:4])
                    v.tensor_copy(bestc, pb[5][:, 0:1])
                    v.memset(rcol[pi], float(RANKS[0]))
                    for e in range(1, 4):
                        v.drain()
                        v.tensor_tensor(dstep, scT[pi][:, e:e + 1], bestc, ALU.is_gt)
                        v.tensor_scalar(scT[pi][:, 0:1], rcol[pi], -1.0, float(RANKS[e]),
                                        ALU.mult, ALU.add)
                        v.drain()
                        v.tensor_tensor(dstep, dstep, scT[pi][:, 0:1], ALU.mult)
                        v.drain()
                        v.tensor_tensor(rcol[pi], rcol[pi], dstep, ALU.add)
                        v.tensor_tensor(bestc, bestc, scT[pi][:, e:e + 1], ALU.max)
                    v.drain()
                    v.tensor_scalar(maskj[pi], jt64[:, :], rcol[pi], None, ALU.is_lt).then_inc(VE.h, 1)
                VC(_)
                VE.c += 1
                pbank_free[5] = (VE, VE.c)
                ve_mask = VE.c
                def _(te, c4=c4, ve_mask=ve_mask):
                    te.wait_ge(VE.h, ve_mask)
                    bank_wait(te, 4)
                    ins = None
                    for k in range(32):
                        ins = te.matmul(pb[4][:, 0:64],
                                        xn1[:, k * W1 + 3 + c4 * 128:k * W1 + 3 + (c4 + 1) * 128],
                                        apk[:, k * 64:(k + 1) * 64], start=(k == 0), stop=(k == 31))
                    ins.then_inc(TM.h, 1)
                TE(_)
                TM.c += 1
                def _(v, pi=pi, w=TM.c):
                    v.wait_ge(TM.h, w)
                    v.drain()
                    v.tensor_tensor(p64m[pi], maskj[pi], pb[4][:, 0:64], ALU.mult).then_inc(VE.h, 1)
                VC(_)
                VE.c += 1
                pbank_free[4] = (VE, VE.c)
                def _(te, pi=pi, w=VE.c):
                    te.wait_ge(VE.h, w)
                    bank_wait(te, 5)
                    te.transpose(pb[5][0:64, 0:128], p64m[pi], ident[:, :]).then_inc(TM.h, 1)
                TE(_)
                TM.c += 1
                def _(sc, c4=c4, w=TM.c):
                    sc.wait_ge(TM.h, w)
                    sc.copy(p64T[:, c4 * 128:(c4 + 1) * 128], pb[5][0:64, 0:128]).then_inc(AC.h, 1)
                SC(_)
                AC.c += 1
                pbank_free[5] = (AC, AC.c)
            ac_p64 = AC.c

            for o in range(32):
                si = o % 2
                ld = load_slab(si, d_wffn[o])
                def _(sy, o=o, w=bltt_war[o % 2]):
                    if w:
                        sy.wait_ge(TM.h, w)
                    sy.dma_start(out=bltt[o % 2][:, :], in_=d_blt[:, o * 128:(o + 1) * 128]).then_inc(SD.h, 16)
                SY(_)
                SD.c += 16
                ld2 = SD.c
                bsel = o % 4
                def _(te, o=o, si=si, ld=ld, ld2=ld2, bsel=bsel, ac_p64=ac_p64):
                    te.wait_ge(SD.h, max(ld, ld2))
                    if o == 0:
                        te.wait_ge(AC.h, ac_p64)
                    bank_wait(te, bsel)
                    for k in range(32):
                        te.matmul(pb[bsel][:, 0:512], slab[si][:, k * 128:(k + 1) * 128],
                                  xn1[:, k * W1 + 3:(k + 1) * W1], start=(k == 0), stop=False)
                    te.matmul(pb[bsel][:, 0:512], bltt[o % 2], p64T,
                              start=False, stop=True).then_inc(TM.h, 1)
                TE(_)
                TM.c += 1
                slab_free[si] = (TM, TM.c)
                bltt_war[o % 2] = TM.c
                sti = o % 2
                def _(v, o=o, sti=sti, w=TM.c, bsel=bsel, w2=ffst_war[sti]):
                    v.wait_ge(TM.h, w)
                    if w2:
                        v.wait_ge(GD.h, w2)
                    v.tensor_scalar(ffst[sti], pb[bsel][:, 0:512], bffn[:, o:o + 1], None,
                                    ALU.add).then_inc(VE.h, 1)
                VC(_)
                VE.c += 1
                pbank_free[bsel] = (VE, VE.c)
                def _(g, o=o, sti=sti, p1=p1, w=VE.c):
                    g.wait_ge(VE.h, w)
                    g.dma_start(out=s_ffn[o][:, p1 * PH1:(p1 + 1) * PH1], in_=ffst[sti]).then_inc(GD.h, 16)
                GP(_)
                GD.c += 16
                ffst_war[sti] = GD.c

        def _(v):
            v.tensor_scalar(ztile[:, 0:8192], HT[:, :], hinv[:, :], None, ALU.mult).then_inc(VE.h, 1)
        VC(_)
        VE.c += 1
        def _(g, w=VE.c, base=GD.c):
            g.wait_ge(VE.h, w)
            g.dma_start(out=cc_in.ap(), in_=ztile[:, 0:8192]).then_inc(GD.h, 16)
            g.wait_ge(GD.h, base + 16)
            g.collective_compute(
                "AllReduce", ALU.add,
                replica_groups=[[0, 1], [2, 3], [4, 5], [6, 7]],
                ins=[cc_in.ap().opt()],
                outs=[cc_out.ap().opt()],
            ).then_inc(CCS.h, 1)
            g.wait_ge(CCS.h, 1)
            g.dma_start(out=HT[:, :], in_=cc_out.ap()).then_inc(GD.h, 16)
        GP(_)
        GD.c += 32
        CCS.c += 1
        def _(v, w=GD.c):
            v.wait_ge(GD.h, w)
            v.tensor_scalar(HT[:, :], HT[:, :], hmask[:, :], None, ALU.mult).then_inc(VE.h, 1)
        VC(_)
        VE.c += 1

        def _(v):
            v.tensor_copy(aerow[0:1, 0:1], aerow[0:1, 0:1]).then_inc(VE.h, 1)
        VC(_)
        VE.c += 1
        def _(sc):
            sc.copy(aerow[0:1, 1:2], aerow[0:1, 1:2]).then_inc(AC.h, 1)
        SC(_)
        AC.c += 1
        def _(g):
            g.memset(aerow[0:1, 2:3], 1.0).then_inc(GC.h, 1)
        GP(_)
        GC.c += 1
        def _(te, wv=VE.c, wa=AC.c, wg=GC.c, wsd=SD.c, wgd=GD.c):
            te.wait_ge(VE.h, wv)
            te.wait_ge(AC.h, wa)
            te.wait_ge(GC.h, wg)
            te.wait_ge(SD.h, wsd)
            te.wait_ge(GD.h, wgd)
            te.nop().then_inc(TM.h, 1)
        TE(_)
        TM.c += 1
        barrier_tok = TM.c
        for EN in (VC, SC, GP, SY):
            def _(e, barrier_tok=barrier_tok):
                e.wait_ge(TM.h, barrier_tok)
            EN(_)

        def _(g):
            g.memset(onesrow[:, :], 1.0).then_inc(GC.h, 1)
        GP(_)
        GC.c += 1
        def _(g, w=GC.c):
            g.wait_ge(GC.h, w)
            for j in range(LRG):
                g.dma_start(out=LT[0:1, j * 128:(j + 1) * 128], in_=onesrow[0:1, :]).then_inc(GD.h, 16)
                g.dma_start(out=RT[1:2, j * 128:(j + 1) * 128], in_=onesrow[0:1, :]).then_inc(GD.h, 16)
        GP(_)
        GD.c += 32 * LRG
        gd_ones = GD.c

        outst_war = [0, 0]
        for p2 in range(NPH2):
            t0 = p2 * PH2
            def _(g, t0=t0, wa=AC.c, wt=TM.c):
                g.wait_ge(AC.h, wa)
                g.wait_ge(TM.h, wt)
                g.dma_start(out=arowt, in_=s_ab[0:1, t0:t0 + PH2]).then_inc(GD.h, 16)
                g.dma_start(out=browt, in_=s_ab[1:2, t0:t0 + PH2]).then_inc(GD.h, 16)
            GP(_)
            GD.c += 32
            def _(te, w=GD.c):
                te.wait_ge(GD.h, w)
                bank_wait(te, 4)
                bank_wait(te, 5)
                te.matmul(pb[4][:, 0:PH2], ones1[0:1, :], arowt, start=True, stop=True)
                te.matmul(pb[5][:, 0:PH2], ones1[0:1, :], browt,
                          start=True, stop=True).then_inc(TM.h, 1)
            TE(_)
            TM.c += 1
            def _(sc, w=TM.c, wv=VE.c):
                sc.wait_ge(TM.h, w)
                sc.wait_ge(VE.h, wv)
                sc.copy(arep2, pb[4][:, 0:PH2])
                sc.copy(brep2, pb[5][:, 0:PH2]).then_inc(AC.h, 1)
            SC(_)
            AC.c += 1
            pbank_free[4] = (AC, AC.c)
            pbank_free[5] = (AC, AC.c)
            ac_ab2 = AC.c

            for c in range(32):
                def _(sy, p2=p2, c=c, w=(TM.c if c == 0 else 0), wv=VE.c, wg=gd_ones):
                    if w:
                        sy.wait_ge(TM.h, w)
                    sy.wait_ge(VE.h, wv)
                    if c == 0 and p2 == 0:
                        sy.wait_ge(GD.h, wg)
                    sy.dma_start(out=xst2, in_=d_x2[p2][:, c * PH2:(c + 1) * PH2]).then_inc(SD.h, 16)
                SY(_)
                SD.c += 16
                def _(v, c=c, w=SD.c, ac_ab2=ac_ab2, wt=(TM.c if c == 0 else 0)):
                    v.wait_ge(SD.h, w)
                    if c == 0:
                        v.wait_ge(AC.h, ac_ab2)
                        if wt:
                            v.wait_ge(TM.h, wt)
                    v.tensor_tensor(tmp2, xst2, arep2, ALU.mult)
                    v.drain()
                    v.tensor_tensor(xn2[:, c * PH2:(c + 1) * PH2], tmp2, brep2, ALU.add).then_inc(VE.h, 1)
                VC(_)
                VE.c += 1
            ve_xn2 = VE.c

            for f in range(64):
                si = f % 2
                ld = load_slab(si, d_w1z[f])
                bsel = f % 4
                def _(te, f=f, si=si, ld=ld, bsel=bsel, ve_xn2=ve_xn2):
                    te.wait_ge(SD.h, ld)
                    if f == 0:
                        te.wait_ge(VE.h, ve_xn2)
                    bank_wait(te, bsel)
                    ins = None
                    for k in range(32):
                        ins = te.matmul(pb[bsel][:, 0:PH2], slab[si][:, k * 128:(k + 1) * 128],
                                        xn2[:, k * PH2:(k + 1) * PH2], start=(k == 0), stop=(k == 31))
                    ins.then_inc(TM.h, 1)
                TE(_)
                TM.c += 1
                slab_free[si] = (TM, TM.c)
                def _(sc, f=f, bsel=bsel, w=TM.c):
                    sc.wait_ge(TM.h, w)
                    sc.copy(ztile[:, f * PH2:(f + 1) * PH2], pb[bsel][:, 0:PH2]).then_inc(AC.h, 1)
                SC(_)
                AC.c += 1
                pbank_free[bsel] = (AC, AC.c)
            def _(sc, wv=VE.c):
                sc.wait_ge(VE.h, wv)
                for f in range(64):
                    ins = sc.activation(out=ztile[:, f * PH2:(f + 1) * PH2],
                                        in_=ztile[:, f * PH2:(f + 1) * PH2], func=AF.Silu)
                ins.then_inc(AC.h, 1)
            SC(_)
            AC.c += 1
            ac_siluz = AC.c

            for c2 in range(2):
                cg = p2 * 2 + c2
                def _(g, cg=cg, w=scr_tok[cg], t=TM.c, wa=AC.c, wv=VE.c):
                    g.wait_ge(GD.h, w)
                    g.wait_ge(TM.h, t)
                    g.wait_ge(AC.h, wa)
                    g.wait_ge(VE.h, wv)
                    g.dma_start(out=packin[0:64, :], in_=s_cum[cg]).then_inc(GD.h, 16)
                    g.dma_start(out=packin[64:128, :], in_=s_dtp[cg]).then_inc(GD.h, 16)
                    g.dma_start(out=dtpc2, in_=s_dtp[cg]).then_inc(GD.h, 16)
                    g.dma_start(out=btc2, in_=s_bt[cg]).then_inc(GD.h, 16)
                    g.dma_start(out=bc2, in_=s_b[cg]).then_inc(GD.h, 16)
                    g.dma_start(out=cc2, in_=s_c[cg]).then_inc(GD.h, 16)
                GP(_)
                GD.c += 96
                gd_chunk = GD.c

                def _(sc, w=gd_chunk):
                    sc.wait_ge(GD.h, w)
                    sc.activation(out=wr2, in_=packin[0:64, :], func=AF.Exp,
                                  bias=packin[0:64, 127:128], scale=-1.0)
                    sc.activation(out=aec[:, :], in_=packin[0:64, 127:128], func=AF.Exp).then_inc(AC.h, 1)
                SC(_)
                AC.c += 1
                def _(v, w=AC.c):
                    v.wait_ge(AC.h, w)
                    v.memset(rssum, 0.0)
                    v.tensor_tensor(wdt2, wr2, dtpc2, ALU.mult).then_inc(VE.h, 1)
                VC(_)
                VE.c += 1
                ve_wdt2 = VE.c
                def _(te, ve_wdt2=ve_wdt2):
                    te.wait_ge(VE.h, ve_wdt2)
                    bank_wait(te, 6)
                    bank_wait(te, 7)
                    te.transpose(pb[6][:, 0:128], packin[:, :], ident[:, :])
                    te.transpose(pb[7][:, 0:64], wdt2, ident[0:64, 0:64])
                    te.transpose(pb[7][0:1, 64:128], aec[:, :], ident[0:64, 0:64])
                    te.matmul(pb[7][:, 128:256], bc2, cc2, start=True, stop=True).then_inc(TM.h, 1)
                TE(_)
                TM.c += 1
                def _(sc, w=TM.c):
                    sc.wait_ge(TM.h, w)
                    sc.copy(packT, pb[6][:, 0:128])
                    sc.copy(wdtT2, pb[7][:, 0:64])
                    sc.copy(aerow[0:1, :], pb[7][0:1, 64:128])
                    sc.copy(g2sb, pb[7][:, 128:256])
                    sc.activation(out=ecT, in_=pb[6][:, 0:64], func=AF.Exp).then_inc(AC.h, 1)
                SC(_)
                AC.c += 1
                pbank_free[6] = (AC, AC.c)
                def _(te, w=AC.c):
                    te.wait_ge(AC.h, w)
                    te.matmul(pb[7][:, 64:128], ones1[0:1, :], aerow[0:1, :],
                              start=True, stop=True).then_inc(TM.h, 1)
                TE(_)
                TM.c += 1
                def _(sc, w=TM.c):
                    sc.wait_ge(TM.h, w)
                    sc.copy(ae2, pb[7][:, 64:128]).then_inc(AC.h, 1)
                SC(_)
                AC.c += 1
                pbank_free[7] = (AC, AC.c)
                ac_ae2 = AC.c

                ac_e_war = [0, 0]
                tm_y1_war = [0, 0]
                ve_yg_war = [0, 0]
                tm_ss_war = [0, 0]
                for gi in range(NH // LRG):
                    def _(g, cg=cg, gi=gi, t=TM.c):
                        if gi > 0:
                            g.wait_ge(TM.h, t)
                        g.dma_start(out=LT[1:2, :], in_=s_ncum[cg]
                                    .rearrange("h t -> (h t)")[None, gi * LRG * 128:(gi + 1) * LRG * 128]).then_inc(GD.h, 16)
                        g.dma_start(out=RT[0:1, :], in_=s_cum[cg]
                                    .rearrange("h t -> (h t)")[None, gi * LRG * 128:(gi + 1) * LRG * 128]).then_inc(GD.h, 16)
                    GP(_)
                    GD.c += 32
                    lr_tok = GD.c

                    for hgj in range(LRG // HG):
                        hgi = gi * (LRG // HG) + hgj
                        gsel = hgi % 2
                        def _(g, cg=cg, hgi=hgi, gsel=gsel, w=xt_tok[cg][NH - 1], t=TM.c):
                            g.wait_ge(GD.h, w)
                            g.wait_ge(TM.h, t)
                            g.dma_start(out=xtg[gsel][:, :],
                                        in_=s_xt[cg, :, hgi * HG:(hgi + 1) * HG, :]
                                        .rearrange("p h q -> p (h q)")).then_inc(GD.h, 16)
                        GP(_)
                        GD.c += 16
                        gd_xtg = GD.c

                        for hh in range(HG):
                            h = hgi * HG + hh
                            hl = h - gi * LRG
                            pi = h % 2
                            bD = 0 if h % 2 == 0 else 1
                            bY2 = 2 if h % 2 == 0 else 3
                            bY1 = 4 if h % 2 == 0 else 5
                            bH = 6 if h % 2 == 0 else 7
                            def _(te, h=h, hl=hl, bD=bD, bY2=bY2, lr=lr_tok, gd_xtg=gd_xtg,
                                  ac_ae2=ac_ae2, ve_w=VE.c):
                                te.wait_ge(GD.h, max(lr, gd_xtg))
                                if h == 0:
                                    te.wait_ge(AC.h, ac_ae2)
                                te.wait_ge(VE.h, ve_w)
                                bank_wait(te, bD)
                                te.matmul(pb[bD][:, 0:128], LT[0:2, hl * 128:(hl + 1) * 128],
                                          RT[0:2, hl * 128:(hl + 1) * 128], start=True, stop=True)
                                bank_wait(te, bY2)
                                te.matmul(pb[bY2][:, 0:128], cc2, HT[:, h * 128:(h + 1) * 128],
                                          start=True, stop=True).then_inc(TM.h, 1)
                            TE(_)
                            TM.c += 1
                            def _(v, pi=pi, bD=bD, w=TM.c, w2=ac_e_war[pi]):
                                v.wait_ge(TM.h, w)
                                if w2:
                                    v.wait_ge(AC.h, w2)
                                v.tensor_tensor(esb[pi], maskbig[:, :], pb[bD][:, 0:128], ALU.min).then_inc(VE.h, 1)
                            VC(_)
                            VE.c += 1
                            def _(sc, pi=pi, w=VE.c):
                                sc.wait_ge(VE.h, w)
                                sc.activation(out=esbE[pi], in_=esb[pi], func=AF.Exp).then_inc(AC.h, 1)
                            SC(_)
                            AC.c += 1
                            pbank_free[bD] = (AC, AC.c)
                            ac_e_war[pi] = AC.c
                            def _(v, pi=pi, h=h, w=AC.c, w2=tm_y1_war[pi]):
                                v.wait_ge(AC.h, w)
                                if w2:
                                    v.wait_ge(TM.h, w2)
                                v.scalar_tensor_tensor(wsb[pi], esbE[pi], packT[:, 64 + h:65 + h],
                                                       g2sb, ALU.mult, ALU.mult).then_inc(VE.h, 1)
                            VC(_)
                            VE.c += 1
                            def _(te, pi=pi, hh=hh, gsel=gsel, bY1=bY1, w=VE.c):
                                te.wait_ge(VE.h, w)
                                bank_wait(te, bY1)
                                te.matmul(pb[bY1][:, 0:128], wsb[pi],
                                          xtg[gsel][:, hh * 128:(hh + 1) * 128],
                                          start=True, stop=True).then_inc(TM.h, 1)
                            TE(_)
                            TM.c += 1
                            tm_y1_war[pi] = TM.c
                            def _(v, pi=pi, h=h, hh=hh, gsel=gsel, bY2=bY2, bY1=bY1, w=TM.c):
                                v.wait_ge(TM.h, w)
                                v.tensor_scalar(y2sb[pi], pb[bY2][:, 0:128], ecT[:, h:h + 1], None, ALU.mult)
                                v.scalar_tensor_tensor(ytf[pi], xtg[gsel][:, hh * 128:(hh + 1) * 128],
                                                       dcol[:, h:h + 1], pb[bY1][:, 0:128],
                                                       ALU.mult, ALU.add)
                                v.drain()
                                v.tensor_tensor(ytf[pi], ytf[pi], y2sb[pi], ALU.add).then_inc(VE.h, 1)
                            VC(_)
                            VE.c += 1
                            ve_yt = VE.c
                            def _(v, pi=pi, h=h, w=TM.c):
                                v.wait_ge(TM.h, w)
                                v.tensor_scalar(btw2[pi], btc2, wdtT2[:, h:h + 1], None,
                                                ALU.mult).then_inc(VE.h, 1)
                            VC(_)
                            VE.c += 1
                            ve_btw2 = VE.c
                            def _(te, pi=pi, hh=hh, gsel=gsel, bH=bH, bY2=bY2, ve_btw2=ve_btw2,
                                  ve_yt=ve_yt):
                                te.wait_ge(VE.h, ve_btw2)
                                bank_wait(te, bH)
                                te.matmul(pb[bH][:, 0:128], btw2[pi],
                                          xtg[gsel][:, hh * 128:(hh + 1) * 128],
                                          start=True, stop=True)
                                te.wait_ge(VE.h, ve_yt)
                                te.transpose(pb[bY2][:, 128:256], ytf[pi], ident[:, :]).then_inc(TM.h, 1)
                            TE(_)
                            TM.c += 1
                            tm_hb = TM.c
                            def _(v, h=h, bH=bH, w=tm_hb):
                                v.wait_ge(TM.h, w)
                                v.scalar_tensor_tensor(HT[:, h * 128:(h + 1) * 128],
                                                       HT[:, h * 128:(h + 1) * 128],
                                                       ae2[:, h:h + 1], pb[bH][:, 0:128],
                                                       ALU.mult, ALU.add).then_inc(VE.h, 1)
                            VC(_)
                            VE.c += 1
                            pbank_free[bH] = (VE, VE.c)
                            def _(v, h=h, c2=c2, bY2=bY2, w=tm_hb, wz=ac_siluz):
                                v.wait_ge(TM.h, w)
                                if h == 0:
                                    v.wait_ge(AC.h, wz)
                                zs = ztile[:, h * PH2 + c2 * 128:h * PH2 + (c2 + 1) * 128]
                                v.scalar_tensor_tensor(zs, zs, nw[:, h:h + 1], pb[bY2][:, 128:256],
                                                       ALU.mult, ALU.mult).then_inc(VE.h, 1)
                            VC(_)
                            VE.c += 1
                            pbank_free[bY2] = (VE, VE.c)
                            ve_yg_war[pi] = VE.c
                            def _(sc, pi=pi, h=h, c2=c2, w=VE.c, w2=tm_ss_war[pi]):
                                sc.wait_ge(VE.h, w)
                                if w2:
                                    sc.wait_ge(TM.h, w2)
                                sc.activation(out=sqt2[pi],
                                              in_=ztile[:, h * PH2 + c2 * 128:h * PH2 + (c2 + 1) * 128],
                                              func=AF.Square).then_inc(AC.h, 1)
                            SC(_)
                            AC.c += 1
                            def _(te, pi=pi, bY1=bY1, w=AC.c):
                                te.wait_ge(AC.h, w)
                                bank_wait(te, bY1)
                                te.matmul(pb[bY1][0:1, 128:256], onescol[:, :], sqt2[pi],
                                          start=True, stop=True).then_inc(TM.h, 1)
                            TE(_)
                            TM.c += 1
                            tm_ss_war[pi] = TM.c
                            def _(v, bY1=bY1, w=TM.c):
                                v.wait_ge(TM.h, w)
                                v.tensor_tensor(rssum, rssum, pb[bY1][0:1, 128:256], ALU.add).then_inc(VE.h, 1)
                            VC(_)
                            VE.c += 1
                            pbank_free[bY1] = (VE, VE.c)

                def _(sc, w=VE.c):
                    sc.wait_ge(VE.h, w)
                    sc.activation(out=rrow, in_=rssum, func=AF.Sqrt,
                                  bias=epsc[0:1, :], scale=r8k[0:1, :]).then_inc(AC.h, 1)
                SC(_)
                AC.c += 1
                def _(v, w=AC.c):
                    v.wait_ge(AC.h, w)
                    v.reciprocal(rrow, rrow).then_inc(VE.h, 1)
                VC(_)
                VE.c += 1
                def _(te, w=VE.c):
                    te.wait_ge(VE.h, w)
                    bank_wait(te, 6)
                    te.matmul(pb[6][:, 128:256], ones1[0:1, :], rrow[0:1, :],
                              start=True, stop=True).then_inc(TM.h, 1)
                TE(_)
                TM.c += 1
                def _(sc, w=TM.c):
                    sc.wait_ge(TM.h, w)
                    sc.copy(rrep, pb[6][:, 128:256]).then_inc(AC.h, 1)
                SC(_)
                AC.c += 1
                pbank_free[6] = (AC, AC.c)
                def _(v, c2=c2, w=AC.c):
                    v.wait_ge(AC.h, w)
                    ins = None
                    for h in range(NH):
                        ins = v.tensor_tensor(ztile[:, h * PH2 + c2 * 128:h * PH2 + (c2 + 1) * 128],
                                              ztile[:, h * PH2 + c2 * 128:h * PH2 + (c2 + 1) * 128],
                                              rrep, ALU.mult)
                    ins.then_inc(VE.h, 1)
                VC(_)
                VE.c += 1
            ve_rms = VE.c

            for o in range(32):
                ld1 = load_slab(0, d_wout[o, 0])
                ld2 = load_slab(1, d_wout[o, 1])
                def _(g, o=o, p2=p2, w=outst_war[o % 2]):
                    if w:
                        g.wait_ge(VD.h, w)
                    g.dma_start(out=ffin[o % 2], in_=s_ffn[o][:, p2 * PH2:(p2 + 1) * PH2]).then_inc(GD.h, 16)
                GP(_)
                GD.c += 16
                gd_ffin = GD.c
                def _(sy, o=o, p2=p2, w=(VD.c if (p2 > 0 or o > 0) else 0), wv=VE.c):
                    if w:
                        sy.wait_ge(VD.h, w)
                    elif wv:
                        sy.wait_ge(VE.h, wv)
                    sy.dma_start(out=xst2, in_=d_x2[p2][:, o * PH2:(o + 1) * PH2]).then_inc(SD.h, 16)
                SY(_)
                SD.c += 16
                sd_x2b = SD.c
                bsel = 2 + (o % 2)
                def _(te, o=o, ld1=ld1, ld2=ld2, bsel=bsel, ve_rms=ve_rms):
                    te.wait_ge(SD.h, max(ld1, ld2))
                    if o == 0:
                        te.wait_ge(VE.h, ve_rms)
                    bank_wait(te, bsel)
                    ins = None
                    for k in range(64):
                        si = k // 32
                        kk = k % 32
                        ins = te.matmul(pb[bsel][:, 0:PH2], slab[si][:, kk * 128:(kk + 1) * 128],
                                        ztile[:, k * PH2:(k + 1) * PH2], start=(k == 0), stop=(k == 63))
                    ins.then_inc(TM.h, 1)
                TE(_)
                TM.c += 1
                slab_free[0] = (TM, TM.c)
                slab_free[1] = (TM, TM.c)
                sti = o % 2
                def _(v, o=o, sti=sti, bsel=bsel, w=TM.c, gd_ffin=gd_ffin,
                      sd_x2b=sd_x2b, w2=outst_war[sti]):
                    v.wait_ge(TM.h, w)
                    v.wait_ge(GD.h, gd_ffin)
                    v.wait_ge(SD.h, sd_x2b)
                    if w2:
                        v.wait_ge(VD.h, w2)
                    v.scalar_tensor_tensor(outst[sti], ffin[o % 2], 1.0, pb[bsel][:, 0:PH2],
                                           ALU.mult, ALU.add)
                    v.drain()
                    v.tensor_tensor(outst[sti], outst[sti], xst2, ALU.add).then_inc(VE.h, 1)
                VC(_)
                VE.c += 1
                def _(sc, o=o, sti=sti, p2=p2, w=VE.c):
                    sc.wait_ge(VE.h, w)
                    sc.dma_start(out=d_out[o * 128:(o + 1) * 128, p2 * PH2:(p2 + 1) * PH2],
                                 in_=outst[sti]).then_inc(VD.h, 16)
                SC(_)
                VD.c += 16
                outst_war[sti] = VD.c
                pbank_free[bsel] = (VD, VD.c)

        def _(sy, w=VD.c):
            sy.wait_ge(VD.h, w)
        SY(_)
    return nc


def _pack_lhsT(cols):
    w = cols.shape[1]
    return np.ascontiguousarray(cols.reshape(32, 128, w).transpose(1, 0, 2).reshape(128, 32 * w))


def _pack_wout(cols):
    r = cols.reshape(64, 128, 128)
    out = np.empty((2, 128, 32 * 128), np.float32)
    for half in range(2):
        out[half] = r[half * 32:(half + 1) * 32].transpose(1, 0, 2).reshape(128, 32 * 128)
    return out


def _prep_weights(inputs):
    W_in = np.ascontiguousarray(inputs["L0_attn_W_in"], dtype=np.float32)
    conv_w = np.ascontiguousarray(inputs["L0_attn_conv_w"], dtype=np.float32)
    W_out = np.ascontiguousarray(inputs["L0_attn_W_out"], dtype=np.float32)
    Wf = np.ascontiguousarray(inputs["L0_ffn_W"], dtype=np.float32)
    Af = np.ascontiguousarray(inputs["L0_ffn_A"], dtype=np.float32)
    Bf = np.ascontiguousarray(inputs["L0_ffn_B"], dtype=np.float32)
    gw = np.ascontiguousarray(inputs["L0_ffn_gate_w"], dtype=np.float32)

    tiles = []
    dtcols = np.zeros((HD, 128), np.float32)
    dtcols[:, 0:64] = W_in[:, DSSM + CONVD:]
    tiles.append(_pack_lhsT(dtcols))
    tiles.append(_pack_lhsT(W_in[:, 2 * DSSM:2 * DSSM + 128]))
    tiles.append(_pack_lhsT(W_in[:, 2 * DSSM + 128:2 * DSSM + 256]))
    for j in range(64):
        tiles.append(_pack_lhsT(W_in[:, DSSM + j * 128:DSSM + (j + 1) * 128]))
    w1a = np.stack(tiles)
    w1z = np.stack([_pack_lhsT(W_in[:, j * 128:(j + 1) * 128]) for j in range(64)])
    wout = np.stack([_pack_wout(W_out[:, o * 128:(o + 1) * 128]) for o in range(32)])
    wffn = np.stack([_pack_lhsT(np.ascontiguousarray(Wf[o * 128:(o + 1) * 128, :].T))
                     for o in range(32)])
    apack = np.ascontiguousarray(Af.T.reshape(32, 128, 64).transpose(1, 0, 2).reshape(128, 32 * 64))
    blt = np.ascontiguousarray(Bf.T)
    gate = np.ascontiguousarray(gw.T.reshape(32, 128, 4).transpose(1, 0, 2).reshape(128, 32 * 4))
    cw = np.concatenate([conv_w[DSSM:DSSM + 128], conv_w[DSSM + 128:DSSM + 256],
                         conv_w[0:DSSM]], axis=0)
    convw = np.ascontiguousarray(cw.reshape(66, 128, KC).transpose(1, 0, 2).reshape(128, 66 * KC))
    return dict(w1a=w1a, w1z=w1z, wout=wout, wffn=wffn, apack=apack, blt=blt,
                gatew=gate, convw=convw)


_NC_CACHE = {}


def kernel(**inputs):
    x = np.asarray(inputs["L0_attn_hidden_states"], dtype=np.float32)
    wp = _prep_weights(inputs)
    dt_bias = np.asarray(inputs["L0_attn_dt_bias"], np.float32).reshape(64, 1)
    A_log = np.asarray(inputs["L0_attn_A_log"], np.float32).reshape(64, 1)
    D_param = np.asarray(inputs["L0_attn_D_param"], np.float32)
    norm_w = np.asarray(inputs["L0_attn_norm_weight"], np.float32)
    bf = np.asarray(inputs["L0_ffn_b"], np.float32)
    gb = np.asarray(inputs["L0_ffn_gate_b"], np.float32).reshape(4, 1)
    dcol = np.ascontiguousarray(np.broadcast_to(D_param[None, :], (128, 64)))
    nwp = np.ascontiguousarray(norm_w.reshape(64, 128).T)
    bffn = np.ascontiguousarray(bf.reshape(32, 128).T)

    if "nc" not in _NC_CACHE:
        _NC_CACHE["nc"] = build_program()
    nc = _NC_CACHE["nc"]

    in_maps = []
    for i in range(8):
        b, half = i // 2, i % 2
        t0 = half * T
        xfm = np.zeros((HD, T + 3), np.float32)
        if half == 1:
            xfm[:, 0:3] = x[b, t0 - 3:t0].T
        xfm[:, 3:] = x[b, t0:t0 + T].T
        x1 = np.empty((NPH1, 128, 32 * W1), np.float32)
        for p in range(NPH1):
            x1[p] = xfm[:, p * PH1:p * PH1 + W1].reshape(32, 128, W1).transpose(1, 0, 2).reshape(128, 32 * W1)
        x2 = np.empty((NPH2, 128, 32 * PH2), np.float32)
        for p in range(NPH2):
            x2[p] = xfm[:, 3 + p * PH2:3 + (p + 1) * PH2].reshape(32, 128, PH2).transpose(1, 0, 2).reshape(128, 32 * PH2)
        m = 1.0 if half == 1 else 0.0
        in_maps.append({
            "x1": x1, "x2": x2,
            "w1a": wp["w1a"], "w1z": wp["w1z"], "wout": wp["wout"], "wffn": wp["wffn"],
            "apack": wp["apack"], "blt": wp["blt"], "gatew": wp["gatew"], "convw": wp["convw"],
            "dtb": dt_bias, "alog": A_log, "dcol": dcol, "nw": nwp, "bffn": bffn,
            "gateb": gb,
            "hmask": np.full((128, 1), m, np.float32),
            "hinv": np.full((128, 1), 1.0 - m, np.float32),
        })
    res = run_bass_kernel_spmd(nc, in_maps, core_ids=list(range(8)))
    out = np.empty((4, 4096, 4096), np.float32)
    for i in range(8):
        b, half = i // 2, i % 2
        out[b, half * T:(half + 1) * T, :] = res.results[i]["out"].T
    return out


# version 02
# speedup vs baseline: 1.0544x; 1.0544x over previous
import contextlib
import numpy as np
import concourse.bass as bass
import concourse.mybir as mybir
from concourse.bass_utils import run_bass_kernel_spmd

F32 = mybir.dt.float32
AF = mybir.ActivationFunctionType
ALU = mybir.AluOpType

HD = 4096
DSSM = 8192
NH = 64
KC = 4
EPS = 1e-5
CONVD = DSSM + 256
T = 2048
NCH = 16
PH1 = 512
NPH1 = 4
PH2 = 256
NPH2 = 8
W1 = PH1 + 3
RANKS = (8, 16, 32, 64)
NF1 = 67
HG = 8
LRG = 8


class Sem:
    def __init__(self, stack, nc, name):
        self.h = stack.enter_context(nc.semaphore(name))
        self.c = 0


def build_program():
    nc = bass.Bass("TRN2", target_bir_lowering=False)

    d_x1 = nc.dram_tensor("x1", [NPH1, 128, 32 * W1], F32, kind="ExternalInput").ap()
    d_x2 = nc.dram_tensor("x2", [NPH2, 128, 32 * PH2], F32, kind="ExternalInput").ap()
    d_w1a = nc.dram_tensor("w1a", [NF1, 128, 32 * 128], F32, kind="ExternalInput").ap()
    d_w1z = nc.dram_tensor("w1z", [64, 128, 32 * 128], F32, kind="ExternalInput").ap()
    d_wout = nc.dram_tensor("wout", [32, 2, 128, 32 * 128], F32, kind="ExternalInput").ap()
    d_wffn = nc.dram_tensor("wffn", [32, 128, 32 * 128], F32, kind="ExternalInput").ap()
    d_apack = nc.dram_tensor("apack", [128, 32 * 64], F32, kind="ExternalInput").ap()
    d_blt = nc.dram_tensor("blt", [64, HD], F32, kind="ExternalInput").ap()
    d_gate = nc.dram_tensor("gatew", [128, 32 * 4], F32, kind="ExternalInput").ap()
    d_convw = nc.dram_tensor("convw", [128, 66 * KC], F32, kind="ExternalInput").ap()
    d_dtb = nc.dram_tensor("dtb", [64, 1], F32, kind="ExternalInput").ap()
    d_alog = nc.dram_tensor("alog", [64, 1], F32, kind="ExternalInput").ap()
    d_dcol = nc.dram_tensor("dcol", [128, 64], F32, kind="ExternalInput").ap()
    d_nw = nc.dram_tensor("nw", [128, 64], F32, kind="ExternalInput").ap()
    d_bffn = nc.dram_tensor("bffn", [128, 32], F32, kind="ExternalInput").ap()
    d_gateb = nc.dram_tensor("gateb", [4, 1], F32, kind="ExternalInput").ap()
    d_hmask = nc.dram_tensor("hmask", [128, 1], F32, kind="ExternalInput").ap()
    d_hinv = nc.dram_tensor("hinv", [128, 1], F32, kind="ExternalInput").ap()
    d_out = nc.dram_tensor("out", [HD, T], F32, kind="ExternalOutput").ap()

    s_xt = nc.dram_tensor("s_xt", [NCH, 128, NH, 128], F32).ap()
    s_b = nc.dram_tensor("s_b", [NCH, 128, 128], F32).ap()
    s_bt = nc.dram_tensor("s_bt", [NCH, 128, 128], F32).ap()
    s_c = nc.dram_tensor("s_c", [NCH, 128, 128], F32).ap()
    s_cum = nc.dram_tensor("s_cum", [NCH, 64, 128], F32).ap()
    s_ncum = nc.dram_tensor("s_ncum", [NCH, 64, 128], F32).ap()
    s_dtp = nc.dram_tensor("s_dtp", [NCH, 64, 128], F32).ap()
    s_ffn = nc.dram_tensor("s_ffn", [32, 128, T], F32).ap()
    s_ab = nc.dram_tensor("s_ab", [2, T], F32).ap()
    cc_in = nc.dram_tensor("cc_in", [128, NH * 128], F32)
    cc_out = nc.dram_tensor("cc_out", [128, NH * 128], F32)

    with contextlib.ExitStack() as st:
        def sbt(name, shape):
            return st.enter_context(nc.sbuf_tensor("t_" + name, shape, F32))

        ident = sbt("ident", [128, 128])
        maskbig = sbt("maskbig", [128, 128])
        jt64 = sbt("jt64", [128, 64])
        convw = sbt("convw", [128, 66 * KC])
        dtb = sbt("dtb", [64, 1])
        alog = sbt("alog", [64, 1])
        negea = sbt("negea", [64, 1])
        gateb = sbt("gateb", [4, 1])
        dcol = sbt("dcol", [128, 64])
        nw = sbt("nw", [128, 64])
        bffn = sbt("bffn", [128, 32])
        hmask = sbt("hmask", [128, 1])
        hinv = sbt("hinv", [128, 1])
        ones1 = sbt("ones1", [1, 128])
        onescol = sbt("onescol", [128, 1])
        zz64 = sbt("zz64", [64, 128])
        slabh = [sbt(f"slabh{i}", [128, 16 * 128]) for i in range(4)]
        HT = sbt("HT", [128, NH * 128])
        xtg = [sbt(f"xtg{i}", [128, HG * 128]) for i in range(2)]
        aec = sbt("aec", [64, 1])
        aerow = sbt("aerow", [1, 64])
        epsc = sbt("epsc", [128, 1])
        r8k = sbt("r8k", [128, 1])

        UC = 33200
        uni = sbt("uni", [128, UC])
        cur = [0]

        def ualloc(pdim, cols):
            off = cur[0]
            cur[0] += cols
            assert cur[0] <= UC, f"union overflow {cur[0]}"
            return uni[0:pdim, off:off + cols]

        xn1 = ualloc(128, 32 * W1)
        tmpx = ualloc(128, W1)
        tmpv2 = ualloc(128, W1)
        tmpg = ualloc(128, W1)
        arep = ualloc(128, W1)
        brep = ualloc(128, W1)
        arow = ualloc(1, W1)
        brow = ualloc(1, W1)
        srow = ualloc(1, W1)
        s2row = ualloc(1, W1)
        xbcf = ualloc(128, W1)
        cvs = ualloc(128, PH1)
        cvs2 = ualloc(128, PH1)
        bph = ualloc(128, PH1)
        cph = ualloc(128, PH1)
        xh = ualloc(128, PH1)
        btph = ualloc(128, 4 * 128)
        dtp1 = ualloc(64, PH1)
        loga1 = ualloc(64, PH1)
        cumc1 = [ualloc(64, 128) for _ in range(4)]
        ngc1 = ualloc(64, 128)
        wr1 = ualloc(64, 128)
        wdt1 = [ualloc(64, 128) for _ in range(4)]
        wdtT1 = [ualloc(128, 64) for _ in range(4)]
        ae1 = [ualloc(128, 64) for _ in range(4)]
        xtst = [ualloc(128, 128) for _ in range(4)]
        btw1 = [ualloc(128, 128) for _ in range(2)]
        scph = ualloc(4, PH1)
        scT = [ualloc(128, 4) for _ in range(2)]
        rcol = [ualloc(128, 1) for _ in range(2)]
        bestc = ualloc(128, 1)
        dstep = ualloc(128, 1)
        maskj = [ualloc(128, 64) for _ in range(2)]
        p64m = [ualloc(128, 64) for _ in range(2)]
        p64T = ualloc(64, PH1)
        ffst = [ualloc(128, PH1) for _ in range(2)]
        apk = ualloc(128, 32 * 64)
        bltt = [ualloc(64, 128) for _ in range(2)]
        gatew = ualloc(128, 32 * 4)
        assert cur[0] <= UC, cur[0]

        cur[0] = 0
        ztile = ualloc(128, 64 * PH2)
        xn2 = ualloc(128, 32 * PH2)
        xst2 = ualloc(128, PH2)
        tmp2 = ualloc(128, PH2)
        LT = ualloc(2, LRG * 128)
        RT = ualloc(2, LRG * 128)
        onesrow = ualloc(1, 128)
        btc2 = ualloc(128, 128)
        bc2 = ualloc(128, 128)
        cc2 = ualloc(128, 128)
        packin = ualloc(128, 128)
        packT = ualloc(128, 128)
        dtpc2 = ualloc(64, 128)
        wr2 = ualloc(64, 128)
        wdt2 = ualloc(64, 128)
        wdtT2 = ualloc(128, 64)
        ae2 = ualloc(128, 64)
        ecT = ualloc(128, 64)
        g2sb = ualloc(128, 128)
        esb = [ualloc(128, 128) for _ in range(2)]
        esbE = [ualloc(128, 128) for _ in range(2)]
        wsb = [ualloc(128, 128) for _ in range(2)]
        y2sb = [ualloc(128, 128) for _ in range(2)]
        ytf = [ualloc(128, 128) for _ in range(2)]
        sszt = [ualloc(128, 128) for _ in range(2)]
        sqt2 = sszt
        btw2 = [ualloc(128, 128) for _ in range(2)]
        arowt = ualloc(1, PH2)
        browt = ualloc(1, PH2)
        rssum = ualloc(1, 128)
        rrow = ualloc(1, 128)
        rrep = ualloc(128, 128)
        outst = [ualloc(128, PH2) for _ in range(2)]
        ffin = [ualloc(128, PH2) for _ in range(2)]
        arep2 = ualloc(128, PH2)
        brep2 = ualloc(128, PH2)
        assert cur[0] <= UC, cur[0]

        pb = [st.enter_context(nc.psum_tensor(f"pb{i}", [128, 512], F32)) for i in range(8)]

        SD = Sem(st, nc, "SD")
        GD = Sem(st, nc, "GD")
        VD = Sem(st, nc, "VD")
        TM = Sem(st, nc, "TM")
        VE = Sem(st, nc, "VE")
        AC = Sem(st, nc, "AC")
        GC = Sem(st, nc, "GC")
        CCS = Sem(st, nc, "CCS")

        block = st.enter_context(nc.Block())
        TE, VC, SC, GP, SY = block.tensor, block.vector, block.scalar, block.gpsimd, block.sync

        pbank_free = [None] * 8
        slab_free = [None, None, None, None]

        class SlabPair:
            def __init__(self, i0, i1):
                self.i0, self.i1 = i0, i1

            def __getitem__(self, sl):
                raise RuntimeError("use slab_tile()")

        def slab_tile(pi, k):
            half = k // 16
            return slabh[pi * 2 + half][:, (k % 16) * 128:((k % 16) + 1) * 128]

        def bank_wait(e, i):
            if pbank_free[i] is not None:
                sem, val = pbank_free[i]
                e.wait_ge(sem.h, val)

        def load_slab(pi, src_ap):
            for half in range(2):
                hi = pi * 2 + half
                gate = slab_free[hi]
                def _(sy, hi=hi, src_ap=src_ap, half=half, gate=gate):
                    if gate is not None:
                        sy.wait_ge(gate[0].h, gate[1])
                    sy.dma_start(out=slabh[hi][:, :],
                                 in_=src_ap[:, half * 2048:(half + 1) * 2048]).then_inc(SD.h, 16)
                SY(_)
                SD.c += 16
            return SD.c

        def slab_used(pi, tok):
            slab_free[pi * 2] = (TM, tok)
            slab_free[pi * 2 + 1] = (TM, tok)

        def _(sy):
            for dst, src in [(convw, d_convw), (dtb, d_dtb), (alog, d_alog), (gateb, d_gateb),
                             (dcol, d_dcol), (nw, d_nw), (bffn, d_bffn), (hmask, d_hmask),
                             (hinv, d_hinv)]:
                sy.dma_start(out=dst[:, :], in_=src[:, :]).then_inc(SD.h, 16)
            sy.dma_start(out=apk, in_=d_apack[:, :]).then_inc(SD.h, 16)
            sy.dma_start(out=gatew, in_=d_gate[:, :]).then_inc(SD.h, 16)
        SY(_)
        SD.c += 11 * 16
        sd_consts = SD.c

        def _(g):
            g.iota(maskbig[:, :], [[1, 128]], base=0, channel_multiplier=-1,
                   allow_small_or_imprecise_dtypes=True)
            g.tensor_scalar(ident[:, :], maskbig[:, :], 0.0, None, ALU.is_equal)
            g.tensor_scalar(maskbig[:, :], maskbig[:, :], 0.0, None, ALU.is_ge)
            g.tensor_scalar(maskbig[:, :], maskbig[:, :], -1.0, 1e30, ALU.add, ALU.mult)
            g.iota(jt64[:, :], [[1, 64]], base=0, channel_multiplier=0,
                   allow_small_or_imprecise_dtypes=True)
            g.memset(ones1[:, :], 1.0)
            g.memset(onescol[:, :], 1.0)
            g.memset(zz64[:, :], 0.0)
            g.memset(epsc[:, :], 1e-5)
            g.memset(r8k[:, :], 1.0 / 8192.0)
            g.memset(HT[:, :], 0.0).then_inc(GC.h, 1)
        GP(_)
        GC.c += 1
        gc_pre = GC.c

        def _(sc):
            sc.wait_ge(SD.h, sd_consts)
            sc.activation(out=negea[:, :], in_=alog[:, :], func=AF.Exp).then_inc(AC.h, 1)
        SC(_)
        AC.c += 1
        def _(v, w=AC.c):
            v.wait_ge(AC.h, w)
            v.tensor_scalar(negea[:, :], negea[:, :], -1.0, None, ALU.mult).then_inc(VE.h, 1)
        VC(_)
        VE.c += 1
        ve_pre = VE.c

        xt_tok = [[0] * NH for _ in range(NCH)]
        scr_tok = [0] * NCH
        cum_war = [None] * 4
        wdt_war = [0] * 4
        dtp_war = dict(ve=0, gd=0)
        bph_war = dict(gd=0, tm=0)
        cph_war = dict(gd=0)
        xh_war = dict(tm=0)
        xtst_war = [0] * 4
        btph_war = dict(gd=0, ve=0)
        ffst_war = [0, 0]
        aec_war = dict(tm=0)
        ngc_war = dict(gd=0)
        stats_war = dict(ac=0, ve=0, gc=0)
        bltt_war = [0, 0]
        conv_ve_tok = 0

        for p1 in range(NPH1):
            def _(sy, p1=p1, t=TM.c, v=VE.c, g=GC.c):
                if p1 > 0:
                    sy.wait_ge(TM.h, t)
                    sy.wait_ge(VE.h, v)
                    sy.wait_ge(GC.h, g)
                sy.dma_start(out=xn1, in_=d_x1[p1]).then_inc(SD.h, 16)
            SY(_)
            SD.c += 16
            x_tok = SD.c

            sq_tm_tok = [0] * 32
            for c in range(32):
                def _(sc, c=c, x_tok=x_tok, w=(sq_tm_tok[c - 2] if c >= 2 else 0),
                      w2=stats_war["ve"], w3=stats_war["gc"]):
                    if c == 0:
                        sc.wait_ge(SD.h, x_tok)
                        if w2:
                            sc.wait_ge(VE.h, w2)
                        if w3:
                            sc.wait_ge(GC.h, w3)
                    if w:
                        sc.wait_ge(TM.h, w)
                    sc.activation(out=(tmpx if c % 2 == 0 else tmpg), in_=xn1[:, c * W1:(c + 1) * W1],
                                  func=AF.Square).then_inc(AC.h, 1)
                SC(_)
                AC.c += 1
                def _(te, c=c, ac_sq=AC.c, x_tok=x_tok, ve_gate=VE.c):
                    te.wait_ge(AC.h, ac_sq)
                    if c == 0:
                        te.wait_ge(SD.h, x_tok)
                        te.wait_ge(VE.h, ve_gate)
                        for i in range(4):
                            bank_wait(te, i)
                    sq = tmpx if c % 2 == 0 else tmpg
                    te.matmul(pb[0][0:1, 0:512], onescol[:, :], xn1[:, c * W1 + 3:(c + 1) * W1],
                              start=(c == 0), stop=(c == 31))
                    te.matmul(pb[1][0:1, 0:512], onescol[:, :], sq[:, 3:W1],
                              start=(c == 0), stop=(c == 31))
                    te.matmul(pb[2][0:1, 0:3], onescol[:, :], xn1[:, c * W1:c * W1 + 3],
                              start=(c == 0), stop=(c == 31))
                    te.matmul(pb[3][0:1, 0:3], onescol[:, :], sq[:, 0:3],
                              start=(c == 0), stop=(c == 31)).then_inc(TM.h, 1)
                TE(_)
                TM.c += 1
                sq_tm_tok[c] = TM.c
            tm_stats = TM.c

            def _(v, tm_stats=tm_stats, w=stats_war["ac"]):
                v.wait_ge(TM.h, tm_stats)
                if w:
                    v.wait_ge(AC.h, w)
                v.tensor_copy(srow[0:1, 3:W1], pb[0][0:1, 0:512])
                v.tensor_copy(srow[0:1, 0:3], pb[2][0:1, 0:3])
                v.tensor_copy(s2row[0:1, 3:W1], pb[1][0:1, 0:512])
                v.tensor_copy(s2row[0:1, 0:3], pb[3][0:1, 0:3])
                v.drain()
                v.tensor_scalar(srow, srow, 1.0 / HD, None, ALU.mult)
                v.tensor_scalar(s2row, s2row, 1.0 / HD, None, ALU.mult)
                v.drain()
                v.tensor_tensor(brow, srow, srow, ALU.mult)
                v.drain()
                v.tensor_tensor(s2row, s2row, brow, ALU.subtract).then_inc(VE.h, 1)
            VC(_)
            VE.c += 1
            for i in range(4):
                pbank_free[i] = (VE, VE.c)
            def _(sc, w=VE.c, wg=gc_pre):
                sc.wait_ge(VE.h, w)
                sc.wait_ge(GC.h, wg)
                sc.activation(out=arow, in_=s2row, func=AF.Sqrt, bias=epsc[0:1, :], scale=1.0).then_inc(AC.h, 1)
            SC(_)
            AC.c += 1
            stats_war["ac"] = AC.c
            def _(v, w=AC.c, p1=p1):
                v.wait_ge(AC.h, w)
                v.reciprocal(arow, arow)
                v.drain()
                v.scalar_tensor_tensor(brow, srow, -1.0, arow, ALU.mult, ALU.mult).then_inc(VE.h, 1)
            VC(_)
            VE.c += 1
            def _(g, p1=p1, w=VE.c):
                g.wait_ge(VE.h, w)
                g.dma_start(out=s_ab[0:1, p1 * PH1:(p1 + 1) * PH1], in_=arow[0:1, 3:W1]).then_inc(GD.h, 16)
                g.dma_start(out=s_ab[1:2, p1 * PH1:(p1 + 1) * PH1], in_=brow[0:1, 3:W1]).then_inc(GD.h, 16)
            GP(_)
            GD.c += 32

            def _(te, w=VE.c):
                te.wait_ge(VE.h, w)
                for i in range(4, 8):
                    bank_wait(te, i)
                te.matmul(pb[4][:, 0:512], ones1[0:1, :], arow[0:1, 0:512], start=True, stop=True)
                te.matmul(pb[5][:, 0:3], ones1[0:1, :], arow[0:1, 512:515], start=True, stop=True)
                te.matmul(pb[6][:, 0:512], ones1[0:1, :], brow[0:1, 0:512], start=True, stop=True)
                te.matmul(pb[7][:, 0:3], ones1[0:1, :], brow[0:1, 512:515], start=True, stop=True).then_inc(TM.h, 1)
            TE(_)
            TM.c += 1
            def _(sc, w=TM.c):
                sc.wait_ge(TM.h, w)
                sc.copy(arep[:, 0:512], pb[4][:, 0:512])
                sc.copy(arep[:, 512:515], pb[5][:, 0:3])
                sc.copy(brep[:, 0:512], pb[6][:, 0:512])
                sc.copy(brep[:, 512:515], pb[7][:, 0:3]).then_inc(AC.h, 1)
            SC(_)
            AC.c += 1
            for i in range(4, 8):
                pbank_free[i] = (AC, AC.c)
            ac_rep = AC.c

            def _(v, w=ac_rep):
                v.wait_ge(AC.h, w)
                for c in range(0, 32, 4):
                    v.tensor_tensor(tmpx, xn1[:, c * W1:(c + 1) * W1], arep, ALU.mult)
                    v.tensor_tensor(tmpv2, xn1[:, (c + 2) * W1:(c + 3) * W1], arep, ALU.mult)
                    v.tensor_tensor(xn1[:, c * W1:(c + 1) * W1], tmpx, brep, ALU.add)
                    v.tensor_tensor(xn1[:, (c + 2) * W1:(c + 3) * W1], tmpv2, brep, ALU.add)
                v.drain()
                v.tensor_copy(srow[0:1, 0:1], srow[0:1, 0:1]).then_inc(VE.h, 1)
            VC(_)
            VE.c += 1
            ve_xn = VE.c
            stats_war["ve"] = VE.c
            def _(g, w=ac_rep):
                g.wait_ge(AC.h, w)
                for c in range(1, 32, 2):
                    g.tensor_tensor(tmpg, xn1[:, c * W1:(c + 1) * W1], arep, ALU.mult)
                    g.tensor_tensor(xn1[:, c * W1:(c + 1) * W1], tmpg, brep, ALU.add)
                g.memset(aerow[0:1, 0:1], 1.0).then_inc(GC.h, 1)
            GP(_)
            GC.c += 1
            gc_xn = GC.c
            stats_war["gc"] = GC.c

            for f in range(NF1):
                si = f % 2
                ld = load_slab(si, d_w1a[f])
                bA, bB = (0, 1) if f % 2 == 0 else (2, 3)
                def _(te, f=f, si=si, ld=ld, bA=bA, bB=bB, ve_xn=ve_xn, gc_xn=gc_xn):
                    te.wait_ge(SD.h, ld)
                    if f == 0:
                        te.wait_ge(VE.h, ve_xn)
                        te.wait_ge(GC.h, gc_xn)
                    bank_wait(te, bA)
                    bank_wait(te, bB)
                    for k in range(32):
                        te.matmul(pb[bA][:, 0:512], slab_tile(si, k),
                                  xn1[:, k * W1 + 3:(k + 1) * W1], start=(k == 0), stop=(k == 31))
                    ins = None
                    for k in range(32):
                        ins = te.matmul(pb[bB][:, 0:3], slab_tile(si, k),
                                        xn1[:, k * W1:k * W1 + 3], start=(k == 0), stop=(k == 31))
                    ins.then_inc(TM.h, 1)
                TE(_)
                TM.c += 1
                slab_used(si, TM.c)
                tm_f = TM.c

                if f == 0:
                    def _(sc, tm_f=tm_f, bA=bA, w1=dtp_war["ve"], w2=dtp_war["gd"]):
                        sc.wait_ge(TM.h, tm_f)
                        if w1:
                            sc.wait_ge(VE.h, w1)
                        if w2:
                            sc.wait_ge(GD.h, w2)
                        sc.activation(out=dtp1, in_=pb[bA][0:64, 0:512], func=AF.Abs,
                                      bias=dtb[:, :])
                        sc.activation(out=loga1, in_=dtp1, func=AF.Exp, scale=-1.0)
                        sc.activation(out=tmpx[0:64, 0:512], in_=loga1, func=AF.Ln, bias=1.0)
                        sc.activation(out=tmpg[0:64, 0:512], in_=pb[bA][0:64, 0:512],
                                      func=AF.Relu, bias=dtb[:, :]).then_inc(AC.h, 1)
                    SC(_)
                    AC.c += 1
                    pbank_free[bA] = (AC, AC.c)
                    pbank_free[bB] = (AC, AC.c)
                    def _(v, w=AC.c):
                        v.wait_ge(AC.h, w)
                        v.wait_ge(VE.h, ve_pre)
                        v.tensor_tensor(dtp1, tmpx[0:64, 0:512], tmpg[0:64, 0:512], ALU.add)
                        v.drain()
                        v.tensor_scalar(loga1, dtp1, negea[:, :], None, ALU.mult).then_inc(VE.h, 1)
                    VC(_)
                    VE.c += 1
                    stats_war["ve"] = VE.c
                    for c4 in range(4):
                        cg = p1 * 4 + c4
                        cw = cum_war[c4]
                        def _(v, c4=c4, cw=cw, w_ngc=ngc_war["gd"]):
                            if cw is not None:
                                v.wait_ge(AC.h, cw["ac"])
                                v.wait_ge(GD.h, cw["gd"])
                            if w_ngc:
                                v.wait_ge(GD.h, w_ngc)
                            if c4 == 0:
                                v.drain()
                            v.tensor_tensor_scan(cumc1[c4], loga1[:, c4 * 128:(c4 + 1) * 128],
                                                 zz64[:, :], 0.0, ALU.add, ALU.add)
                            v.drain()
                            v.tensor_scalar(ngc1, cumc1[c4], -1.0, None, ALU.mult).then_inc(VE.h, 1)
                        VC(_)
                        VE.c += 1
                        ve_cum = VE.c
                        def _(g, c4=c4, cg=cg, ve_cum=ve_cum):
                            g.wait_ge(VE.h, ve_cum)
                            g.dma_start(out=s_cum[cg], in_=cumc1[c4]).then_inc(GD.h, 16)
                            g.dma_start(out=s_ncum[cg], in_=ngc1).then_inc(GD.h, 16)
                            g.dma_start(out=s_dtp[cg], in_=dtp1[:, c4 * 128:(c4 + 1) * 128]).then_inc(GD.h, 16)
                        GP(_)
                        GD.c += 48
                        gd_cum = GD.c
                        ngc_war["gd"] = GD.c
                        def _(sc, c4=c4, ve_cum=ve_cum, w=aec_war["tm"]):
                            sc.wait_ge(VE.h, ve_cum)
                            if w:
                                sc.wait_ge(TM.h, w)
                            sc.activation(out=wr1, in_=cumc1[c4], func=AF.Exp,
                                          bias=cumc1[c4][:, 127:128], scale=-1.0)
                            sc.activation(out=aec[:, :], in_=cumc1[c4][:, 127:128],
                                          func=AF.Exp).then_inc(AC.h, 1)
                        SC(_)
                        AC.c += 1
                        ac_wr = AC.c
                        def _(v, c4=c4, ac_wr=ac_wr, w=wdt_war[c4]):
                            v.wait_ge(AC.h, ac_wr)
                            if w:
                                v.wait_ge(TM.h, w)
                            v.tensor_tensor(wdt1[c4], wr1, dtp1[:, c4 * 128:(c4 + 1) * 128],
                                            ALU.mult).then_inc(VE.h, 1)
                        VC(_)
                        VE.c += 1
                        ve_wdt = VE.c
                        def _(te, c4=c4, ve_wdt=ve_wdt):
                            te.wait_ge(VE.h, ve_wdt)
                            bank_wait(te, 6)
                            te.transpose(pb[6][:, 0:64], wdt1[c4], ident[0:64, 0:64])
                            te.transpose(pb[6][0:1, 64:128], aec[:, :], ident[0:64, 0:64]).then_inc(TM.h, 1)
                        TE(_)
                        TM.c += 1
                        wdt_war[c4] = TM.c
                        aec_war["tm"] = TM.c
                        def _(sc, c4=c4, w=TM.c):
                            sc.wait_ge(TM.h, w)
                            sc.copy(wdtT1[c4], pb[6][:, 0:64])
                            sc.copy(aerow[0:1, :], pb[6][0:1, 64:128]).then_inc(AC.h, 1)
                        SC(_)
                        AC.c += 1
                        pbank_free[6] = (AC, AC.c)
                        def _(te, w=AC.c):
                            te.wait_ge(AC.h, w)
                            bank_wait(te, 7)
                            te.matmul(pb[7][:, 0:64], ones1[0:1, :], aerow[0:1, :],
                                      start=True, stop=True).then_inc(TM.h, 1)
                        TE(_)
                        TM.c += 1
                        def _(sc, c4=c4, w=TM.c):
                            sc.wait_ge(TM.h, w)
                            sc.copy(ae1[c4], pb[7][:, 0:64]).then_inc(AC.h, 1)
                        SC(_)
                        AC.c += 1
                        pbank_free[7] = (AC, AC.c)
                        cum_war[c4] = dict(ac=AC.c, gd=gd_cum)
                    dtp_war["ve"] = VE.c
                    dtp_war["gd"] = GD.c
                    continue

                def _(sc, tm_f=tm_f, bA=bA, bB=bB, w=conv_ve_tok):
                    sc.wait_ge(TM.h, tm_f)
                    if w:
                        sc.wait_ge(VE.h, w)
                    sc.copy(xbcf[:, 3:W1], pb[bA][:, 0:512])
                    sc.copy(xbcf[:, 0:3], pb[bB][:, 0:3]).then_inc(AC.h, 1)
                SC(_)
                AC.c += 1
                pbank_free[bA] = (AC, AC.c)
                pbank_free[bB] = (AC, AC.c)
                ac_evac = AC.c

                fc = f - 1
                def _(v, fc=fc, ac_evac=ac_evac):
                    v.wait_ge(AC.h, ac_evac)
                    v.tensor_scalar(cvs, xbcf[:, 0:512], convw[:, fc * 4:fc * 4 + 1], None, ALU.mult)
                    v.tensor_scalar(cvs2, xbcf[:, 1:513], convw[:, fc * 4 + 1:fc * 4 + 2], None, ALU.mult)
                    v.scalar_tensor_tensor(cvs, xbcf[:, 2:514], convw[:, fc * 4 + 2:fc * 4 + 3],
                                           cvs, ALU.mult, ALU.add)
                    v.scalar_tensor_tensor(cvs2, xbcf[:, 3:515], convw[:, fc * 4 + 3:fc * 4 + 4],
                                           cvs2, ALU.mult, ALU.add)
                    v.drain()
                    v.tensor_tensor(cvs, cvs, cvs2, ALU.add).then_inc(VE.h, 1)
                VC(_)
                VE.c += 1
                conv_ve_tok = VE.c
                ve_conv = VE.c

                dst = bph if f == 1 else (cph if f == 2 else xh)
                war = (bph_war if f == 1 else (cph_war if f == 2 else xh_war))
                def _(sc, dst=dst, ve_conv=ve_conv, w1=war.get("gd", 0), w2=war.get("tm", 0)):
                    sc.wait_ge(VE.h, ve_conv)
                    if w1:
                        sc.wait_ge(GD.h, w1)
                    if w2:
                        sc.wait_ge(TM.h, w2)
                    sc.activation(out=dst, in_=cvs, func=AF.Silu).then_inc(AC.h, 1)
                SC(_)
                AC.c += 1
                ac_silu = AC.c

                if f == 1:
                    for c4 in range(4):
                        cg = p1 * 4 + c4
                        def _(g, c4=c4, cg=cg, ac_silu=ac_silu):
                            g.wait_ge(AC.h, ac_silu)
                            g.dma_start(out=s_b[cg], in_=bph[:, c4 * 128:(c4 + 1) * 128]).then_inc(GD.h, 16)
                        GP(_)
                        GD.c += 16
                        def _(te, c4=c4, ac_silu=ac_silu):
                            te.wait_ge(AC.h, ac_silu)
                            bank_wait(te, 6)
                            te.transpose(pb[6][:, 0:128], bph[:, c4 * 128:(c4 + 1) * 128],
                                         ident[:, :]).then_inc(TM.h, 1)
                        TE(_)
                        TM.c += 1
                        bph_war["tm"] = TM.c
                        def _(sc, c4=c4, w=TM.c, w2=btph_war["gd"], w3=btph_war["ve"]):
                            sc.wait_ge(TM.h, w)
                            if c4 == 0:
                                if w2:
                                    sc.wait_ge(GD.h, w2)
                                if w3:
                                    sc.wait_ge(VE.h, w3)
                            sc.copy(btph[:, c4 * 128:(c4 + 1) * 128], pb[6][:, 0:128]).then_inc(AC.h, 1)
                        SC(_)
                        AC.c += 1
                        pbank_free[6] = (AC, AC.c)
                        def _(g, c4=c4, cg=cg, w=AC.c):
                            g.wait_ge(AC.h, w)
                            g.dma_start(out=s_bt[cg], in_=btph[:, c4 * 128:(c4 + 1) * 128]).then_inc(GD.h, 16)
                        GP(_)
                        GD.c += 16
                        scr_tok[cg] = GD.c
                    bph_war["gd"] = GD.c
                    btph_war["gd"] = GD.c
                elif f == 2:
                    for c4 in range(4):
                        cg = p1 * 4 + c4
                        def _(g, c4=c4, cg=cg, ac_silu=ac_silu):
                            g.wait_ge(AC.h, ac_silu)
                            g.dma_start(out=s_c[cg], in_=cph[:, c4 * 128:(c4 + 1) * 128]).then_inc(GD.h, 16)
                        GP(_)
                        GD.c += 16
                        scr_tok[cg] = max(scr_tok[cg], GD.c)
                    cph_war["gd"] = GD.c
                else:
                    h = f - 3
                    for c4 in range(4):
                        cg = p1 * 4 + c4
                        bsel = 6 if c4 % 2 == 0 else 7
                        def _(te, c4=c4, ac_silu=ac_silu, bsel=bsel):
                            te.wait_ge(AC.h, ac_silu)
                            bank_wait(te, bsel)
                            te.transpose(pb[bsel][:, 0:128], xh[:, c4 * 128:(c4 + 1) * 128],
                                         ident[:, :]).then_inc(TM.h, 1)
                        TE(_)
                        TM.c += 1
                        xh_war["tm"] = TM.c
                        def _(sc, c4=c4, w=TM.c, bsel=bsel, w2=xtst_war[c4]):
                            sc.wait_ge(TM.h, w)
                            if w2:
                                sc.wait_ge(GD.h, w2)
                            sc.copy(xtst[c4], pb[bsel][:, 0:128]).then_inc(AC.h, 1)
                        SC(_)
                        AC.c += 1
                        pbank_free[bsel] = (AC, AC.c)
                        def _(g, c4=c4, cg=cg, h=h, w=AC.c):
                            g.wait_ge(AC.h, w)
                            g.dma_start(out=s_xt[cg, :, h, :], in_=xtst[c4]).then_inc(GD.h, 16)
                        GP(_)
                        GD.c += 16
                        xt_tok[cg][h] = GD.c
                        xtst_war[c4] = GD.c

            for c4 in range(4):
                cg = p1 * 4 + c4
                for hgi in range(NH // HG):
                    gsel = hgi % 2
                    def _(g, cg=cg, hgi=hgi, gsel=gsel, w=xt_tok[cg][NH - 1], t=TM.c):
                        g.wait_ge(GD.h, w)
                        if t:
                            g.wait_ge(TM.h, t)
                        g.dma_start(out=xtg[gsel][:, :],
                                    in_=s_xt[cg, :, hgi * HG:(hgi + 1) * HG, :]
                                    .rearrange("p h q -> p (h q)")).then_inc(GD.h, 16)
                    GP(_)
                    GD.c += 16
                    gd_xtg = GD.c
                    for hh in range(HG):
                        h = hgi * HG + hh
                        pi = h % 2
                        def _(v, c4=c4, h=h, pi=pi, w=TM.c):
                            if w:
                                v.wait_ge(TM.h, w)
                            v.tensor_scalar(btw1[pi], btph[:, c4 * 128:(c4 + 1) * 128],
                                            wdtT1[c4][:, h:h + 1], None, ALU.mult).then_inc(VE.h, 1)
                        VC(_)
                        VE.c += 1
                        btph_war["ve"] = VE.c
                        ve_btw = VE.c
                        bsel = 6 if h % 2 == 0 else 7
                        def _(te, pi=pi, hh=hh, gsel=gsel, bsel=bsel, ve_btw=ve_btw, gd_xtg=gd_xtg):
                            te.wait_ge(VE.h, ve_btw)
                            te.wait_ge(GD.h, gd_xtg)
                            bank_wait(te, bsel)
                            te.matmul(pb[bsel][:, 0:128], btw1[pi],
                                      xtg[gsel][:, hh * 128:(hh + 1) * 128],
                                      start=True, stop=True).then_inc(TM.h, 1)
                        TE(_)
                        TM.c += 1
                        def _(v, c4=c4, h=h, bsel=bsel, w=TM.c):
                            v.wait_ge(TM.h, w)
                            v.scalar_tensor_tensor(HT[:, h * 128:(h + 1) * 128],
                                                   HT[:, h * 128:(h + 1) * 128],
                                                   ae1[c4][:, h:h + 1], pb[bsel][:, 0:128],
                                                   ALU.mult, ALU.add).then_inc(VE.h, 1)
                        VC(_)
                        VE.c += 1
                        pbank_free[bsel] = (VE, VE.c)

            def _(te):
                bank_wait(te, 4)
                ins = None
                for k in range(32):
                    ins = te.matmul(pb[4][0:4, 0:512], gatew[:, k * 4:(k + 1) * 4],
                                    xn1[:, k * W1 + 3:(k + 1) * W1], start=(k == 0), stop=(k == 31))
                ins.then_inc(TM.h, 1)
            TE(_)
            TM.c += 1
            def _(v, w=TM.c):
                v.wait_ge(TM.h, w)
                v.tensor_scalar(scph, pb[4][0:4, 0:512], gateb[:, :], None, ALU.add).then_inc(VE.h, 1)
            VC(_)
            VE.c += 1
            pbank_free[4] = (VE, VE.c)
            ve_scg = VE.c

            for c4 in range(4):
                pi = c4 % 2
                def _(te, c4=c4, ve_scg=ve_scg):
                    te.wait_ge(VE.h, ve_scg)
                    bank_wait(te, 5)
                    te.transpose(pb[5][:, 0:4], scph[0:4, c4 * 128:(c4 + 1) * 128],
                                 ident[0:4, 0:4]).then_inc(TM.h, 1)
                TE(_)
                TM.c += 1
                def _(v, pi=pi, w=TM.c):
                    v.wait_ge(TM.h, w)
                    v.tensor_copy(scT[pi], pb[5][:, 0:4])
                    v.tensor_copy(bestc, pb[5][:, 0:1])
                    v.memset(rcol[pi], float(RANKS[0]))
                    for e in range(1, 4):
                        v.drain()
                        v.tensor_tensor(dstep, scT[pi][:, e:e + 1], bestc, ALU.is_gt)
                        v.tensor_scalar(scT[pi][:, 0:1], rcol[pi], -1.0, float(RANKS[e]),
                                        ALU.mult, ALU.add)
                        v.drain()
                        v.tensor_tensor(dstep, dstep, scT[pi][:, 0:1], ALU.mult)
                        v.drain()
                        v.tensor_tensor(rcol[pi], rcol[pi], dstep, ALU.add)
                        v.tensor_tensor(bestc, bestc, scT[pi][:, e:e + 1], ALU.max)
                    v.drain()
                    v.tensor_scalar(maskj[pi], jt64[:, :], rcol[pi], None, ALU.is_lt).then_inc(VE.h, 1)
                VC(_)
                VE.c += 1
                pbank_free[5] = (VE, VE.c)
                ve_mask = VE.c
                def _(te, c4=c4, ve_mask=ve_mask):
                    te.wait_ge(VE.h, ve_mask)
                    bank_wait(te, 4)
                    ins = None
                    for k in range(32):
                        ins = te.matmul(pb[4][:, 0:64],
                                        xn1[:, k * W1 + 3 + c4 * 128:k * W1 + 3 + (c4 + 1) * 128],
                                        apk[:, k * 64:(k + 1) * 64], start=(k == 0), stop=(k == 31))
                    ins.then_inc(TM.h, 1)
                TE(_)
                TM.c += 1
                def _(v, pi=pi, w=TM.c):
                    v.wait_ge(TM.h, w)
                    v.drain()
                    v.tensor_tensor(p64m[pi], maskj[pi], pb[4][:, 0:64], ALU.mult).then_inc(VE.h, 1)
                VC(_)
                VE.c += 1
                pbank_free[4] = (VE, VE.c)
                def _(te, pi=pi, w=VE.c):
                    te.wait_ge(VE.h, w)
                    bank_wait(te, 5)
                    te.transpose(pb[5][0:64, 0:128], p64m[pi], ident[:, :]).then_inc(TM.h, 1)
                TE(_)
                TM.c += 1
                def _(sc, c4=c4, w=TM.c):
                    sc.wait_ge(TM.h, w)
                    sc.copy(p64T[:, c4 * 128:(c4 + 1) * 128], pb[5][0:64, 0:128]).then_inc(AC.h, 1)
                SC(_)
                AC.c += 1
                pbank_free[5] = (AC, AC.c)
            ac_p64 = AC.c

            for o in range(32):
                si = o % 2
                ld = load_slab(si, d_wffn[o])
                def _(sy, o=o, w=bltt_war[o % 2]):
                    if w:
                        sy.wait_ge(TM.h, w)
                    sy.dma_start(out=bltt[o % 2][:, :], in_=d_blt[:, o * 128:(o + 1) * 128]).then_inc(SD.h, 16)
                SY(_)
                SD.c += 16
                ld2 = SD.c
                bsel = o % 4
                def _(te, o=o, si=si, ld=ld, ld2=ld2, bsel=bsel, ac_p64=ac_p64):
                    te.wait_ge(SD.h, max(ld, ld2))
                    if o == 0:
                        te.wait_ge(AC.h, ac_p64)
                    bank_wait(te, bsel)
                    for k in range(32):
                        te.matmul(pb[bsel][:, 0:512], slab_tile(si, k),
                                  xn1[:, k * W1 + 3:(k + 1) * W1], start=(k == 0), stop=False)
                    te.matmul(pb[bsel][:, 0:512], bltt[o % 2], p64T,
                              start=False, stop=True).then_inc(TM.h, 1)
                TE(_)
                TM.c += 1
                slab_used(si, TM.c)
                bltt_war[o % 2] = TM.c
                sti = o % 2
                def _(v, o=o, sti=sti, w=TM.c, bsel=bsel, w2=ffst_war[sti]):
                    v.wait_ge(TM.h, w)
                    if w2:
                        v.wait_ge(GD.h, w2)
                    v.tensor_scalar(ffst[sti], pb[bsel][:, 0:512], bffn[:, o:o + 1], None,
                                    ALU.add).then_inc(VE.h, 1)
                VC(_)
                VE.c += 1
                pbank_free[bsel] = (VE, VE.c)
                def _(g, o=o, sti=sti, p1=p1, w=VE.c):
                    g.wait_ge(VE.h, w)
                    g.dma_start(out=s_ffn[o][:, p1 * PH1:(p1 + 1) * PH1], in_=ffst[sti]).then_inc(GD.h, 16)
                GP(_)
                GD.c += 16
                ffst_war[sti] = GD.c

        def _(v):
            v.tensor_scalar(ztile[:, 0:8192], HT[:, :], hinv[:, :], None, ALU.mult).then_inc(VE.h, 1)
        VC(_)
        VE.c += 1
        def _(g, w=VE.c, base=GD.c):
            g.wait_ge(VE.h, w)
            g.dma_start(out=cc_in.ap(), in_=ztile[:, 0:8192]).then_inc(GD.h, 16)
            g.wait_ge(GD.h, base + 16)
            g.collective_compute(
                "AllReduce", ALU.add,
                replica_groups=[[0, 1], [2, 3], [4, 5], [6, 7]],
                ins=[cc_in.ap().opt()],
                outs=[cc_out.ap().opt()],
            ).then_inc(CCS.h, 1)
            g.wait_ge(CCS.h, 1)
            g.dma_start(out=HT[:, :], in_=cc_out.ap()).then_inc(GD.h, 16)
        GP(_)
        GD.c += 32
        CCS.c += 1
        def _(v, w=GD.c):
            v.wait_ge(GD.h, w)
            v.tensor_scalar(HT[:, :], HT[:, :], hmask[:, :], None, ALU.mult).then_inc(VE.h, 1)
        VC(_)
        VE.c += 1

        def _(v):
            v.tensor_copy(aerow[0:1, 0:1], aerow[0:1, 0:1]).then_inc(VE.h, 1)
        VC(_)
        VE.c += 1
        def _(sc):
            sc.copy(aerow[0:1, 1:2], aerow[0:1, 1:2]).then_inc(AC.h, 1)
        SC(_)
        AC.c += 1
        def _(g):
            g.memset(aerow[0:1, 2:3], 1.0).then_inc(GC.h, 1)
        GP(_)
        GC.c += 1
        def _(te, wv=VE.c, wa=AC.c, wg=GC.c, wsd=SD.c, wgd=GD.c):
            te.wait_ge(VE.h, wv)
            te.wait_ge(AC.h, wa)
            te.wait_ge(GC.h, wg)
            te.wait_ge(SD.h, wsd)
            te.wait_ge(GD.h, wgd)
            te.nop().then_inc(TM.h, 1)
        TE(_)
        TM.c += 1
        barrier_tok = TM.c
        for EN in (VC, SC, GP, SY):
            def _(e, barrier_tok=barrier_tok):
                e.wait_ge(TM.h, barrier_tok)
            EN(_)

        def _(g):
            g.memset(onesrow[:, :], 1.0).then_inc(GC.h, 1)
        GP(_)
        GC.c += 1
        def _(g, w=GC.c):
            g.wait_ge(GC.h, w)
            for j in range(LRG):
                g.dma_start(out=LT[0:1, j * 128:(j + 1) * 128], in_=onesrow[0:1, :]).then_inc(GD.h, 16)
                g.dma_start(out=RT[1:2, j * 128:(j + 1) * 128], in_=onesrow[0:1, :]).then_inc(GD.h, 16)
        GP(_)
        GD.c += 32 * LRG
        gd_ones = GD.c

        outst_war = [0, 0]
        ve_ht_tok = [0] * NH
        xtg_rd_tok = [0, 0]
        lr_rd_tok = 0
        for p2 in range(NPH2):
            t0 = p2 * PH2
            def _(g, t0=t0, wa=AC.c, wt=TM.c):
                g.wait_ge(AC.h, wa)
                g.wait_ge(TM.h, wt)
                g.dma_start(out=arowt, in_=s_ab[0:1, t0:t0 + PH2]).then_inc(GD.h, 16)
                g.dma_start(out=browt, in_=s_ab[1:2, t0:t0 + PH2]).then_inc(GD.h, 16)
            GP(_)
            GD.c += 32
            def _(te, w=GD.c):
                te.wait_ge(GD.h, w)
                bank_wait(te, 4)
                bank_wait(te, 5)
                te.matmul(pb[4][:, 0:PH2], ones1[0:1, :], arowt, start=True, stop=True)
                te.matmul(pb[5][:, 0:PH2], ones1[0:1, :], browt,
                          start=True, stop=True).then_inc(TM.h, 1)
            TE(_)
            TM.c += 1
            def _(sc, w=TM.c, wv=VE.c):
                sc.wait_ge(TM.h, w)
                sc.wait_ge(VE.h, wv)
                sc.copy(arep2, pb[4][:, 0:PH2])
                sc.copy(brep2, pb[5][:, 0:PH2]).then_inc(AC.h, 1)
            SC(_)
            AC.c += 1
            pbank_free[4] = (AC, AC.c)
            pbank_free[5] = (AC, AC.c)
            ac_ab2 = AC.c

            def _(sy, p2=p2, w=TM.c, wv=VE.c, wg=gd_ones):
                sy.wait_ge(TM.h, w)
                sy.wait_ge(VE.h, wv)
                if p2 == 0:
                    sy.wait_ge(GD.h, wg)
                sy.dma_start(out=ztile[:, 0:32 * PH2], in_=d_x2[p2]).then_inc(SD.h, 16)
            SY(_)
            SD.c += 16
            def _(v, w=SD.c, ac_ab2=ac_ab2, wt=TM.c):
                v.wait_ge(SD.h, w)
                v.wait_ge(AC.h, ac_ab2)
                v.wait_ge(TM.h, wt)
                xn3 = xn2.rearrange("p (c t) -> p c t", c=32)
                v.tensor_tensor(xn3, ztile[:, 0:32 * PH2].rearrange("p (c t) -> p c t", c=32),
                                arep2[:, None, :].broadcast_to([128, 32, PH2]), ALU.mult)
                v.drain()
                v.tensor_tensor(xn3, xn3,
                                brep2[:, None, :].broadcast_to([128, 32, PH2]), ALU.add).then_inc(VE.h, 1)
            VC(_)
            VE.c += 1
            ve_xn2 = VE.c

            for f in range(64):
                si = f % 2
                ld = load_slab(si, d_w1z[f])
                bsel = f % 4
                def _(te, f=f, si=si, ld=ld, bsel=bsel, ve_xn2=ve_xn2):
                    te.wait_ge(SD.h, ld)
                    if f == 0:
                        te.wait_ge(VE.h, ve_xn2)
                    bank_wait(te, bsel)
                    ins = None
                    for k in range(32):
                        ins = te.matmul(pb[bsel][:, 0:PH2], slab_tile(si, k),
                                        xn2[:, k * PH2:(k + 1) * PH2], start=(k == 0), stop=(k == 31))
                    ins.then_inc(TM.h, 1)
                TE(_)
                TM.c += 1
                slab_used(si, TM.c)
                def _(sc, f=f, bsel=bsel, w=TM.c):
                    sc.wait_ge(TM.h, w)
                    sc.copy(ztile[:, f * PH2:(f + 1) * PH2], pb[bsel][:, 0:PH2]).then_inc(AC.h, 1)
                SC(_)
                AC.c += 1
                pbank_free[bsel] = (AC, AC.c)
            def _(sc, wv=VE.c):
                sc.wait_ge(VE.h, wv)
                for f in range(64):
                    ins = sc.activation(out=ztile[:, f * PH2:(f + 1) * PH2],
                                        in_=ztile[:, f * PH2:(f + 1) * PH2], func=AF.Silu)
                ins.then_inc(AC.h, 1)
            SC(_)
            AC.c += 1
            ac_siluz = AC.c

            for c2 in range(2):
                cg = p2 * 2 + c2
                def _(g, cg=cg, w=scr_tok[cg], t=TM.c, wa=AC.c, wv=VE.c):
                    g.wait_ge(GD.h, w)
                    g.wait_ge(TM.h, t)
                    g.wait_ge(AC.h, wa)
                    g.wait_ge(VE.h, wv)
                    g.dma_start(out=packin[0:64, :], in_=s_cum[cg]).then_inc(GD.h, 16)
                    g.dma_start(out=packin[64:128, :], in_=s_dtp[cg]).then_inc(GD.h, 16)
                    g.dma_start(out=dtpc2, in_=s_dtp[cg]).then_inc(GD.h, 16)
                    g.dma_start(out=btc2, in_=s_bt[cg]).then_inc(GD.h, 16)
                    g.dma_start(out=bc2, in_=s_b[cg]).then_inc(GD.h, 16)
                    g.dma_start(out=cc2, in_=s_c[cg]).then_inc(GD.h, 16)
                GP(_)
                GD.c += 96
                gd_chunk = GD.c

                def _(sc, w=gd_chunk):
                    sc.wait_ge(GD.h, w)
                    sc.activation(out=wr2, in_=packin[0:64, :], func=AF.Exp,
                                  bias=packin[0:64, 127:128], scale=-1.0)
                    sc.activation(out=aec[:, :], in_=packin[0:64, 127:128], func=AF.Exp).then_inc(AC.h, 1)
                SC(_)
                AC.c += 1
                def _(v, w=AC.c):
                    v.wait_ge(AC.h, w)
                    v.memset(rssum, 0.0)
                    v.tensor_tensor(wdt2, wr2, dtpc2, ALU.mult).then_inc(VE.h, 1)
                VC(_)
                VE.c += 1
                ve_wdt2 = VE.c
                def _(te, ve_wdt2=ve_wdt2):
                    te.wait_ge(VE.h, ve_wdt2)
                    bank_wait(te, 6)
                    bank_wait(te, 7)
                    te.transpose(pb[6][:, 0:128], packin[:, :], ident[:, :])
                    te.transpose(pb[7][:, 0:64], wdt2, ident[0:64, 0:64])
                    te.transpose(pb[7][0:1, 64:128], aec[:, :], ident[0:64, 0:64])
                    te.matmul(pb[7][:, 128:256], bc2, cc2, start=True, stop=True).then_inc(TM.h, 1)
                TE(_)
                TM.c += 1
                def _(sc, w=TM.c):
                    sc.wait_ge(TM.h, w)
                    sc.copy(packT, pb[6][:, 0:128])
                    sc.copy(wdtT2, pb[7][:, 0:64])
                    sc.copy(aerow[0:1, :], pb[7][0:1, 64:128])
                    sc.copy(g2sb, pb[7][:, 128:256])
                    sc.activation(out=ecT, in_=pb[6][:, 0:64], func=AF.Exp).then_inc(AC.h, 1)
                SC(_)
                AC.c += 1
                pbank_free[6] = (AC, AC.c)
                def _(te, w=AC.c):
                    te.wait_ge(AC.h, w)
                    te.matmul(pb[7][:, 64:128], ones1[0:1, :], aerow[0:1, :],
                              start=True, stop=True).then_inc(TM.h, 1)
                TE(_)
                TM.c += 1
                def _(sc, w=TM.c):
                    sc.wait_ge(TM.h, w)
                    sc.copy(ae2, pb[7][:, 64:128]).then_inc(AC.h, 1)
                SC(_)
                AC.c += 1
                pbank_free[7] = (AC, AC.c)
                ac_ae2 = AC.c

                ac_e_war = [0, 0]
                tm_y1_war = [0, 0]
                ve_yg_war = [0, 0]
                tm_ss_war = [0, 0]
                for gi in range(NH // LRG):
                    def _(g, cg=cg, gi=gi, t=lr_rd_tok):
                        if t:
                            g.wait_ge(TM.h, t)
                        g.dma_start(out=LT[1:2, :], in_=s_ncum[cg]
                                    .rearrange("h t -> (h t)")[None, gi * LRG * 128:(gi + 1) * LRG * 128]).then_inc(GD.h, 16)
                        g.dma_start(out=RT[0:1, :], in_=s_cum[cg]
                                    .rearrange("h t -> (h t)")[None, gi * LRG * 128:(gi + 1) * LRG * 128]).then_inc(GD.h, 16)
                    GP(_)
                    GD.c += 32
                    lr_tok = GD.c

                    for hgj in range(LRG // HG):
                        hgi = gi * (LRG // HG) + hgj
                        gsel = hgi % 2
                        def _(g, cg=cg, hgi=hgi, gsel=gsel, w=xt_tok[cg][NH - 1], t=xtg_rd_tok[gsel]):
                            g.wait_ge(GD.h, w)
                            if t:
                                g.wait_ge(TM.h, t)
                            g.dma_start(out=xtg[gsel][:, :],
                                        in_=s_xt[cg, :, hgi * HG:(hgi + 1) * HG, :]
                                        .rearrange("p h q -> p (h q)")).then_inc(GD.h, 16)
                        GP(_)
                        GD.c += 16
                        gd_xtg = GD.c

                        for hh in range(HG):
                            h = hgi * HG + hh
                            hl = h - gi * LRG
                            pi = h % 2
                            bD = 0 if h % 2 == 0 else 1
                            bY2 = 2 if h % 2 == 0 else 3
                            bY1 = 4 if h % 2 == 0 else 5
                            bH = 6 if h % 2 == 0 else 7
                            def _(te, h=h, hl=hl, bD=bD, bY2=bY2, lr=lr_tok, gd_xtg=gd_xtg,
                                  ac_ae2=ac_ae2, ve_w=ve_ht_tok[h]):
                                te.wait_ge(GD.h, max(lr, gd_xtg))
                                if h == 0:
                                    te.wait_ge(AC.h, ac_ae2)
                                if ve_w:
                                    te.wait_ge(VE.h, ve_w)
                                bank_wait(te, bD)
                                te.matmul(pb[bD][:, 0:128], LT[0:2, hl * 128:(hl + 1) * 128],
                                          RT[0:2, hl * 128:(hl + 1) * 128], start=True, stop=True)
                                bank_wait(te, bY2)
                                te.matmul(pb[bY2][:, 0:128], cc2, HT[:, h * 128:(h + 1) * 128],
                                          start=True, stop=True).then_inc(TM.h, 1)
                            TE(_)
                            TM.c += 1
                            lr_rd_tok = TM.c
                            xtg_rd_tok[gsel] = TM.c
                            def _(v, pi=pi, bD=bD, w=TM.c, w2=ac_e_war[pi]):
                                v.wait_ge(TM.h, w)
                                if w2:
                                    v.wait_ge(AC.h, w2)
                                v.tensor_tensor(esb[pi], maskbig[:, :], pb[bD][:, 0:128], ALU.min).then_inc(VE.h, 1)
                            VC(_)
                            VE.c += 1
                            def _(sc, pi=pi, w=VE.c):
                                sc.wait_ge(VE.h, w)
                                sc.activation(out=esbE[pi], in_=esb[pi], func=AF.Exp).then_inc(AC.h, 1)
                            SC(_)
                            AC.c += 1
                            pbank_free[bD] = (AC, AC.c)
                            ac_e_war[pi] = AC.c
                            def _(v, pi=pi, h=h, w=AC.c, w2=tm_y1_war[pi]):
                                v.wait_ge(AC.h, w)
                                if w2:
                                    v.wait_ge(TM.h, w2)
                                v.scalar_tensor_tensor(wsb[pi], esbE[pi], packT[:, 64 + h:65 + h],
                                                       g2sb, ALU.mult, ALU.mult).then_inc(VE.h, 1)
                            VC(_)
                            VE.c += 1
                            def _(te, pi=pi, hh=hh, gsel=gsel, bY1=bY1, w=VE.c):
                                te.wait_ge(VE.h, w)
                                bank_wait(te, bY1)
                                te.matmul(pb[bY1][:, 0:128], wsb[pi],
                                          xtg[gsel][:, hh * 128:(hh + 1) * 128],
                                          start=True, stop=True).then_inc(TM.h, 1)
                            TE(_)
                            TM.c += 1
                            tm_y1_war[pi] = TM.c
                            xtg_rd_tok[gsel] = TM.c
                            def _(v, pi=pi, h=h, hh=hh, gsel=gsel, bY2=bY2, bY1=bY1, w=TM.c):
                                v.wait_ge(TM.h, w)
                                v.tensor_scalar(y2sb[pi], pb[bY2][:, 0:128], ecT[:, h:h + 1], None, ALU.mult)
                                v.scalar_tensor_tensor(ytf[pi], xtg[gsel][:, hh * 128:(hh + 1) * 128],
                                                       dcol[:, h:h + 1], pb[bY1][:, 0:128],
                                                       ALU.mult, ALU.add)
                                v.drain()
                                v.tensor_tensor(ytf[pi], ytf[pi], y2sb[pi], ALU.add).then_inc(VE.h, 1)
                            VC(_)
                            VE.c += 1
                            ve_yt = VE.c
                            def _(v, pi=pi, h=h, w=TM.c):
                                v.wait_ge(TM.h, w)
                                v.tensor_scalar(btw2[pi], btc2, wdtT2[:, h:h + 1], None,
                                                ALU.mult).then_inc(VE.h, 1)
                            VC(_)
                            VE.c += 1
                            ve_btw2 = VE.c
                            def _(te, pi=pi, hh=hh, gsel=gsel, bH=bH, bY2=bY2, ve_btw2=ve_btw2,
                                  ve_yt=ve_yt):
                                te.wait_ge(VE.h, ve_btw2)
                                bank_wait(te, bH)
                                te.matmul(pb[bH][:, 0:128], btw2[pi],
                                          xtg[gsel][:, hh * 128:(hh + 1) * 128],
                                          start=True, stop=True)
                                te.wait_ge(VE.h, ve_yt)
                                te.transpose(pb[bY2][:, 128:256], ytf[pi], ident[:, :]).then_inc(TM.h, 1)
                            TE(_)
                            TM.c += 1
                            tm_hb = TM.c
                            xtg_rd_tok[gsel] = TM.c
                            def _(v, h=h, bH=bH, w=tm_hb):
                                v.wait_ge(TM.h, w)
                                v.scalar_tensor_tensor(HT[:, h * 128:(h + 1) * 128],
                                                       HT[:, h * 128:(h + 1) * 128],
                                                       ae2[:, h:h + 1], pb[bH][:, 0:128],
                                                       ALU.mult, ALU.add).then_inc(VE.h, 1)
                            VC(_)
                            VE.c += 1
                            pbank_free[bH] = (VE, VE.c)
                            ve_ht_tok[h] = VE.c
                            def _(v, h=h, c2=c2, bY2=bY2, w=tm_hb, wz=ac_siluz):
                                v.wait_ge(TM.h, w)
                                if h == 0:
                                    v.wait_ge(AC.h, wz)
                                zs = ztile[:, h * PH2 + c2 * 128:h * PH2 + (c2 + 1) * 128]
                                v.scalar_tensor_tensor(zs, zs, nw[:, h:h + 1], pb[bY2][:, 128:256],
                                                       ALU.mult, ALU.mult).then_inc(VE.h, 1)
                            VC(_)
                            VE.c += 1
                            pbank_free[bY2] = (VE, VE.c)
                            ve_yg_war[pi] = VE.c
                            def _(sc, pi=pi, h=h, c2=c2, w=VE.c, w2=tm_ss_war[pi]):
                                sc.wait_ge(VE.h, w)
                                if w2:
                                    sc.wait_ge(TM.h, w2)
                                sc.activation(out=sqt2[pi],
                                              in_=ztile[:, h * PH2 + c2 * 128:h * PH2 + (c2 + 1) * 128],
                                              func=AF.Square).then_inc(AC.h, 1)
                            SC(_)
                            AC.c += 1
                            def _(te, pi=pi, bY1=bY1, w=AC.c):
                                te.wait_ge(AC.h, w)
                                bank_wait(te, bY1)
                                te.matmul(pb[bY1][0:1, 128:256], onescol[:, :], sqt2[pi],
                                          start=True, stop=True).then_inc(TM.h, 1)
                            TE(_)
                            TM.c += 1
                            tm_ss_war[pi] = TM.c
                            def _(v, bY1=bY1, w=TM.c):
                                v.wait_ge(TM.h, w)
                                v.tensor_tensor(rssum, rssum, pb[bY1][0:1, 128:256], ALU.add).then_inc(VE.h, 1)
                            VC(_)
                            VE.c += 1
                            pbank_free[bY1] = (VE, VE.c)

                def _(sc, w=VE.c):
                    sc.wait_ge(VE.h, w)
                    sc.activation(out=rrow, in_=rssum, func=AF.Sqrt,
                                  bias=epsc[0:1, :], scale=r8k[0:1, :]).then_inc(AC.h, 1)
                SC(_)
                AC.c += 1
                def _(v, w=AC.c):
                    v.wait_ge(AC.h, w)
                    v.reciprocal(rrow, rrow).then_inc(VE.h, 1)
                VC(_)
                VE.c += 1
                def _(te, w=VE.c):
                    te.wait_ge(VE.h, w)
                    bank_wait(te, 6)
                    te.matmul(pb[6][:, 128:256], ones1[0:1, :], rrow[0:1, :],
                              start=True, stop=True).then_inc(TM.h, 1)
                TE(_)
                TM.c += 1
                def _(sc, w=TM.c):
                    sc.wait_ge(TM.h, w)
                    sc.copy(rrep, pb[6][:, 128:256]).then_inc(AC.h, 1)
                SC(_)
                AC.c += 1
                pbank_free[6] = (AC, AC.c)
                def _(v, c2=c2, w=AC.c):
                    v.wait_ge(AC.h, w)
                    ins = None
                    for h in range(NH):
                        ins = v.tensor_tensor(ztile[:, h * PH2 + c2 * 128:h * PH2 + (c2 + 1) * 128],
                                              ztile[:, h * PH2 + c2 * 128:h * PH2 + (c2 + 1) * 128],
                                              rrep, ALU.mult)
                    ins.then_inc(VE.h, 1)
                VC(_)
                VE.c += 1
            ve_rms = VE.c

            for o in range(32):
                def _(g, o=o, p2=p2, w=outst_war[o % 2]):
                    if w:
                        g.wait_ge(VD.h, w)
                    g.dma_start(out=ffin[o % 2], in_=s_ffn[o][:, p2 * PH2:(p2 + 1) * PH2]).then_inc(GD.h, 16)
                GP(_)
                GD.c += 16
                gd_ffin = GD.c
                def _(sy, o=o, p2=p2, w=(VD.c if (p2 > 0 or o > 0) else 0), wv=VE.c):
                    if w:
                        sy.wait_ge(VD.h, w)
                    elif wv:
                        sy.wait_ge(VE.h, wv)
                    sy.dma_start(out=xst2, in_=d_x2[p2][:, o * PH2:(o + 1) * PH2]).then_inc(SD.h, 16)
                SY(_)
                SD.c += 16
                sd_x2b = SD.c
                bsel = 2 + (o % 2)
                for g in range(4):
                    hb = (o * 4 + g) % 4
                    gate = slab_free[hb]
                    def _(sy, o=o, g=g, hb=hb, gate=gate):
                        if gate is not None:
                            sy.wait_ge(gate[0].h, gate[1])
                        sy.dma_start(out=slabh[hb][:, :],
                                     in_=d_wout[o, g // 2][:, (g % 2) * 2048:((g % 2) + 1) * 2048]).then_inc(SD.h, 16)
                    SY(_)
                    SD.c += 16
                    ldg = SD.c
                    def _(te, o=o, g=g, hb=hb, ldg=ldg, bsel=bsel, ve_rms=ve_rms):
                        te.wait_ge(SD.h, ldg)
                        if o == 0 and g == 0:
                            te.wait_ge(VE.h, ve_rms)
                        if g == 0:
                            bank_wait(te, bsel)
                        ins = None
                        for kk in range(16):
                            k = g * 16 + kk
                            ins = te.matmul(pb[bsel][:, 0:PH2], slabh[hb][:, kk * 128:(kk + 1) * 128],
                                            ztile[:, k * PH2:(k + 1) * PH2],
                                            start=(k == 0), stop=(k == 63))
                        ins.then_inc(TM.h, 1)
                    TE(_)
                    TM.c += 1
                    slab_free[hb] = (TM, TM.c)
                sti = o % 2
                def _(v, o=o, sti=sti, bsel=bsel, w=TM.c, gd_ffin=gd_ffin,
                      sd_x2b=sd_x2b, w2=outst_war[sti]):
                    v.wait_ge(TM.h, w)
                    v.wait_ge(GD.h, gd_ffin)
                    v.wait_ge(SD.h, sd_x2b)
                    if w2:
                        v.wait_ge(VD.h, w2)
                    v.scalar_tensor_tensor(outst[sti], ffin[o % 2], 1.0, pb[bsel][:, 0:PH2],
                                           ALU.mult, ALU.add)
                    v.drain()
                    v.tensor_tensor(outst[sti], outst[sti], xst2, ALU.add).then_inc(VE.h, 1)
                VC(_)
                VE.c += 1
                def _(sc, o=o, sti=sti, p2=p2, w=VE.c):
                    sc.wait_ge(VE.h, w)
                    sc.dma_start(out=d_out[o * 128:(o + 1) * 128, p2 * PH2:(p2 + 1) * PH2],
                                 in_=outst[sti]).then_inc(VD.h, 16)
                SC(_)
                VD.c += 16
                outst_war[sti] = VD.c
                pbank_free[bsel] = (VD, VD.c)

        def _(sy, w=VD.c):
            sy.wait_ge(VD.h, w)
        SY(_)
    return nc


def _pack_lhsT(cols):
    w = cols.shape[1]
    return np.ascontiguousarray(cols.reshape(32, 128, w).transpose(1, 0, 2).reshape(128, 32 * w))


def _pack_wout(cols):
    r = cols.reshape(64, 128, 128)
    out = np.empty((2, 128, 32 * 128), np.float32)
    for half in range(2):
        out[half] = r[half * 32:(half + 1) * 32].transpose(1, 0, 2).reshape(128, 32 * 128)
    return out


def _prep_weights(inputs):
    W_in = np.ascontiguousarray(inputs["L0_attn_W_in"], dtype=np.float32)
    conv_w = np.ascontiguousarray(inputs["L0_attn_conv_w"], dtype=np.float32)
    W_out = np.ascontiguousarray(inputs["L0_attn_W_out"], dtype=np.float32)
    Wf = np.ascontiguousarray(inputs["L0_ffn_W"], dtype=np.float32)
    Af = np.ascontiguousarray(inputs["L0_ffn_A"], dtype=np.float32)
    Bf = np.ascontiguousarray(inputs["L0_ffn_B"], dtype=np.float32)
    gw = np.ascontiguousarray(inputs["L0_ffn_gate_w"], dtype=np.float32)

    tiles = []
    dtcols = np.zeros((HD, 128), np.float32)
    dtcols[:, 0:64] = W_in[:, DSSM + CONVD:]
    tiles.append(_pack_lhsT(dtcols))
    tiles.append(_pack_lhsT(W_in[:, 2 * DSSM:2 * DSSM + 128]))
    tiles.append(_pack_lhsT(W_in[:, 2 * DSSM + 128:2 * DSSM + 256]))
    for j in range(64):
        tiles.append(_pack_lhsT(W_in[:, DSSM + j * 128:DSSM + (j + 1) * 128]))
    w1a = np.stack(tiles)
    w1z = np.stack([_pack_lhsT(W_in[:, j * 128:(j + 1) * 128]) for j in range(64)])
    wout = np.stack([_pack_wout(W_out[:, o * 128:(o + 1) * 128]) for o in range(32)])
    wffn = np.stack([_pack_lhsT(np.ascontiguousarray(Wf[o * 128:(o + 1) * 128, :].T))
                     for o in range(32)])
    apack = np.ascontiguousarray(Af.T.reshape(32, 128, 64).transpose(1, 0, 2).reshape(128, 32 * 64))
    blt = np.ascontiguousarray(Bf.T)
    gate = np.ascontiguousarray(gw.T.reshape(32, 128, 4).transpose(1, 0, 2).reshape(128, 32 * 4))
    cw = np.concatenate([conv_w[DSSM:DSSM + 128], conv_w[DSSM + 128:DSSM + 256],
                         conv_w[0:DSSM]], axis=0)
    convw = np.ascontiguousarray(cw.reshape(66, 128, KC).transpose(1, 0, 2).reshape(128, 66 * KC))
    return dict(w1a=w1a, w1z=w1z, wout=wout, wffn=wffn, apack=apack, blt=blt,
                gatew=gate, convw=convw)


_NC_CACHE = {}


def kernel(**inputs):
    x = np.asarray(inputs["L0_attn_hidden_states"], dtype=np.float32)
    wp = _prep_weights(inputs)
    dt_bias = np.asarray(inputs["L0_attn_dt_bias"], np.float32).reshape(64, 1)
    A_log = np.asarray(inputs["L0_attn_A_log"], np.float32).reshape(64, 1)
    D_param = np.asarray(inputs["L0_attn_D_param"], np.float32)
    norm_w = np.asarray(inputs["L0_attn_norm_weight"], np.float32)
    bf = np.asarray(inputs["L0_ffn_b"], np.float32)
    gb = np.asarray(inputs["L0_ffn_gate_b"], np.float32).reshape(4, 1)
    dcol = np.ascontiguousarray(np.broadcast_to(D_param[None, :], (128, 64)))
    nwp = np.ascontiguousarray(norm_w.reshape(64, 128).T)
    bffn = np.ascontiguousarray(bf.reshape(32, 128).T)

    if "nc" not in _NC_CACHE:
        _NC_CACHE["nc"] = build_program()
    nc = _NC_CACHE["nc"]

    in_maps = []
    for i in range(8):
        b, half = i // 2, i % 2
        t0 = half * T
        xfm = np.zeros((HD, T + 3), np.float32)
        if half == 1:
            xfm[:, 0:3] = x[b, t0 - 3:t0].T
        xfm[:, 3:] = x[b, t0:t0 + T].T
        x1 = np.empty((NPH1, 128, 32 * W1), np.float32)
        for p in range(NPH1):
            x1[p] = xfm[:, p * PH1:p * PH1 + W1].reshape(32, 128, W1).transpose(1, 0, 2).reshape(128, 32 * W1)
        x2 = np.empty((NPH2, 128, 32 * PH2), np.float32)
        for p in range(NPH2):
            x2[p] = xfm[:, 3 + p * PH2:3 + (p + 1) * PH2].reshape(32, 128, PH2).transpose(1, 0, 2).reshape(128, 32 * PH2)
        m = 1.0 if half == 1 else 0.0
        in_maps.append({
            "x1": x1, "x2": x2,
            "w1a": wp["w1a"], "w1z": wp["w1z"], "wout": wp["wout"], "wffn": wp["wffn"],
            "apack": wp["apack"], "blt": wp["blt"], "gatew": wp["gatew"], "convw": wp["convw"],
            "dtb": dt_bias, "alog": A_log, "dcol": dcol, "nw": nwp, "bffn": bffn,
            "gateb": gb,
            "hmask": np.full((128, 1), m, np.float32),
            "hinv": np.full((128, 1), 1.0 - m, np.float32),
        })
    res = run_bass_kernel_spmd(nc, in_maps, core_ids=list(range(8)))
    out = np.empty((4, 4096, 4096), np.float32)
    for i in range(8):
        b, half = i // 2, i % 2
        out[b, half * T:(half + 1) * T, :] = res.results[i]["out"].T
    return out


# version 04 (final)
# speedup vs baseline: 1.1468x; 1.0876x over previous
import contextlib
import numpy as np
import concourse.bass as bass
import concourse.mybir as mybir
from concourse.bass_utils import run_bass_kernel_spmd

F32 = mybir.dt.float32
AF = mybir.ActivationFunctionType
ALU = mybir.AluOpType

HD = 4096
DSSM = 8192
NH = 64
KC = 4
EPS = 1e-5
CONVD = DSSM + 256
T = 2048
NCH = 16
PH1 = 512
NPH1 = 4
PH2 = 256
NPH2 = 8
W1 = PH1 + 3
RANKS = (8, 16, 32, 64)
NF1 = 67
HG = 8
LRG = 8


class Sem:
    def __init__(self, stack, nc, name):
        self.h = stack.enter_context(nc.semaphore(name))
        self.c = 0


def build_program():
    nc = bass.Bass("TRN2", target_bir_lowering=False)

    d_x1 = nc.dram_tensor("x1", [NPH1, 128, 32 * W1], F32, kind="ExternalInput").ap()
    d_x2 = nc.dram_tensor("x2", [NPH2, 128, 32 * PH2], F32, kind="ExternalInput").ap()
    d_w1a = nc.dram_tensor("w1a", [NF1, 128, 32 * 128], F32, kind="ExternalInput").ap()
    d_w1z = nc.dram_tensor("w1z", [64, 128, 32 * 128], F32, kind="ExternalInput").ap()
    d_wout = nc.dram_tensor("wout", [32, 2, 128, 32 * 128], F32, kind="ExternalInput").ap()
    d_wffn = nc.dram_tensor("wffn", [32, 128, 32 * 128], F32, kind="ExternalInput").ap()
    d_apack = nc.dram_tensor("apack", [128, 32 * 64], F32, kind="ExternalInput").ap()
    d_blt = nc.dram_tensor("blt", [64, HD], F32, kind="ExternalInput").ap()
    d_gate = nc.dram_tensor("gatew", [128, 32 * 4], F32, kind="ExternalInput").ap()
    d_convw = nc.dram_tensor("convw", [128, 66 * KC], F32, kind="ExternalInput").ap()
    d_dtb = nc.dram_tensor("dtb", [64, 1], F32, kind="ExternalInput").ap()
    d_alog = nc.dram_tensor("alog", [64, 1], F32, kind="ExternalInput").ap()
    d_dcol = nc.dram_tensor("dcol", [128, 64], F32, kind="ExternalInput").ap()
    d_nw = nc.dram_tensor("nw", [128, 64], F32, kind="ExternalInput").ap()
    d_bffn = nc.dram_tensor("bffn", [128, 32], F32, kind="ExternalInput").ap()
    d_gateb = nc.dram_tensor("gateb", [4, 1], F32, kind="ExternalInput").ap()
    d_hmask = nc.dram_tensor("hmask", [128, 1], F32, kind="ExternalInput").ap()
    d_hinv = nc.dram_tensor("hinv", [128, 1], F32, kind="ExternalInput").ap()
    d_out = nc.dram_tensor("out", [HD, T], F32, kind="ExternalOutput").ap()

    s_xt = nc.dram_tensor("s_xt", [NCH, 128, NH, 128], F32).ap()
    s_b = nc.dram_tensor("s_b", [NCH, 128, 128], F32).ap()
    s_bt = nc.dram_tensor("s_bt", [NCH, 128, 128], F32).ap()
    s_c = nc.dram_tensor("s_c", [NCH, 128, 128], F32).ap()
    s_cum = nc.dram_tensor("s_cum", [NCH, 64, 128], F32).ap()
    s_ncum = nc.dram_tensor("s_ncum", [NCH, 64, 128], F32).ap()
    s_dtp = nc.dram_tensor("s_dtp", [NCH, 64, 128], F32).ap()
    s_ffn = nc.dram_tensor("s_ffn", [32, 128, T], F32).ap()
    s_ab = nc.dram_tensor("s_ab", [2, T], F32).ap()
    cc_in = nc.dram_tensor("cc_in", [128, NH * 128], F32)
    cc_out = nc.dram_tensor("cc_out", [128, NH * 128], F32)

    with contextlib.ExitStack() as st:
        def sbt(name, shape):
            return st.enter_context(nc.sbuf_tensor("t_" + name, shape, F32))

        ident = sbt("ident", [128, 128])
        maskbig = sbt("maskbig", [128, 128])
        jt64 = sbt("jt64", [128, 64])
        convw = sbt("convw", [128, 66 * KC])
        dtb = sbt("dtb", [64, 1])
        alog = sbt("alog", [64, 1])
        negea = sbt("negea", [64, 1])
        gateb = sbt("gateb", [4, 1])
        dcol = sbt("dcol", [128, 64])
        nw = sbt("nw", [128, 64])
        bffn = sbt("bffn", [128, 32])
        hmask = sbt("hmask", [128, 1])
        hinv = sbt("hinv", [128, 1])
        ones1 = sbt("ones1", [1, 128])
        onescol = sbt("onescol", [128, 1])
        zz64 = sbt("zz64", [64, 128])
        slabh = [sbt(f"slabh{i}", [128, 16 * 128]) for i in range(4)]
        HT = sbt("HT", [128, NH * 128])
        xtg = [sbt(f"xtg{i}", [128, HG * 128]) for i in range(2)]
        aec = sbt("aec", [64, 1])
        aerow = sbt("aerow", [1, 64])
        epsc = sbt("epsc", [128, 1])
        r8k = sbt("r8k", [128, 1])

        UC = 33248
        uni = sbt("uni", [128, UC])
        cur = [0]

        def ualloc(pdim, cols):
            off = cur[0]
            cur[0] += cols
            assert cur[0] <= UC, f"union overflow {cur[0]}"
            return uni[0:pdim, off:off + cols]

        xn1 = ualloc(128, 32 * W1)
        tmpx = ualloc(128, W1)
        tmpv2 = ualloc(128, W1)
        tmpg = ualloc(128, W1)
        arep = ualloc(128, W1)
        brep = ualloc(128, W1)
        arow = ualloc(1, W1)
        brow = ualloc(1, W1)
        srow = ualloc(1, W1)
        s2row = ualloc(1, W1)
        xbcf = ualloc(128, W1)
        cvs = ualloc(128, PH1)
        cvs2 = ualloc(128, PH1)
        bph = ualloc(128, PH1)
        cph = ualloc(128, PH1)
        xh = ualloc(128, PH1)
        btph = ualloc(128, 4 * 128)
        dtp1 = ualloc(64, PH1)
        loga1 = ualloc(64, PH1)
        cumc1 = [ualloc(64, 128) for _ in range(4)]
        ngc1 = ualloc(64, 128)
        wr1 = ualloc(64, 128)
        wdt1 = [ualloc(64, 128) for _ in range(4)]
        wdtT1 = [ualloc(128, 64) for _ in range(4)]
        ae1 = [ualloc(128, 64) for _ in range(4)]
        xtst = [ualloc(128, 128) for _ in range(4)]
        btw1 = [ualloc(128, 128) for _ in range(2)]
        scph = ualloc(4, PH1)
        scT = [ualloc(128, 4) for _ in range(2)]
        rcol = [ualloc(128, 1) for _ in range(2)]
        bestc = ualloc(128, 1)
        dstep = ualloc(128, 1)
        maskj = [ualloc(128, 64) for _ in range(2)]
        p64m = [ualloc(128, 64) for _ in range(2)]
        p64T = ualloc(64, PH1)
        ffst = [ualloc(128, PH1) for _ in range(2)]
        apk = ualloc(128, 32 * 64)
        bltt = [ualloc(64, 128) for _ in range(2)]
        gatew = ualloc(128, 32 * 4)
        hal = ualloc(128, NF1 * 3)
        assert cur[0] <= UC, cur[0]

        cur[0] = 0
        ztile = ualloc(128, 64 * PH2)
        xn2 = ualloc(128, 32 * PH2)
        xst2 = ualloc(128, PH2)
        tmp2 = ualloc(128, PH2)
        LT = ualloc(2, LRG * 128)
        RT = ualloc(2, LRG * 128)
        onesrow = ualloc(1, 128)
        btc2 = ualloc(128, 128)
        bc2 = ualloc(128, 128)
        cc2 = ualloc(128, 128)
        packin = ualloc(128, 128)
        packT = ualloc(128, 128)
        dtpc2 = ualloc(64, 128)
        wr2 = ualloc(64, 128)
        wdt2 = ualloc(64, 128)
        wdtT2 = ualloc(128, 64)
        ae2 = ualloc(128, 64)
        ecT = ualloc(128, 64)
        g2sb = ualloc(128, 128)
        esb = [ualloc(128, 128) for _ in range(2)]
        esbE = [ualloc(128, 128) for _ in range(2)]
        wsb = [ualloc(128, 128) for _ in range(2)]
        y2sb = [ualloc(128, 128) for _ in range(2)]
        ytf = [ualloc(128, 128) for _ in range(2)]
        sszt = [ualloc(128, 128) for _ in range(2)]
        sqt2 = sszt
        btw2 = [ualloc(128, 128) for _ in range(2)]
        arowt = ualloc(1, PH2)
        browt = ualloc(1, PH2)
        rssum = ualloc(1, 128)
        rrow = ualloc(1, 128)
        rrep = ualloc(128, 128)
        outst = [ualloc(128, PH2) for _ in range(2)]
        ffin = [ualloc(128, PH2) for _ in range(2)]
        arep2 = ualloc(128, PH2)
        brep2 = ualloc(128, PH2)
        assert cur[0] <= UC, cur[0]

        pb = [st.enter_context(nc.psum_tensor(f"pb{i}", [128, 512], F32)) for i in range(8)]

        SD = Sem(st, nc, "SD")
        GD = Sem(st, nc, "GD")
        VD = Sem(st, nc, "VD")
        TM = Sem(st, nc, "TM")
        VE = Sem(st, nc, "VE")
        AC = Sem(st, nc, "AC")
        GC = Sem(st, nc, "GC")
        CCS = Sem(st, nc, "CCS")

        block = st.enter_context(nc.Block())
        TE, VC, SC, GP, SY = block.tensor, block.vector, block.scalar, block.gpsimd, block.sync

        pbank_free = [None] * 8
        slab_free = [None, None, None, None]

        class SlabPair:
            def __init__(self, i0, i1):
                self.i0, self.i1 = i0, i1

            def __getitem__(self, sl):
                raise RuntimeError("use slab_tile()")

        def slab_tile(pi, k):
            half = k // 16
            return slabh[pi * 2 + half][:, (k % 16) * 128:((k % 16) + 1) * 128]

        def bank_wait(e, i):
            if pbank_free[i] is not None:
                sem, val = pbank_free[i]
                e.wait_ge(sem.h, val)

        def load_slab(pi, src_ap):
            for half in range(2):
                hi = pi * 2 + half
                gate = slab_free[hi]
                def _(sy, hi=hi, src_ap=src_ap, half=half, gate=gate):
                    if gate is not None:
                        sy.wait_ge(gate[0].h, gate[1])
                    sy.dma_start(out=slabh[hi][:, :],
                                 in_=src_ap[:, half * 2048:(half + 1) * 2048]).then_inc(SD.h, 16)
                SY(_)
                SD.c += 16
            return SD.c

        def slab_used(pi, tok):
            slab_free[pi * 2] = (TM, tok)
            slab_free[pi * 2 + 1] = (TM, tok)

        def _(sy):
            for dst, src in [(convw, d_convw), (dtb, d_dtb), (alog, d_alog), (gateb, d_gateb),
                             (dcol, d_dcol), (nw, d_nw), (bffn, d_bffn), (hmask, d_hmask),
                             (hinv, d_hinv)]:
                sy.dma_start(out=dst[:, :], in_=src[:, :]).then_inc(SD.h, 16)
            sy.dma_start(out=apk, in_=d_apack[:, :]).then_inc(SD.h, 16)
            sy.dma_start(out=gatew, in_=d_gate[:, :]).then_inc(SD.h, 16)
        SY(_)
        SD.c += 11 * 16
        sd_consts = SD.c

        def _(g):
            g.iota(maskbig[:, :], [[1, 128]], base=0, channel_multiplier=-1,
                   allow_small_or_imprecise_dtypes=True)
            g.tensor_scalar(ident[:, :], maskbig[:, :], 0.0, None, ALU.is_equal)
            g.tensor_scalar(maskbig[:, :], maskbig[:, :], 0.0, None, ALU.is_ge)
            g.tensor_scalar(maskbig[:, :], maskbig[:, :], -1.0, 1e30, ALU.add, ALU.mult)
            g.iota(jt64[:, :], [[1, 64]], base=0, channel_multiplier=0,
                   allow_small_or_imprecise_dtypes=True)
            g.memset(ones1[:, :], 1.0)
            g.memset(onescol[:, :], 1.0)
            g.memset(zz64[:, :], 0.0)
            g.memset(epsc[:, :], 1e-5)
            g.memset(r8k[:, :], 1.0 / 8192.0)
            g.memset(HT[:, :], 0.0).then_inc(GC.h, 1)
        GP(_)
        GC.c += 1
        gc_pre = GC.c

        def _(sc):
            sc.wait_ge(SD.h, sd_consts)
            sc.activation(out=negea[:, :], in_=alog[:, :], func=AF.Exp).then_inc(AC.h, 1)
        SC(_)
        AC.c += 1
        def _(v, w=AC.c):
            v.wait_ge(AC.h, w)
            v.tensor_scalar(negea[:, :], negea[:, :], -1.0, None, ALU.mult).then_inc(VE.h, 1)
        VC(_)
        VE.c += 1
        ve_pre = VE.c

        xt_tok = [[0] * NH for _ in range(NCH)]
        scr_tok = [0] * NCH
        cum_war = [None] * 4
        wdt_war = [0] * 4
        dtp_war = dict(ve=0, gd=0)
        bph_war = dict(gd=0, tm=0)
        cph_war = dict(gd=0)
        xh_war = dict(tm=0)
        xtst_war = [0] * 4
        btph_war = dict(gd=0, ve=0)
        ffst_war = [0, 0]
        aec_war = dict(tm=0)
        ngc_war = dict(gd=0)
        stats_war = dict(ac=0, ve=0, gc=0)
        bltt_war = [0, 0]
        conv_ve_tok = 0

        for p1 in range(NPH1):
            def _(sy, p1=p1, t=TM.c, v=VE.c, g=GC.c):
                if p1 > 0:
                    sy.wait_ge(TM.h, t)
                    sy.wait_ge(VE.h, v)
                    sy.wait_ge(GC.h, g)
                sy.dma_start(out=xn1, in_=d_x1[p1]).then_inc(SD.h, 16)
            SY(_)
            SD.c += 16
            x_tok = SD.c

            sq_tm_tok = [0] * 32
            for c in range(32):
                def _(sc, c=c, x_tok=x_tok, w=(sq_tm_tok[c - 2] if c >= 2 else 0),
                      w2=stats_war["ve"], w3=stats_war["gc"]):
                    if c == 0:
                        sc.wait_ge(SD.h, x_tok)
                        if w2:
                            sc.wait_ge(VE.h, w2)
                        if w3:
                            sc.wait_ge(GC.h, w3)
                    if w:
                        sc.wait_ge(TM.h, w)
                    sc.activation(out=(tmpx if c % 2 == 0 else tmpg), in_=xn1[:, c * W1:(c + 1) * W1],
                                  func=AF.Square).then_inc(AC.h, 1)
                SC(_)
                AC.c += 1
                def _(te, c=c, ac_sq=AC.c, x_tok=x_tok, ve_gate=VE.c):
                    te.wait_ge(AC.h, ac_sq)
                    if c == 0:
                        te.wait_ge(SD.h, x_tok)
                        te.wait_ge(VE.h, ve_gate)
                        for i in range(4):
                            bank_wait(te, i)
                    sq = tmpx if c % 2 == 0 else tmpg
                    te.matmul(pb[0][0:1, 0:512], onescol[:, :], xn1[:, c * W1 + 3:(c + 1) * W1],
                              start=(c == 0), stop=(c == 31))
                    te.matmul(pb[1][0:1, 0:512], onescol[:, :], sq[:, 3:W1],
                              start=(c == 0), stop=(c == 31))
                    te.matmul(pb[2][0:1, 0:3], onescol[:, :], xn1[:, c * W1:c * W1 + 3],
                              start=(c == 0), stop=(c == 31))
                    te.matmul(pb[3][0:1, 0:3], onescol[:, :], sq[:, 0:3],
                              start=(c == 0), stop=(c == 31)).then_inc(TM.h, 1)
                TE(_)
                TM.c += 1
                sq_tm_tok[c] = TM.c
            tm_stats = TM.c

            def _(v, tm_stats=tm_stats, w=stats_war["ac"]):
                v.wait_ge(TM.h, tm_stats)
                if w:
                    v.wait_ge(AC.h, w)
                v.tensor_copy(srow[0:1, 3:W1], pb[0][0:1, 0:512])
                v.tensor_copy(srow[0:1, 0:3], pb[2][0:1, 0:3])
                v.tensor_copy(s2row[0:1, 3:W1], pb[1][0:1, 0:512])
                v.tensor_copy(s2row[0:1, 0:3], pb[3][0:1, 0:3])
                v.drain()
                v.tensor_scalar(srow, srow, 1.0 / HD, None, ALU.mult)
                v.tensor_scalar(s2row, s2row, 1.0 / HD, None, ALU.mult)
                v.drain()
                v.tensor_tensor(brow, srow, srow, ALU.mult)
                v.drain()
                v.tensor_tensor(s2row, s2row, brow, ALU.subtract).then_inc(VE.h, 1)
            VC(_)
            VE.c += 1
            for i in range(4):
                pbank_free[i] = (VE, VE.c)
            def _(sc, w=VE.c, wg=gc_pre):
                sc.wait_ge(VE.h, w)
                sc.wait_ge(GC.h, wg)
                sc.activation(out=arow, in_=s2row, func=AF.Sqrt, bias=epsc[0:1, :], scale=1.0).then_inc(AC.h, 1)
            SC(_)
            AC.c += 1
            stats_war["ac"] = AC.c
            def _(v, w=AC.c, p1=p1):
                v.wait_ge(AC.h, w)
                v.reciprocal(arow, arow)
                v.drain()
                v.scalar_tensor_tensor(brow, srow, -1.0, arow, ALU.mult, ALU.mult).then_inc(VE.h, 1)
            VC(_)
            VE.c += 1
            def _(g, p1=p1, w=VE.c):
                g.wait_ge(VE.h, w)
                g.dma_start(out=s_ab[0:1, p1 * PH1:(p1 + 1) * PH1], in_=arow[0:1, 3:W1]).then_inc(GD.h, 16)
                g.dma_start(out=s_ab[1:2, p1 * PH1:(p1 + 1) * PH1], in_=brow[0:1, 3:W1]).then_inc(GD.h, 16)
            GP(_)
            GD.c += 32

            def _(te, w=VE.c):
                te.wait_ge(VE.h, w)
                for i in range(4, 8):
                    bank_wait(te, i)
                te.matmul(pb[4][:, 0:512], ones1[0:1, :], arow[0:1, 0:512], start=True, stop=True)
                te.matmul(pb[5][:, 0:3], ones1[0:1, :], arow[0:1, 512:515], start=True, stop=True)
                te.matmul(pb[6][:, 0:512], ones1[0:1, :], brow[0:1, 0:512], start=True, stop=True)
                te.matmul(pb[7][:, 0:3], ones1[0:1, :], brow[0:1, 512:515], start=True, stop=True).then_inc(TM.h, 1)
            TE(_)
            TM.c += 1
            def _(sc, w=TM.c):
                sc.wait_ge(TM.h, w)
                sc.copy(arep[:, 0:512], pb[4][:, 0:512])
                sc.copy(arep[:, 512:515], pb[5][:, 0:3])
                sc.copy(brep[:, 0:512], pb[6][:, 0:512])
                sc.copy(brep[:, 512:515], pb[7][:, 0:3]).then_inc(AC.h, 1)
            SC(_)
            AC.c += 1
            for i in range(4, 8):
                pbank_free[i] = (AC, AC.c)
            ac_rep = AC.c

            def _(v, w=ac_rep):
                v.wait_ge(AC.h, w)
                for c in range(0, 32, 4):
                    v.tensor_tensor(tmpx, xn1[:, c * W1:(c + 1) * W1], arep, ALU.mult)
                    v.tensor_tensor(tmpv2, xn1[:, (c + 2) * W1:(c + 3) * W1], arep, ALU.mult)
                    v.tensor_tensor(xn1[:, c * W1:(c + 1) * W1], tmpx, brep, ALU.add)
                    v.tensor_tensor(xn1[:, (c + 2) * W1:(c + 3) * W1], tmpv2, brep, ALU.add)
                v.drain()
                v.tensor_copy(srow[0:1, 0:1], srow[0:1, 0:1]).then_inc(VE.h, 1)
            VC(_)
            VE.c += 1
            ve_xn = VE.c
            stats_war["ve"] = VE.c
            def _(g, w=ac_rep):
                g.wait_ge(AC.h, w)
                for c in range(1, 32, 2):
                    g.tensor_tensor(tmpg, xn1[:, c * W1:(c + 1) * W1], arep, ALU.mult)
                    g.tensor_tensor(xn1[:, c * W1:(c + 1) * W1], tmpg, brep, ALU.add)
                g.memset(aerow[0:1, 0:1], 1.0).then_inc(GC.h, 1)
            GP(_)
            GC.c += 1
            gc_xn = GC.c
            stats_war["gc"] = GC.c

            for f in range(NF1):
                si = f % 2
                ld = load_slab(si, d_w1a[f])
                bA, bB = (0, 1) if f % 2 == 0 else (2, 3)
                need_halo_mm = (p1 == 0 and f >= 1)
                def _(te, f=f, si=si, ld=ld, bA=bA, bB=bB, ve_xn=ve_xn, gc_xn=gc_xn,
                      need_halo_mm=need_halo_mm):
                    te.wait_ge(SD.h, ld)
                    if f == 0:
                        te.wait_ge(VE.h, ve_xn)
                        te.wait_ge(GC.h, gc_xn)
                    bank_wait(te, bA)
                    if need_halo_mm:
                        bank_wait(te, bB)
                    ins = None
                    for k in range(32):
                        ins = te.matmul(pb[bA][:, 0:512], slab_tile(si, k),
                                        xn1[:, k * W1 + 3:(k + 1) * W1], start=(k == 0), stop=(k == 31))
                    if need_halo_mm:
                        for k in range(32):
                            ins = te.matmul(pb[bB][:, 0:3], slab_tile(si, k),
                                            xn1[:, k * W1:k * W1 + 3], start=(k == 0), stop=(k == 31))
                    ins.then_inc(TM.h, 1)
                TE(_)
                TM.c += 1
                slab_used(si, TM.c)
                tm_f = TM.c

                if f == 0:
                    def _(sc, tm_f=tm_f, bA=bA, w1=dtp_war["ve"], w2=dtp_war["gd"]):
                        sc.wait_ge(TM.h, tm_f)
                        if w1:
                            sc.wait_ge(VE.h, w1)
                        if w2:
                            sc.wait_ge(GD.h, w2)
                        sc.activation(out=dtp1, in_=pb[bA][0:64, 0:512], func=AF.Abs,
                                      bias=dtb[:, :])
                        sc.activation(out=loga1, in_=dtp1, func=AF.Exp, scale=-1.0)
                        sc.activation(out=tmpx[0:64, 0:512], in_=loga1, func=AF.Ln, bias=1.0)
                        sc.activation(out=tmpg[0:64, 0:512], in_=pb[bA][0:64, 0:512],
                                      func=AF.Relu, bias=dtb[:, :]).then_inc(AC.h, 1)
                    SC(_)
                    AC.c += 1
                    pbank_free[bA] = (AC, AC.c)
                    pbank_free[bB] = (AC, AC.c)
                    def _(v, w=AC.c):
                        v.wait_ge(AC.h, w)
                        v.wait_ge(VE.h, ve_pre)
                        v.tensor_tensor(dtp1, tmpx[0:64, 0:512], tmpg[0:64, 0:512], ALU.add)
                        v.drain()
                        v.tensor_scalar(loga1, dtp1, negea[:, :], None, ALU.mult).then_inc(VE.h, 1)
                    VC(_)
                    VE.c += 1
                    stats_war["ve"] = VE.c
                    for c4 in range(4):
                        cg = p1 * 4 + c4
                        cw = cum_war[c4]
                        def _(v, c4=c4, cw=cw, w_ngc=ngc_war["gd"]):
                            if cw is not None:
                                v.wait_ge(AC.h, cw["ac"])
                                v.wait_ge(GD.h, cw["gd"])
                            if w_ngc:
                                v.wait_ge(GD.h, w_ngc)
                            if c4 == 0:
                                v.drain()
                            v.tensor_tensor_scan(cumc1[c4], loga1[:, c4 * 128:(c4 + 1) * 128],
                                                 zz64[:, :], 0.0, ALU.add, ALU.add)
                            v.drain()
                            v.tensor_scalar(ngc1, cumc1[c4], -1.0, None, ALU.mult).then_inc(VE.h, 1)
                        VC(_)
                        VE.c += 1
                        ve_cum = VE.c
                        def _(g, c4=c4, cg=cg, ve_cum=ve_cum):
                            g.wait_ge(VE.h, ve_cum)
                            g.dma_start(out=s_cum[cg], in_=cumc1[c4]).then_inc(GD.h, 16)
                            g.dma_start(out=s_ncum[cg], in_=ngc1).then_inc(GD.h, 16)
                            g.dma_start(out=s_dtp[cg], in_=dtp1[:, c4 * 128:(c4 + 1) * 128]).then_inc(GD.h, 16)
                        GP(_)
                        GD.c += 48
                        gd_cum = GD.c
                        ngc_war["gd"] = GD.c
                        def _(sc, c4=c4, ve_cum=ve_cum, w=aec_war["tm"]):
                            sc.wait_ge(VE.h, ve_cum)
                            if w:
                                sc.wait_ge(TM.h, w)
                            sc.activation(out=wr1, in_=cumc1[c4], func=AF.Exp,
                                          bias=cumc1[c4][:, 127:128], scale=-1.0)
                            sc.activation(out=aec[:, :], in_=cumc1[c4][:, 127:128],
                                          func=AF.Exp).then_inc(AC.h, 1)
                        SC(_)
                        AC.c += 1
                        ac_wr = AC.c
                        def _(v, c4=c4, ac_wr=ac_wr, w=wdt_war[c4]):
                            v.wait_ge(AC.h, ac_wr)
                            if w:
                                v.wait_ge(TM.h, w)
                            v.tensor_tensor(wdt1[c4], wr1, dtp1[:, c4 * 128:(c4 + 1) * 128],
                                            ALU.mult).then_inc(VE.h, 1)
                        VC(_)
                        VE.c += 1
                        ve_wdt = VE.c
                        def _(te, c4=c4, ve_wdt=ve_wdt):
                            te.wait_ge(VE.h, ve_wdt)
                            bank_wait(te, 6)
                            te.transpose(pb[6][:, 0:64], wdt1[c4], ident[0:64, 0:64])
                            te.transpose(pb[6][0:1, 64:128], aec[:, :], ident[0:64, 0:64]).then_inc(TM.h, 1)
                        TE(_)
                        TM.c += 1
                        wdt_war[c4] = TM.c
                        aec_war["tm"] = TM.c
                        def _(sc, c4=c4, w=TM.c):
                            sc.wait_ge(TM.h, w)
                            sc.copy(wdtT1[c4], pb[6][:, 0:64])
                            sc.copy(aerow[0:1, :], pb[6][0:1, 64:128]).then_inc(AC.h, 1)
                        SC(_)
                        AC.c += 1
                        pbank_free[6] = (AC, AC.c)
                        def _(te, w=AC.c):
                            te.wait_ge(AC.h, w)
                            bank_wait(te, 7)
                            te.matmul(pb[7][:, 0:64], ones1[0:1, :], aerow[0:1, :],
                                      start=True, stop=True).then_inc(TM.h, 1)
                        TE(_)
                        TM.c += 1
                        def _(sc, c4=c4, w=TM.c):
                            sc.wait_ge(TM.h, w)
                            sc.copy(ae1[c4], pb[7][:, 0:64]).then_inc(AC.h, 1)
                        SC(_)
                        AC.c += 1
                        pbank_free[7] = (AC, AC.c)
                        cum_war[c4] = dict(ac=AC.c, gd=gd_cum)
                    dtp_war["ve"] = VE.c
                    dtp_war["gd"] = GD.c
                    continue

                hsl = hal[:, (f - 1) * 3:f * 3]
                def _(sc, tm_f=tm_f, bA=bA, bB=bB, w=conv_ve_tok, p1=p1, hsl=hsl):
                    sc.wait_ge(TM.h, tm_f)
                    if w:
                        sc.wait_ge(VE.h, w)
                    if p1 == 0:
                        sc.copy(xbcf[:, 0:3], pb[bB][:, 0:3])
                    else:
                        sc.copy(xbcf[:, 0:3], hsl)
                    if p1 < NPH1 - 1:
                        sc.copy(xbcf[:, 3:W1], pb[bA][:, 0:512])
                        sc.copy(hsl, pb[bA][:, 509:512]).then_inc(AC.h, 1)
                    else:
                        sc.copy(xbcf[:, 3:W1], pb[bA][:, 0:512]).then_inc(AC.h, 1)
                SC(_)
                AC.c += 1
                pbank_free[bA] = (AC, AC.c)
                if p1 == 0:
                    pbank_free[bB] = (AC, AC.c)
                ac_evac = AC.c

                fc = f - 1
                def _(v, fc=fc, ac_evac=ac_evac):
                    v.wait_ge(AC.h, ac_evac)
                    v.tensor_scalar(cvs, xbcf[:, 0:512], convw[:, fc * 4:fc * 4 + 1], None, ALU.mult)
                    v.tensor_scalar(cvs2, xbcf[:, 1:513], convw[:, fc * 4 + 1:fc * 4 + 2], None, ALU.mult)
                    v.scalar_tensor_tensor(cvs, xbcf[:, 2:514], convw[:, fc * 4 + 2:fc * 4 + 3],
                                           cvs, ALU.mult, ALU.add)
                    v.scalar_tensor_tensor(cvs2, xbcf[:, 3:515], convw[:, fc * 4 + 3:fc * 4 + 4],
                                           cvs2, ALU.mult, ALU.add)
                    v.drain()
                    v.tensor_tensor(cvs, cvs, cvs2, ALU.add).then_inc(VE.h, 1)
                VC(_)
                VE.c += 1
                conv_ve_tok = VE.c
                ve_conv = VE.c

                dst = bph if f == 1 else (cph if f == 2 else xh)
                war = (bph_war if f == 1 else (cph_war if f == 2 else xh_war))
                def _(sc, dst=dst, ve_conv=ve_conv, w1=war.get("gd", 0), w2=war.get("tm", 0)):
                    sc.wait_ge(VE.h, ve_conv)
                    if w1:
                        sc.wait_ge(GD.h, w1)
                    if w2:
                        sc.wait_ge(TM.h, w2)
                    sc.activation(out=dst, in_=cvs, func=AF.Silu).then_inc(AC.h, 1)
                SC(_)
                AC.c += 1
                ac_silu = AC.c

                if f == 1:
                    for c4 in range(4):
                        cg = p1 * 4 + c4
                        def _(g, c4=c4, cg=cg, ac_silu=ac_silu):
                            g.wait_ge(AC.h, ac_silu)
                            g.dma_start(out=s_b[cg], in_=bph[:, c4 * 128:(c4 + 1) * 128]).then_inc(GD.h, 16)
                        GP(_)
                        GD.c += 16
                        def _(te, c4=c4, ac_silu=ac_silu):
                            te.wait_ge(AC.h, ac_silu)
                            bank_wait(te, 6)
                            te.transpose(pb[6][:, 0:128], bph[:, c4 * 128:(c4 + 1) * 128],
                                         ident[:, :]).then_inc(TM.h, 1)
                        TE(_)
                        TM.c += 1
                        bph_war["tm"] = TM.c
                        def _(sc, c4=c4, w=TM.c, w2=btph_war["gd"], w3=btph_war["ve"]):
                            sc.wait_ge(TM.h, w)
                            if c4 == 0:
                                if w2:
                                    sc.wait_ge(GD.h, w2)
                                if w3:
                                    sc.wait_ge(VE.h, w3)
                            sc.copy(btph[:, c4 * 128:(c4 + 1) * 128], pb[6][:, 0:128]).then_inc(AC.h, 1)
                        SC(_)
                        AC.c += 1
                        pbank_free[6] = (AC, AC.c)
                        def _(g, c4=c4, cg=cg, w=AC.c):
                            g.wait_ge(AC.h, w)
                            g.dma_start(out=s_bt[cg], in_=btph[:, c4 * 128:(c4 + 1) * 128]).then_inc(GD.h, 16)
                        GP(_)
                        GD.c += 16
                        scr_tok[cg] = GD.c
                    bph_war["gd"] = GD.c
                    btph_war["gd"] = GD.c
                elif f == 2:
                    for c4 in range(4):
                        cg = p1 * 4 + c4
                        def _(g, c4=c4, cg=cg, ac_silu=ac_silu):
                            g.wait_ge(AC.h, ac_silu)
                            g.dma_start(out=s_c[cg], in_=cph[:, c4 * 128:(c4 + 1) * 128]).then_inc(GD.h, 16)
                        GP(_)
                        GD.c += 16
                        scr_tok[cg] = max(scr_tok[cg], GD.c)
                    cph_war["gd"] = GD.c
                else:
                    h = f - 3
                    for c4 in range(4):
                        cg = p1 * 4 + c4
                        bsel = 6 if c4 % 2 == 0 else 7
                        def _(te, c4=c4, ac_silu=ac_silu, bsel=bsel):
                            te.wait_ge(AC.h, ac_silu)
                            bank_wait(te, bsel)
                            te.transpose(pb[bsel][:, 0:128], xh[:, c4 * 128:(c4 + 1) * 128],
                                         ident[:, :]).then_inc(TM.h, 1)
                        TE(_)
                        TM.c += 1
                        xh_war["tm"] = TM.c
                        def _(sc, c4=c4, w=TM.c, bsel=bsel, w2=xtst_war[c4]):
                            sc.wait_ge(TM.h, w)
                            if w2:
                                sc.wait_ge(GD.h, w2)
                            sc.copy(xtst[c4], pb[bsel][:, 0:128]).then_inc(AC.h, 1)
                        SC(_)
                        AC.c += 1
                        pbank_free[bsel] = (AC, AC.c)
                        def _(g, c4=c4, cg=cg, h=h, w=AC.c):
                            g.wait_ge(AC.h, w)
                            g.dma_start(out=s_xt[cg, :, h, :], in_=xtst[c4]).then_inc(GD.h, 16)
                        GP(_)
                        GD.c += 16
                        xt_tok[cg][h] = GD.c
                        xtst_war[c4] = GD.c

            for c4 in range(4):
                cg = p1 * 4 + c4
                for hgi in range(NH // HG):
                    gsel = hgi % 2
                    def _(g, cg=cg, hgi=hgi, gsel=gsel, w=xt_tok[cg][NH - 1], t=TM.c):
                        g.wait_ge(GD.h, w)
                        if t:
                            g.wait_ge(TM.h, t)
                        g.dma_start(out=xtg[gsel][:, :],
                                    in_=s_xt[cg, :, hgi * HG:(hgi + 1) * HG, :]
                                    .rearrange("p h q -> p (h q)")).then_inc(GD.h, 16)
                    GP(_)
                    GD.c += 16
                    gd_xtg = GD.c
                    for hh in range(HG):
                        h = hgi * HG + hh
                        pi = h % 2
                        def _(v, c4=c4, h=h, pi=pi, w=TM.c):
                            if w:
                                v.wait_ge(TM.h, w)
                            v.tensor_scalar(btw1[pi], btph[:, c4 * 128:(c4 + 1) * 128],
                                            wdtT1[c4][:, h:h + 1], None, ALU.mult).then_inc(VE.h, 1)
                        VC(_)
                        VE.c += 1
                        btph_war["ve"] = VE.c
                        ve_btw = VE.c
                        bsel = 6 if h % 2 == 0 else 7
                        def _(te, pi=pi, hh=hh, gsel=gsel, bsel=bsel, ve_btw=ve_btw, gd_xtg=gd_xtg):
                            te.wait_ge(VE.h, ve_btw)
                            te.wait_ge(GD.h, gd_xtg)
                            bank_wait(te, bsel)
                            te.matmul(pb[bsel][:, 0:128], btw1[pi],
                                      xtg[gsel][:, hh * 128:(hh + 1) * 128],
                                      start=True, stop=True).then_inc(TM.h, 1)
                        TE(_)
                        TM.c += 1
                        def _(v, c4=c4, h=h, bsel=bsel, w=TM.c):
                            v.wait_ge(TM.h, w)
                            v.scalar_tensor_tensor(HT[:, h * 128:(h + 1) * 128],
                                                   HT[:, h * 128:(h + 1) * 128],
                                                   ae1[c4][:, h:h + 1], pb[bsel][:, 0:128],
                                                   ALU.mult, ALU.add).then_inc(VE.h, 1)
                        VC(_)
                        VE.c += 1
                        pbank_free[bsel] = (VE, VE.c)

            def _(te):
                bank_wait(te, 4)
                ins = None
                for k in range(32):
                    ins = te.matmul(pb[4][0:4, 0:512], gatew[:, k * 4:(k + 1) * 4],
                                    xn1[:, k * W1 + 3:(k + 1) * W1], start=(k == 0), stop=(k == 31))
                ins.then_inc(TM.h, 1)
            TE(_)
            TM.c += 1
            def _(v, w=TM.c):
                v.wait_ge(TM.h, w)
                v.tensor_scalar(scph, pb[4][0:4, 0:512], gateb[:, :], None, ALU.add).then_inc(VE.h, 1)
            VC(_)
            VE.c += 1
            pbank_free[4] = (VE, VE.c)
            ve_scg = VE.c

            for c4 in range(4):
                pi = c4 % 2
                def _(te, c4=c4, ve_scg=ve_scg):
                    te.wait_ge(VE.h, ve_scg)
                    bank_wait(te, 5)
                    te.transpose(pb[5][:, 0:4], scph[0:4, c4 * 128:(c4 + 1) * 128],
                                 ident[0:4, 0:4]).then_inc(TM.h, 1)
                TE(_)
                TM.c += 1
                def _(v, pi=pi, w=TM.c):
                    v.wait_ge(TM.h, w)
                    v.tensor_copy(scT[pi], pb[5][:, 0:4])
                    v.tensor_copy(bestc, pb[5][:, 0:1])
                    v.memset(rcol[pi], float(RANKS[0]))
                    for e in range(1, 4):
                        v.drain()
                        v.tensor_tensor(dstep, scT[pi][:, e:e + 1], bestc, ALU.is_gt)
                        v.tensor_scalar(scT[pi][:, 0:1], rcol[pi], -1.0, float(RANKS[e]),
                                        ALU.mult, ALU.add)
                        v.drain()
                        v.tensor_tensor(dstep, dstep, scT[pi][:, 0:1], ALU.mult)
                        v.drain()
                        v.tensor_tensor(rcol[pi], rcol[pi], dstep, ALU.add)
                        v.tensor_tensor(bestc, bestc, scT[pi][:, e:e + 1], ALU.max)
                    v.drain()
                    v.tensor_scalar(maskj[pi], jt64[:, :], rcol[pi], None, ALU.is_lt).then_inc(VE.h, 1)
                VC(_)
                VE.c += 1
                pbank_free[5] = (VE, VE.c)
                ve_mask = VE.c
                def _(te, c4=c4, ve_mask=ve_mask):
                    te.wait_ge(VE.h, ve_mask)
                    bank_wait(te, 4)
                    ins = None
                    for k in range(32):
                        ins = te.matmul(pb[4][:, 0:64],
                                        xn1[:, k * W1 + 3 + c4 * 128:k * W1 + 3 + (c4 + 1) * 128],
                                        apk[:, k * 64:(k + 1) * 64], start=(k == 0), stop=(k == 31))
                    ins.then_inc(TM.h, 1)
                TE(_)
                TM.c += 1
                def _(v, pi=pi, w=TM.c):
                    v.wait_ge(TM.h, w)
                    v.drain()
                    v.tensor_tensor(p64m[pi], maskj[pi], pb[4][:, 0:64], ALU.mult).then_inc(VE.h, 1)
                VC(_)
                VE.c += 1
                pbank_free[4] = (VE, VE.c)
                def _(te, pi=pi, w=VE.c):
                    te.wait_ge(VE.h, w)
                    bank_wait(te, 5)
                    te.transpose(pb[5][0:64, 0:128], p64m[pi], ident[:, :]).then_inc(TM.h, 1)
                TE(_)
                TM.c += 1
                def _(sc, c4=c4, w=TM.c):
                    sc.wait_ge(TM.h, w)
                    sc.copy(p64T[:, c4 * 128:(c4 + 1) * 128], pb[5][0:64, 0:128]).then_inc(AC.h, 1)
                SC(_)
                AC.c += 1
                pbank_free[5] = (AC, AC.c)
            ac_p64 = AC.c

            for o in range(32):
                si = o % 2
                ld = load_slab(si, d_wffn[o])
                def _(sy, o=o, w=bltt_war[o % 2]):
                    if w:
                        sy.wait_ge(TM.h, w)
                    sy.dma_start(out=bltt[o % 2][:, :], in_=d_blt[:, o * 128:(o + 1) * 128]).then_inc(SD.h, 16)
                SY(_)
                SD.c += 16
                ld2 = SD.c
                bsel = o % 4
                def _(te, o=o, si=si, ld=ld, ld2=ld2, bsel=bsel, ac_p64=ac_p64):
                    te.wait_ge(SD.h, max(ld, ld2))
                    if o == 0:
                        te.wait_ge(AC.h, ac_p64)
                    bank_wait(te, bsel)
                    for k in range(32):
                        te.matmul(pb[bsel][:, 0:512], slab_tile(si, k),
                                  xn1[:, k * W1 + 3:(k + 1) * W1], start=(k == 0), stop=False)
                    te.matmul(pb[bsel][:, 0:512], bltt[o % 2], p64T,
                              start=False, stop=True).then_inc(TM.h, 1)
                TE(_)
                TM.c += 1
                slab_used(si, TM.c)
                bltt_war[o % 2] = TM.c
                sti = o % 2
                def _(v, o=o, sti=sti, w=TM.c, bsel=bsel, w2=ffst_war[sti]):
                    v.wait_ge(TM.h, w)
                    if w2:
                        v.wait_ge(GD.h, w2)
                    v.tensor_scalar(ffst[sti], pb[bsel][:, 0:512], bffn[:, o:o + 1], None,
                                    ALU.add).then_inc(VE.h, 1)
                VC(_)
                VE.c += 1
                pbank_free[bsel] = (VE, VE.c)
                def _(g, o=o, sti=sti, p1=p1, w=VE.c):
                    g.wait_ge(VE.h, w)
                    g.dma_start(out=s_ffn[o][:, p1 * PH1:(p1 + 1) * PH1], in_=ffst[sti]).then_inc(GD.h, 16)
                GP(_)
                GD.c += 16
                ffst_war[sti] = GD.c

        def _(v):
            v.tensor_scalar(ztile[:, 0:8192], HT[:, :], hinv[:, :], None, ALU.mult).then_inc(VE.h, 1)
        VC(_)
        VE.c += 1
        def _(g, w=VE.c, base=GD.c):
            g.wait_ge(VE.h, w)
            g.dma_start(out=cc_in.ap(), in_=ztile[:, 0:8192]).then_inc(GD.h, 16)
            g.wait_ge(GD.h, base + 16)
            g.collective_compute(
                "AllReduce", ALU.add,
                replica_groups=[[0, 1], [2, 3], [4, 5], [6, 7]],
                ins=[cc_in.ap().opt()],
                outs=[cc_out.ap().opt()],
            ).then_inc(CCS.h, 1)
            g.wait_ge(CCS.h, 1)
            g.dma_start(out=HT[:, :], in_=cc_out.ap()).then_inc(GD.h, 16)
        GP(_)
        GD.c += 32
        CCS.c += 1
        def _(v, w=GD.c):
            v.wait_ge(GD.h, w)
            v.tensor_scalar(HT[:, :], HT[:, :], hmask[:, :], None, ALU.mult).then_inc(VE.h, 1)
        VC(_)
        VE.c += 1

        def _(v):
            v.tensor_copy(aerow[0:1, 0:1], aerow[0:1, 0:1]).then_inc(VE.h, 1)
        VC(_)
        VE.c += 1
        def _(sc):
            sc.copy(aerow[0:1, 1:2], aerow[0:1, 1:2]).then_inc(AC.h, 1)
        SC(_)
        AC.c += 1
        def _(g):
            g.memset(aerow[0:1, 2:3], 1.0).then_inc(GC.h, 1)
        GP(_)
        GC.c += 1
        def _(te, wv=VE.c, wa=AC.c, wg=GC.c, wsd=SD.c, wgd=GD.c):
            te.wait_ge(VE.h, wv)
            te.wait_ge(AC.h, wa)
            te.wait_ge(GC.h, wg)
            te.wait_ge(SD.h, wsd)
            te.wait_ge(GD.h, wgd)
            te.nop().then_inc(TM.h, 1)
        TE(_)
        TM.c += 1
        barrier_tok = TM.c
        for EN in (VC, SC, GP, SY):
            def _(e, barrier_tok=barrier_tok):
                e.wait_ge(TM.h, barrier_tok)
            EN(_)

        def _(g):
            g.memset(onesrow[:, :], 1.0).then_inc(GC.h, 1)
        GP(_)
        GC.c += 1
        def _(g, w=GC.c):
            g.wait_ge(GC.h, w)
            for j in range(LRG):
                g.dma_start(out=LT[0:1, j * 128:(j + 1) * 128], in_=onesrow[0:1, :]).then_inc(GD.h, 16)
                g.dma_start(out=RT[1:2, j * 128:(j + 1) * 128], in_=onesrow[0:1, :]).then_inc(GD.h, 16)
        GP(_)
        GD.c += 32 * LRG
        gd_ones = GD.c

        outst_war = [0, 0]
        ve_ht_tok = [0] * NH
        xtg_rd_tok = [0, 0]
        lr_rd_tok = 0
        for p2 in range(NPH2):
            t0 = p2 * PH2
            def _(g, t0=t0, wa=AC.c, wt=TM.c):
                g.wait_ge(AC.h, wa)
                g.wait_ge(TM.h, wt)
                g.dma_start(out=arowt, in_=s_ab[0:1, t0:t0 + PH2]).then_inc(GD.h, 16)
                g.dma_start(out=browt, in_=s_ab[1:2, t0:t0 + PH2]).then_inc(GD.h, 16)
            GP(_)
            GD.c += 32
            def _(te, w=GD.c):
                te.wait_ge(GD.h, w)
                bank_wait(te, 4)
                bank_wait(te, 5)
                te.matmul(pb[4][:, 0:PH2], ones1[0:1, :], arowt, start=True, stop=True)
                te.matmul(pb[5][:, 0:PH2], ones1[0:1, :], browt,
                          start=True, stop=True).then_inc(TM.h, 1)
            TE(_)
            TM.c += 1
            def _(sc, w=TM.c, wv=VE.c):
                sc.wait_ge(TM.h, w)
                sc.wait_ge(VE.h, wv)
                sc.copy(arep2, pb[4][:, 0:PH2])
                sc.copy(brep2, pb[5][:, 0:PH2]).then_inc(AC.h, 1)
            SC(_)
            AC.c += 1
            pbank_free[4] = (AC, AC.c)
            pbank_free[5] = (AC, AC.c)
            ac_ab2 = AC.c

            def _(sy, p2=p2, w=TM.c, wv=VE.c, wg=gd_ones):
                sy.wait_ge(TM.h, w)
                sy.wait_ge(VE.h, wv)
                if p2 == 0:
                    sy.wait_ge(GD.h, wg)
                sy.dma_start(out=ztile[:, 0:32 * PH2], in_=d_x2[p2]).then_inc(SD.h, 16)
            SY(_)
            SD.c += 16
            def _(v, w=SD.c, ac_ab2=ac_ab2, wt=TM.c):
                v.wait_ge(SD.h, w)
                v.wait_ge(AC.h, ac_ab2)
                v.wait_ge(TM.h, wt)
                xn3 = xn2.rearrange("p (c t) -> p c t", c=32)
                v.tensor_tensor(xn3, ztile[:, 0:32 * PH2].rearrange("p (c t) -> p c t", c=32),
                                arep2[:, None, :].broadcast_to([128, 32, PH2]), ALU.mult)
                v.drain()
                v.tensor_tensor(xn3, xn3,
                                brep2[:, None, :].broadcast_to([128, 32, PH2]), ALU.add).then_inc(VE.h, 1)
            VC(_)
            VE.c += 1
            ve_xn2 = VE.c

            for f in range(64):
                si = f % 2
                ld = load_slab(si, d_w1z[f])
                bsel = f % 4
                def _(te, f=f, si=si, ld=ld, bsel=bsel, ve_xn2=ve_xn2):
                    te.wait_ge(SD.h, ld)
                    if f == 0:
                        te.wait_ge(VE.h, ve_xn2)
                    bank_wait(te, bsel)
                    ins = None
                    for k in range(32):
                        ins = te.matmul(pb[bsel][:, 0:PH2], slab_tile(si, k),
                                        xn2[:, k * PH2:(k + 1) * PH2], start=(k == 0), stop=(k == 31))
                    ins.then_inc(TM.h, 1)
                TE(_)
                TM.c += 1
                slab_used(si, TM.c)
                def _(sc, f=f, bsel=bsel, w=TM.c):
                    sc.wait_ge(TM.h, w)
                    sc.copy(ztile[:, f * PH2:(f + 1) * PH2], pb[bsel][:, 0:PH2]).then_inc(AC.h, 1)
                SC(_)
                AC.c += 1
                pbank_free[bsel] = (AC, AC.c)
            def _(sc, wv=VE.c):
                sc.wait_ge(VE.h, wv)
                for f in range(64):
                    ins = sc.activation(out=ztile[:, f * PH2:(f + 1) * PH2],
                                        in_=ztile[:, f * PH2:(f + 1) * PH2], func=AF.Silu)
                ins.then_inc(AC.h, 1)
            SC(_)
            AC.c += 1
            ac_siluz = AC.c

            for c2 in range(2):
                cg = p2 * 2 + c2
                def _(g, cg=cg, w=scr_tok[cg], t=TM.c, wa=AC.c, wv=VE.c):
                    g.wait_ge(GD.h, w)
                    g.wait_ge(TM.h, t)
                    g.wait_ge(AC.h, wa)
                    g.wait_ge(VE.h, wv)
                    g.dma_start(out=packin[0:64, :], in_=s_cum[cg]).then_inc(GD.h, 16)
                    g.dma_start(out=packin[64:128, :], in_=s_dtp[cg]).then_inc(GD.h, 16)
                    g.dma_start(out=dtpc2, in_=s_dtp[cg]).then_inc(GD.h, 16)
                    g.dma_start(out=btc2, in_=s_bt[cg]).then_inc(GD.h, 16)
                    g.dma_start(out=bc2, in_=s_b[cg]).then_inc(GD.h, 16)
                    g.dma_start(out=cc2, in_=s_c[cg]).then_inc(GD.h, 16)
                GP(_)
                GD.c += 96
                gd_chunk = GD.c

                def _(sc, w=gd_chunk):
                    sc.wait_ge(GD.h, w)
                    sc.activation(out=wr2, in_=packin[0:64, :], func=AF.Exp,
                                  bias=packin[0:64, 127:128], scale=-1.0)
                    sc.activation(out=aec[:, :], in_=packin[0:64, 127:128], func=AF.Exp).then_inc(AC.h, 1)
                SC(_)
                AC.c += 1
                def _(v, w=AC.c):
                    v.wait_ge(AC.h, w)
                    v.memset(rssum, 0.0)
                    v.tensor_tensor(wdt2, wr2, dtpc2, ALU.mult).then_inc(VE.h, 1)
                VC(_)
                VE.c += 1
                ve_wdt2 = VE.c
                def _(te, ve_wdt2=ve_wdt2):
                    te.wait_ge(VE.h, ve_wdt2)
                    bank_wait(te, 6)
                    bank_wait(te, 7)
                    te.transpose(pb[6][:, 0:128], packin[:, :], ident[:, :])
                    te.transpose(pb[7][:, 0:64], wdt2, ident[0:64, 0:64])
                    te.transpose(pb[7][0:1, 64:128], aec[:, :], ident[0:64, 0:64])
                    te.matmul(pb[7][:, 128:256], bc2, cc2, start=True, stop=True).then_inc(TM.h, 1)
                TE(_)
                TM.c += 1
                def _(sc, w=TM.c):
                    sc.wait_ge(TM.h, w)
                    sc.copy(packT, pb[6][:, 0:128])
                    sc.copy(wdtT2, pb[7][:, 0:64])
                    sc.copy(aerow[0:1, :], pb[7][0:1, 64:128])
                    sc.copy(g2sb, pb[7][:, 128:256])
                    sc.activation(out=ecT, in_=pb[6][:, 0:64], func=AF.Exp).then_inc(AC.h, 1)
                SC(_)
                AC.c += 1
                pbank_free[6] = (AC, AC.c)
                def _(te, w=AC.c):
                    te.wait_ge(AC.h, w)
                    te.matmul(pb[7][:, 64:128], ones1[0:1, :], aerow[0:1, :],
                              start=True, stop=True).then_inc(TM.h, 1)
                TE(_)
                TM.c += 1
                def _(sc, w=TM.c):
                    sc.wait_ge(TM.h, w)
                    sc.copy(ae2, pb[7][:, 64:128]).then_inc(AC.h, 1)
                SC(_)
                AC.c += 1
                pbank_free[7] = (AC, AC.c)
                ac_ae2 = AC.c

                ac_e_war = [0, 0]
                tm_y1_war = [0, 0]
                ve_yg_war = [0, 0]
                tm_ss_war = [0, 0]
                for gi in range(NH // LRG):
                    def _(g, cg=cg, gi=gi, t=lr_rd_tok):
                        if t:
                            g.wait_ge(TM.h, t)
                        g.dma_start(out=LT[1:2, :], in_=s_ncum[cg]
                                    .rearrange("h t -> (h t)")[None, gi * LRG * 128:(gi + 1) * LRG * 128]).then_inc(GD.h, 16)
                        g.dma_start(out=RT[0:1, :], in_=s_cum[cg]
                                    .rearrange("h t -> (h t)")[None, gi * LRG * 128:(gi + 1) * LRG * 128]).then_inc(GD.h, 16)
                    GP(_)
                    GD.c += 32
                    lr_tok = GD.c

                    for hgj in range(LRG // HG):
                        hgi = gi * (LRG // HG) + hgj
                        gsel = hgi % 2
                        def _(g, cg=cg, hgi=hgi, gsel=gsel, w=xt_tok[cg][NH - 1], t=xtg_rd_tok[gsel]):
                            g.wait_ge(GD.h, w)
                            if t:
                                g.wait_ge(TM.h, t)
                            g.dma_start(out=xtg[gsel][:, :],
                                        in_=s_xt[cg, :, hgi * HG:(hgi + 1) * HG, :]
                                        .rearrange("p h q -> p (h q)")).then_inc(GD.h, 16)
                        GP(_)
                        GD.c += 16
                        gd_xtg = GD.c

                        for hh in range(HG):
                            h = hgi * HG + hh
                            hl = h - gi * LRG
                            pi = h % 2
                            bD = 0 if h % 2 == 0 else 1
                            bY2 = 2 if h % 2 == 0 else 3
                            bY1 = 4 if h % 2 == 0 else 5
                            bH = 6 if h % 2 == 0 else 7
                            def _(te, h=h, hl=hl, bD=bD, bY2=bY2, lr=lr_tok, gd_xtg=gd_xtg,
                                  ac_ae2=ac_ae2, ve_w=ve_ht_tok[h]):
                                te.wait_ge(GD.h, max(lr, gd_xtg))
                                if h == 0:
                                    te.wait_ge(AC.h, ac_ae2)
                                if ve_w:
                                    te.wait_ge(VE.h, ve_w)
                                bank_wait(te, bD)
                                te.matmul(pb[bD][:, 0:128], LT[0:2, hl * 128:(hl + 1) * 128],
                                          RT[0:2, hl * 128:(hl + 1) * 128], start=True, stop=True)
                                bank_wait(te, bY2)
                                te.matmul(pb[bY2][:, 0:128], cc2, HT[:, h * 128:(h + 1) * 128],
                                          start=True, stop=True).then_inc(TM.h, 1)
                            TE(_)
                            TM.c += 1
                            lr_rd_tok = TM.c
                            xtg_rd_tok[gsel] = TM.c
                            def _(v, pi=pi, bD=bD, w=TM.c, w2=ac_e_war[pi]):
                                v.wait_ge(TM.h, w)
                                if w2:
                                    v.wait_ge(AC.h, w2)
                                v.tensor_tensor(esb[pi], maskbig[:, :], pb[bD][:, 0:128], ALU.min).then_inc(VE.h, 1)
                            VC(_)
                            VE.c += 1
                            def _(sc, pi=pi, w=VE.c):
                                sc.wait_ge(VE.h, w)
                                sc.activation(out=esbE[pi], in_=esb[pi], func=AF.Exp).then_inc(AC.h, 1)
                            SC(_)
                            AC.c += 1
                            pbank_free[bD] = (AC, AC.c)
                            ac_e_war[pi] = AC.c
                            def _(v, pi=pi, h=h, w=AC.c, w2=tm_y1_war[pi]):
                                v.wait_ge(AC.h, w)
                                if w2:
                                    v.wait_ge(TM.h, w2)
                                v.scalar_tensor_tensor(wsb[pi], esbE[pi], packT[:, 64 + h:65 + h],
                                                       g2sb, ALU.mult, ALU.mult).then_inc(VE.h, 1)
                            VC(_)
                            VE.c += 1
                            def _(te, pi=pi, hh=hh, gsel=gsel, bY1=bY1, w=VE.c):
                                te.wait_ge(VE.h, w)
                                bank_wait(te, bY1)
                                te.matmul(pb[bY1][:, 0:128], wsb[pi],
                                          xtg[gsel][:, hh * 128:(hh + 1) * 128],
                                          start=True, stop=True).then_inc(TM.h, 1)
                            TE(_)
                            TM.c += 1
                            tm_y1_war[pi] = TM.c
                            xtg_rd_tok[gsel] = TM.c
                            def _(v, pi=pi, h=h, hh=hh, gsel=gsel, bY2=bY2, bY1=bY1, w=TM.c):
                                v.wait_ge(TM.h, w)
                                v.tensor_scalar(y2sb[pi], pb[bY2][:, 0:128], ecT[:, h:h + 1], None, ALU.mult)
                                v.scalar_tensor_tensor(ytf[pi], xtg[gsel][:, hh * 128:(hh + 1) * 128],
                                                       dcol[:, h:h + 1], pb[bY1][:, 0:128],
                                                       ALU.mult, ALU.add)
                                v.drain()
                                v.tensor_tensor(ytf[pi], ytf[pi], y2sb[pi], ALU.add).then_inc(VE.h, 1)
                            VC(_)
                            VE.c += 1
                            ve_yt = VE.c
                            def _(v, pi=pi, h=h, w=TM.c):
                                v.wait_ge(TM.h, w)
                                v.tensor_scalar(btw2[pi], btc2, wdtT2[:, h:h + 1], None,
                                                ALU.mult).then_inc(VE.h, 1)
                            VC(_)
                            VE.c += 1
                            ve_btw2 = VE.c
                            def _(te, pi=pi, hh=hh, gsel=gsel, bH=bH, bY2=bY2, ve_btw2=ve_btw2,
                                  ve_yt=ve_yt):
                                te.wait_ge(VE.h, ve_btw2)
                                bank_wait(te, bH)
                                te.matmul(pb[bH][:, 0:128], btw2[pi],
                                          xtg[gsel][:, hh * 128:(hh + 1) * 128],
                                          start=True, stop=True)
                                te.wait_ge(VE.h, ve_yt)
                                te.transpose(pb[bY2][:, 128:256], ytf[pi], ident[:, :]).then_inc(TM.h, 1)
                            TE(_)
                            TM.c += 1
                            tm_hb = TM.c
                            xtg_rd_tok[gsel] = TM.c
                            def _(v, h=h, bH=bH, w=tm_hb):
                                v.wait_ge(TM.h, w)
                                v.scalar_tensor_tensor(HT[:, h * 128:(h + 1) * 128],
                                                       HT[:, h * 128:(h + 1) * 128],
                                                       ae2[:, h:h + 1], pb[bH][:, 0:128],
                                                       ALU.mult, ALU.add).then_inc(VE.h, 1)
                            VC(_)
                            VE.c += 1
                            pbank_free[bH] = (VE, VE.c)
                            ve_ht_tok[h] = VE.c
                            def _(v, h=h, c2=c2, pi=pi, bY2=bY2, w=tm_hb, wz=ac_siluz,
                                  w2=tm_ss_war[pi]):
                                v.wait_ge(TM.h, w)
                                if h == 0:
                                    v.wait_ge(AC.h, wz)
                                if w2:
                                    v.wait_ge(TM.h, w2)
                                zs = ztile[:, h * PH2 + c2 * 128:h * PH2 + (c2 + 1) * 128]
                                v.scalar_tensor_tensor(zs, zs, nw[:, h:h + 1], pb[bY2][:, 128:256],
                                                       ALU.mult, ALU.mult)
                                v.drain()
                                v.tensor_tensor(sqt2[pi], zs, zs, ALU.mult).then_inc(VE.h, 1)
                            VC(_)
                            VE.c += 1
                            pbank_free[bY2] = (VE, VE.c)
                            ve_yg_war[pi] = VE.c
                            def _(te, pi=pi, bY1=bY1, w=VE.c):
                                te.wait_ge(VE.h, w)
                                bank_wait(te, bY1)
                                te.matmul(pb[bY1][0:1, 128:256], onescol[:, :], sqt2[pi],
                                          start=True, stop=True).then_inc(TM.h, 1)
                            TE(_)
                            TM.c += 1
                            tm_ss_war[pi] = TM.c
                            def _(v, bY1=bY1, w=TM.c):
                                v.wait_ge(TM.h, w)
                                v.tensor_tensor(rssum, rssum, pb[bY1][0:1, 128:256], ALU.add).then_inc(VE.h, 1)
                            VC(_)
                            VE.c += 1
                            pbank_free[bY1] = (VE, VE.c)

                def _(sc, w=VE.c):
                    sc.wait_ge(VE.h, w)
                    sc.activation(out=rrow, in_=rssum, func=AF.Sqrt,
                                  bias=epsc[0:1, :], scale=r8k[0:1, :]).then_inc(AC.h, 1)
                SC(_)
                AC.c += 1
                def _(v, w=AC.c):
                    v.wait_ge(AC.h, w)
                    v.reciprocal(rrow, rrow).then_inc(VE.h, 1)
                VC(_)
                VE.c += 1
                def _(te, w=VE.c):
                    te.wait_ge(VE.h, w)
                    bank_wait(te, 6)
                    te.matmul(pb[6][:, 128:256], ones1[0:1, :], rrow[0:1, :],
                              start=True, stop=True).then_inc(TM.h, 1)
                TE(_)
                TM.c += 1
                def _(sc, w=TM.c):
                    sc.wait_ge(TM.h, w)
                    sc.copy(rrep, pb[6][:, 128:256]).then_inc(AC.h, 1)
                SC(_)
                AC.c += 1
                pbank_free[6] = (AC, AC.c)
                def _(v, c2=c2, w=AC.c):
                    v.wait_ge(AC.h, w)
                    ins = None
                    for h in range(NH):
                        ins = v.tensor_tensor(ztile[:, h * PH2 + c2 * 128:h * PH2 + (c2 + 1) * 128],
                                              ztile[:, h * PH2 + c2 * 128:h * PH2 + (c2 + 1) * 128],
                                              rrep, ALU.mult)
                    ins.then_inc(VE.h, 1)
                VC(_)
                VE.c += 1
            ve_rms = VE.c

            for o in range(32):
                def _(g, o=o, p2=p2, w=outst_war[o % 2]):
                    if w:
                        g.wait_ge(VD.h, w)
                    g.dma_start(out=ffin[o % 2], in_=s_ffn[o][:, p2 * PH2:(p2 + 1) * PH2]).then_inc(GD.h, 16)
                GP(_)
                GD.c += 16
                gd_ffin = GD.c
                def _(sy, o=o, p2=p2, w=(VD.c if (p2 > 0 or o > 0) else 0), wv=VE.c):
                    if w:
                        sy.wait_ge(VD.h, w)
                    elif wv:
                        sy.wait_ge(VE.h, wv)
                    sy.dma_start(out=xst2, in_=d_x2[p2][:, o * PH2:(o + 1) * PH2]).then_inc(SD.h, 16)
                SY(_)
                SD.c += 16
                sd_x2b = SD.c
                bsel = 2 + (o % 2)
                for g in range(4):
                    hb = (o * 4 + g) % 4
                    gate = slab_free[hb]
                    def _(sy, o=o, g=g, hb=hb, gate=gate):
                        if gate is not None:
                            sy.wait_ge(gate[0].h, gate[1])
                        sy.dma_start(out=slabh[hb][:, :],
                                     in_=d_wout[o, g // 2][:, (g % 2) * 2048:((g % 2) + 1) * 2048]).then_inc(SD.h, 16)
                    SY(_)
                    SD.c += 16
                    ldg = SD.c
                    def _(te, o=o, g=g, hb=hb, ldg=ldg, bsel=bsel, ve_rms=ve_rms):
                        te.wait_ge(SD.h, ldg)
                        if o == 0 and g == 0:
                            te.wait_ge(VE.h, ve_rms)
                        if g == 0:
                            bank_wait(te, bsel)
                        ins = None
                        for kk in range(16):
                            k = g * 16 + kk
                            ins = te.matmul(pb[bsel][:, 0:PH2], slabh[hb][:, kk * 128:(kk + 1) * 128],
                                            ztile[:, k * PH2:(k + 1) * PH2],
                                            start=(k == 0), stop=(k == 63))
                        ins.then_inc(TM.h, 1)
                    TE(_)
                    TM.c += 1
                    slab_free[hb] = (TM, TM.c)
                sti = o % 2
                def _(v, o=o, sti=sti, bsel=bsel, w=TM.c, gd_ffin=gd_ffin,
                      sd_x2b=sd_x2b, w2=outst_war[sti]):
                    v.wait_ge(TM.h, w)
                    v.wait_ge(GD.h, gd_ffin)
                    v.wait_ge(SD.h, sd_x2b)
                    if w2:
                        v.wait_ge(VD.h, w2)
                    v.scalar_tensor_tensor(outst[sti], ffin[o % 2], 1.0, pb[bsel][:, 0:PH2],
                                           ALU.mult, ALU.add)
                    v.drain()
                    v.tensor_tensor(outst[sti], outst[sti], xst2, ALU.add).then_inc(VE.h, 1)
                VC(_)
                VE.c += 1
                def _(sc, o=o, sti=sti, p2=p2, w=VE.c):
                    sc.wait_ge(VE.h, w)
                    sc.dma_start(out=d_out[o * 128:(o + 1) * 128, p2 * PH2:(p2 + 1) * PH2],
                                 in_=outst[sti]).then_inc(VD.h, 16)
                SC(_)
                VD.c += 16
                outst_war[sti] = VD.c
                pbank_free[bsel] = (VD, VD.c)

        def _(sy, w=VD.c):
            sy.wait_ge(VD.h, w)
        SY(_)
    return nc


def _pack_lhsT(cols):
    w = cols.shape[1]
    return np.ascontiguousarray(cols.reshape(32, 128, w).transpose(1, 0, 2).reshape(128, 32 * w))


def _pack_wout(cols):
    r = cols.reshape(64, 128, 128)
    out = np.empty((2, 128, 32 * 128), np.float32)
    for half in range(2):
        out[half] = r[half * 32:(half + 1) * 32].transpose(1, 0, 2).reshape(128, 32 * 128)
    return out


def _prep_weights(inputs):
    W_in = np.ascontiguousarray(inputs["L0_attn_W_in"], dtype=np.float32)
    conv_w = np.ascontiguousarray(inputs["L0_attn_conv_w"], dtype=np.float32)
    W_out = np.ascontiguousarray(inputs["L0_attn_W_out"], dtype=np.float32)
    Wf = np.ascontiguousarray(inputs["L0_ffn_W"], dtype=np.float32)
    Af = np.ascontiguousarray(inputs["L0_ffn_A"], dtype=np.float32)
    Bf = np.ascontiguousarray(inputs["L0_ffn_B"], dtype=np.float32)
    gw = np.ascontiguousarray(inputs["L0_ffn_gate_w"], dtype=np.float32)

    tiles = []
    dtcols = np.zeros((HD, 128), np.float32)
    dtcols[:, 0:64] = W_in[:, DSSM + CONVD:]
    tiles.append(_pack_lhsT(dtcols))
    tiles.append(_pack_lhsT(W_in[:, 2 * DSSM:2 * DSSM + 128]))
    tiles.append(_pack_lhsT(W_in[:, 2 * DSSM + 128:2 * DSSM + 256]))
    for j in range(64):
        tiles.append(_pack_lhsT(W_in[:, DSSM + j * 128:DSSM + (j + 1) * 128]))
    w1a = np.stack(tiles)
    w1z = np.stack([_pack_lhsT(W_in[:, j * 128:(j + 1) * 128]) for j in range(64)])
    wout = np.stack([_pack_wout(W_out[:, o * 128:(o + 1) * 128]) for o in range(32)])
    wffn = np.stack([_pack_lhsT(np.ascontiguousarray(Wf[o * 128:(o + 1) * 128, :].T))
                     for o in range(32)])
    apack = np.ascontiguousarray(Af.T.reshape(32, 128, 64).transpose(1, 0, 2).reshape(128, 32 * 64))
    blt = np.ascontiguousarray(Bf.T)
    gate = np.ascontiguousarray(gw.T.reshape(32, 128, 4).transpose(1, 0, 2).reshape(128, 32 * 4))
    cw = np.concatenate([conv_w[DSSM:DSSM + 128], conv_w[DSSM + 128:DSSM + 256],
                         conv_w[0:DSSM]], axis=0)
    convw = np.ascontiguousarray(cw.reshape(66, 128, KC).transpose(1, 0, 2).reshape(128, 66 * KC))
    return dict(w1a=w1a, w1z=w1z, wout=wout, wffn=wffn, apack=apack, blt=blt,
                gatew=gate, convw=convw)


_NC_CACHE = {}


def kernel(**inputs):
    x = np.asarray(inputs["L0_attn_hidden_states"], dtype=np.float32)
    wp = _prep_weights(inputs)
    dt_bias = np.asarray(inputs["L0_attn_dt_bias"], np.float32).reshape(64, 1)
    A_log = np.asarray(inputs["L0_attn_A_log"], np.float32).reshape(64, 1)
    D_param = np.asarray(inputs["L0_attn_D_param"], np.float32)
    norm_w = np.asarray(inputs["L0_attn_norm_weight"], np.float32)
    bf = np.asarray(inputs["L0_ffn_b"], np.float32)
    gb = np.asarray(inputs["L0_ffn_gate_b"], np.float32).reshape(4, 1)
    dcol = np.ascontiguousarray(np.broadcast_to(D_param[None, :], (128, 64)))
    nwp = np.ascontiguousarray(norm_w.reshape(64, 128).T)
    bffn = np.ascontiguousarray(bf.reshape(32, 128).T)

    if "nc" not in _NC_CACHE:
        _NC_CACHE["nc"] = build_program()
    nc = _NC_CACHE["nc"]

    in_maps = []
    for i in range(8):
        b, half = i // 2, i % 2
        t0 = half * T
        xfm = np.zeros((HD, T + 3), np.float32)
        if half == 1:
            xfm[:, 0:3] = x[b, t0 - 3:t0].T
        xfm[:, 3:] = x[b, t0:t0 + T].T
        x1 = np.empty((NPH1, 128, 32 * W1), np.float32)
        for p in range(NPH1):
            x1[p] = xfm[:, p * PH1:p * PH1 + W1].reshape(32, 128, W1).transpose(1, 0, 2).reshape(128, 32 * W1)
        x2 = np.empty((NPH2, 128, 32 * PH2), np.float32)
        for p in range(NPH2):
            x2[p] = xfm[:, 3 + p * PH2:3 + (p + 1) * PH2].reshape(32, 128, PH2).transpose(1, 0, 2).reshape(128, 32 * PH2)
        m = 1.0 if half == 1 else 0.0
        in_maps.append({
            "x1": x1, "x2": x2,
            "w1a": wp["w1a"], "w1z": wp["w1z"], "wout": wp["wout"], "wffn": wp["wffn"],
            "apack": wp["apack"], "blt": wp["blt"], "gatew": wp["gatew"], "convw": wp["convw"],
            "dtb": dt_bias, "alog": A_log, "dcol": dcol, "nw": nwp, "bffn": bffn,
            "gateb": gb,
            "hmask": np.full((128, 1), m, np.float32),
            "hinv": np.full((128, 1), 1.0 - m, np.float32),
        })
    res = run_bass_kernel_spmd(nc, in_maps, core_ids=list(range(8)))
    out = np.empty((4, 4096, 4096), np.float32)
    for i in range(8):
        b, half = i // 2, i % 2
        out[b, half * T:(half + 1) * T, :] = res.results[i]["out"].T
    return out
